# version 20
# baseline (speedup 1.0000x reference)
import numpy as np
import concourse.bass as bass
import concourse.mybir as mybir
from concourse.bass_utils import run_bass_kernel_spmd
from contextlib import ExitStack

F32 = mybir.dt.float32
BF16 = mybir.dt.bfloat16
U16 = mybir.dt.uint16
I32 = mybir.dt.int32
AF = mybir.ActivationFunctionType
ALU = mybir.AluOpType
AX = mybir.AxisListType

C = 512
D = 512
DM = 2048
NS = 4
ND = 16
NE = 4
EPS_LN = 1e-5
EPS_RMS = 1.1920929e-7


class PB:
    ENGS = ("sync", "gpsimd", "scalar", "vector", "tensor")

    def __init__(self, nc, stack):
        self.nc = nc
        self.stack = stack
        self.q = {e: [] for e in self.ENGS}
        self.sems = {}
        self.ctr = {}
        self.fence = {e: [] for e in self.ENGS}
        for e in self.ENGS:
            self.sem("p_" + e)

    def sem(self, name):
        if name not in self.sems:
            self.sems[name] = self.stack.enter_context(self.nc.semaphore(name))
            self.ctr[name] = 0
        return self.sems[name]

    def step(self, eng, fn, waits=(), amt=1, semname=None, serial=None):
        sn = semname or ("p_" + eng)
        self.sem(sn)
        wl = {}
        if serial is None:
            serial = eng in ("vector", "scalar")
        if serial and self.ctr["p_" + eng] > 0:
            wl["p_" + eng] = self.ctr["p_" + eng]
        for w in list(waits) + self.fence[eng]:
            if w is None:
                continue
            s, v = w
            if v > wl.get(s, -1):
                wl[s] = v
        self.fence[eng] = []
        self.ctr[sn] += amt
        tok = (sn, self.ctr[sn])
        self.q[eng].append((sorted(wl.items()), fn, sn, amt))
        return tok

    def dma(self, eng, sem, out_ap, in_ap, waits=(), **kw):
        self.sem(sem)
        prev = self.ctr[sem]
        w = list(waits) + ([(sem, prev)] if prev > 0 else [])
        return self.step(eng, lambda e: e.dma_start(out_ap, in_ap, **kw),
                         waits=w, amt=16, semname=sem, serial=False)

    def barrier(self):
        toks = [self.step(e, lambda en: None,
                          waits=[("p_" + e, self.ctr["p_" + e])]) for e in self.ENGS]
        for e in self.ENGS:
            self.fence[e] = list(toks)

    def emit_all(self):
        nc = self.nc
        with nc.Block() as block:
            def mk(eng_name):
                def run(e):
                    for wl, fn, sn, amt in self.q[eng_name]:
                        for s, v in wl:
                            e.wait_ge(self.sems[s], v)
                        inst = fn(e)
                        if inst is not None:
                            inst.then_inc(self.sems[sn], amt)
                        else:
                            e.nop(nofuse=True).then_inc(self.sems[sn], amt)
                return run
            block.sync(mk("sync"))
            block.gpsimd(mk("gpsimd"))
            block.scalar(mk("scalar"))
            block.vector(mk("vector"))
            block.tensor(mk("tensor"))


def build_nc(n_own, n_oth, Hs):
    H2s = 2 * Hs
    nch = n_own // C
    noch = n_oth // C
    NHC = Hs // 128
    NTQ = n_own // C
    NGG = Hs // 256
    NPE = DM // 256

    nc = bass.Bass("TRN2", target_bir_lowering=False)
    din = lambda name, shape, dt=F32: nc.dram_tensor(name, shape, dt, kind="ExternalInput").ap()

    x = din("x", [n_own, DM]);        xt = din("xt", [DM, n_own])
    xo = din("xo", [n_oth, DM]);      xot = din("xot", [DM, n_oth])
    wq = din("wq", [DM, D]);          wk = din("wk", [3, DM, D])
    wv = din("wv", [3, DM, D]);       wo = din("wo", [D, DM])
    wg = din("wg", [DM, 8]);          cst = din("cst", [1, 16])
    wgun = din("wgun", [H2s, DM]);    wpn = din("wpn", [DM, Hs])
    expn = din("expn", [4, 128, C]);  tauv = din("tauv", [4, 128, 1])
    sigv = din("sigv", [4, 128, 1])
    id32 = din("id32", [128, 128]);   idbf = din("idbf", [128, 128], BF16)
    seg = din("seg", [128, 2], BF16); segtb = din("segtb", [2, 128], BF16)
    segtf = din("segtf", [2, 128]);   onesr = din("onesr", [1, 128])
    ones1f = din("ones1f", [128, 1]); ones1b = din("ones1b", [128, 1], BF16)

    out = nc.dram_tensor("out", [n_own, DM], F32, kind="ExternalOutput").ap()

    r1d = nc.dram_tensor("r1d", [n_own, DM], F32, kind="Internal").ap()
    hhi = nc.dram_tensor("hhi", [Hs, n_own], BF16, kind="Internal").ap()
    hlo = nc.dram_tensor("hlo", [Hs, n_own], BF16, kind="Internal").ap()
    wph = nc.dram_tensor("wph", [Hs, DM], U16, kind="Internal").ap()
    wgh_n = nc.dram_tensor("wgh_n", [H2s, DM], U16, kind="Internal").ap()
    s_w = {}
    for nm, shp in [("q", [DM, D]), ("k", [3, DM, D]), ("v", [3, DM, D]), ("o", [D, DM])]:
        s_w[nm + "h"] = nc.dram_tensor("s_w" + nm + "h", shp, BF16, kind="Internal").ap()
        s_w[nm + "l"] = nc.dram_tensor("s_w" + nm + "l", shp, BF16, kind="Internal").ap()

    st = ExitStack()
    p = PB(nc, st)
    sbt = lambda nm, sh, dt: st.enter_context(nc.sbuf_tensor(nm, sh, dt))

    PS = st.enter_context(nc.psum_tensor("PS", [128, 8, 512], F32))
    bfree = [None] * 8

    csts = sbt("csts", [1, 16], F32)
    srow = sbt("srow", [1, 16], F32)
    srob = sbt("srob", [1, 4], BF16)
    sbc = sbt("sbc", [128, 16], F32)
    ident = sbt("ident", [128, 128], F32)
    identb = sbt("identb", [128, 128], BF16)
    segs = sbt("segs", [128, 2], BF16)
    segtbs = sbt("segtbs", [2, 128], BF16)
    segtfs = sbt("segtfs", [2, 128], F32)
    onesrs = sbt("onesrs", [1, 128], F32)
    ones1fs = sbt("ones1fs", [128, 1], F32)
    ones1bs = sbt("ones1bs", [128, 1], BF16)
    invr = sbt("invr", [128, NTQ, 4], F32)
    ap_st = ExitStack()
    apt = lambda nm, sh, dt: ap_st.enter_context(nc.sbuf_tensor(nm, sh, dt))
    dgh = apt("dgh", [128, 3, 128], BF16)
    dgl = apt("dgl", [128, 3, 128], BF16)
    cvec = apt("cvec", [128, 4, 4], F32)
    evecs = apt("evecs", [128, 2, 4, 4], F32)
    msk = apt("msk", [128, 3, 4, 512], BF16)
    tauvs = apt("tauvs", [128, 4, 1], F32)
    sigvs = apt("sigvs", [128, 4, 1], F32)
    csr_q = apt("csr_q", [1, 512], BF16)
    csr_k = apt("csr_k", [1, 3, 512], BF16)
    csr_v = apt("csr_v", [1, 3, 512], BF16)
    csr_g = apt("csr_g", [1, 8], BF16)
    wgh = apt("wgh", [128, ND, 8], BF16)
    wgl = apt("wgl", [128, ND, 8], BF16)
    Sh = apt("Sh", [128, 3, 4, 512], BF16)
    Sl = apt("Sl", [128, 3, 4, 512], BF16)

    st0a = ExitStack()
    expns = st0a.enter_context(nc.sbuf_tensor("expns", [128, 4, 512], F32))
    t_cst = p.dma("sync", "d_cst", csts[:], cst[:, :])
    for dst_, src_ in [(ident, id32), (identb, idbf), (segs, seg), (segtbs, segtb),
                       (segtfs, segtf), (onesrs, onesr), (ones1fs, ones1f), (ones1bs, ones1b)]:
        p.dma("sync", "d_cst", dst_[:], src_[:, :])
    p.dma("sync", "d_cst", tauvs[:], tauv.rearrange("s p o -> p s o"), allow_slow_non_contiguous=True)
    p.dma("sync", "d_cst", sigvs[:], sigv.rearrange("s p o -> p s o"), allow_slow_non_contiguous=True)
    t_ldall = p.dma("sync", "d_cst", expns[:], expn.rearrange("s p t -> p s t"))

    t0m = p.step("vector", lambda e: e.memset(srow[:], 0.0))
    t1 = p.step("scalar", lambda e: e.activation(srow[:, 14:15], csts[:, 0:1], AF.Sigmoid), waits=[t_ldall, t0m])
    t1 = p.step("scalar", lambda e: e.activation(srow[:, 0:1], srow[:, 14:15], AF.Ln), waits=[t1])
    t1 = p.step("scalar", lambda e: e.activation(srow[:, 14:15], csts[:, 1:2], AF.Sigmoid), waits=[t1])
    t1 = p.step("scalar", lambda e: e.activation(srow[:, 1:2], srow[:, 14:15], AF.Ln), waits=[t1])
    t1 = p.step("scalar", lambda e: e.activation(srow[:, 14:15], csts[:, 2:3], AF.Sigmoid), waits=[t1])
    t1 = p.step("scalar", lambda e: e.activation(srow[:, 2:3], srow[:, 14:15], AF.Ln), waits=[t1])
    tds = []
    for i in range(3):
        tds.append(p.step("scalar", lambda e, i=i: e.activation(
            srow[:, 3 + i:4 + i], csts[:, 5:6], AF.Exp, scale=srow[:, i:i + 1]), waits=[t1]))
    t2 = p.step("vector", lambda e: e.tensor_copy(srob[:, 0:3], srow[:, 3:6]), waits=tds)
    t2 = p.step("vector", lambda e: e.tensor_copy(srow[:, 6:9], srob[:, 0:3]), waits=[t2])
    t2 = p.step("vector", lambda e: e.tensor_tensor(srow[:, 9:12], srow[:, 3:6], srow[:, 6:9],
                                                    ALU.subtract), waits=[t2])
    t2 = p.step("vector", lambda e: e.tensor_copy(srow[:, 12:14], csts[:, 3:5]), waits=[t2])
    t3 = p.step("tensor", lambda e: e.matmul(PS[:, 7, 0:16], onesrs[:], srow[:, 0:16],
                                             start=True, stop=True), waits=[t2], serial=False)
    t_sbc = p.step("vector", lambda e: e.tensor_copy(sbc[:], PS[:, 7, 0:16]), waits=[t3])
    bfree[7] = t_sbc
    for i in range(3):
        p.step("vector", lambda e, i=i: e.tensor_scalar_mul(dgh[:, i, :], identb[:],
                                                            sbc[:, 6 + i:7 + i]), waits=[t_sbc])
        p.step("vector", lambda e, i=i: e.tensor_scalar_mul(dgl[:, i, :], identb[:],
                                                            sbc[:, 9 + i:10 + i]), waits=[t_sbc])
    for i in range(3):
        for s in range(4):
            p.step("scalar", lambda e, i=i, s=s: e.activation(
                cvec[:, s, i:i + 1], tauvs[:, s, :], AF.Exp, scale=sbc[:, i:i + 1]), waits=[t_sbc])
            te = p.step("scalar", lambda e, i=i, s=s: e.activation(
                evecs[:, 0, s, i:i + 1], sigvs[:, s, :], AF.Exp, scale=sbc[:, i:i + 1]), waits=[t_sbc])
            p.step("vector", lambda e, i=i, s=s: e.tensor_scalar_mul(
                evecs[:, 1, s, i:i + 1], evecs[:, 0, s, i:i + 1], sbc[:, 13:14]), waits=[te])
    for i in range(3):
        for s in range(4):
            p.step("scalar", lambda e, i=i, s=s: e.activation(
                msk[:, i, s, :], expns[:, s, :], AF.Exp, scale=sbc[:, i:i + 1]), waits=[t_sbc])

    p.barrier()
    st0a.close()

    with ExitStack() as st0:
        wst = st0.enter_context(nc.sbuf_tensor("wst", [128, 2, 512], F32))
        whst = st0.enter_context(nc.sbuf_tensor("whst", [128, 2, 512], BF16))
        wlst = st0.enter_context(nc.sbuf_tensor("wlst", [128, 2, 512], BF16))
        wcons = [None, None]

        def prep_weight(src_ap, dsth, dstl, ntile, csum_to, tile_sl):
            tcs = None
            first = [True]
            for t in range(ntile):
                par = t % 2
                tl = p.dma("sync", "d_w%d" % par, wst[:, par, :], tile_sl(src_ap, t),
                           waits=wcons[par] or [])
                th = p.step("vector", lambda e, par=par: e.tensor_copy(whst[:, par, :], wst[:, par, :]),
                            waits=[tl])
                tlo = p.step("vector", lambda e, par=par: e.tensor_tensor(
                    wlst[:, par, :], wst[:, par, :], whst[:, par, :], ALU.subtract), waits=[th])
                if csum_to is not None:
                    def csmm(e, par=par, t=t, f=first[0]):
                        e.matmul(PS[0:1, 6, :], ones1bs[:], whst[:, par, :], start=f, stop=False)
                        return e.matmul(PS[0:1, 6, :], ones1bs[:], wlst[:, par, :],
                                        start=False, stop=(t == ntile - 1))
                    tcs = p.step("tensor", csmm, waits=[tlo, bfree[6] if first[0] else None],
                                 serial=False)
                    first[0] = False
                d1 = p.dma("gpsimd", "st_wei", tile_sl(dsth, t), whst[:, par, :], waits=[tlo])
                d2 = p.dma("gpsimd", "st_wei", tile_sl(dstl, t), wlst[:, par, :], waits=[tlo])
                wcons[par] = [d2, tcs] if csum_to is not None else [d2]
            if csum_to is not None:
                tv = p.step("vector", lambda e: e.tensor_copy(csum_to, PS[0:1, 6, :]), waits=[tcs])
                bfree[6] = tv

        rowtile = lambda ap_, t: ap_[t * 128:(t + 1) * 128, :]
        wo_tile = lambda ap_, t: ap_[(t // 4) * 128:(t // 4 + 1) * 128, (t % 4) * 512:(t % 4 + 1) * 512]
        prep_weight(wq, s_w["qh"], s_w["ql"], ND, csr_q[:, :], rowtile)
        for i in range(3):
            prep_weight(wk[i], s_w["kh"][i], s_w["kl"][i], ND, csr_k[:, i, :], rowtile)
            prep_weight(wv[i], s_w["vh"][i], s_w["vl"][i], ND, csr_v[:, i, :], rowtile)
        prep_weight(wo, s_w["oh"], s_w["ol"], 16, None, wo_tile)
        tcs = None
        for t in range(ND):
            par = t % 2
            tl = p.dma("sync", "d_w%d" % par, wst[:, par, 0:8], wg[t * 128:(t + 1) * 128, :],
                       waits=wcons[par] or [])
            th = p.step("vector", lambda e, par=par, t=t: e.tensor_copy(wgh[:, t, :], wst[:, par, 0:8]),
                        waits=[tl])
            tlo = p.step("vector", lambda e, par=par, t=t: e.tensor_tensor(
                wgl[:, t, :], wst[:, par, 0:8], wgh[:, t, :], ALU.subtract), waits=[th])
            def gcs(e, t=t):
                e.matmul(PS[0:1, 6, 0:8], ones1bs[:], wgh[:, t, :], start=(t == 0), stop=False)
                return e.matmul(PS[0:1, 6, 0:8], ones1bs[:], wgl[:, t, :],
                                start=False, stop=(t == ND - 1))
            tcs = p.step("tensor", gcs, waits=[tlo, bfree[6] if t == 0 else None], serial=False)
            wcons[par] = [tcs]
        tv = p.step("vector", lambda e: e.tensor_copy(csr_g[:], PS[0:1, 6, 0:8]), waits=[tcs])
        bfree[6] = tv
        n_wei_stores = p.ctr["st_wei"]

    p.barrier()

    st1 = ExitStack()
    sb1 = lambda nm, sh, dt: st1.enter_context(nc.sbuf_tensor(nm, sh, dt))
    xsb = sb1("xsb", [128, NS, DM], F32)
    xtst = sb1("xtst", [128, 2, 512], F32)
    xth = sb1("xth", [128, ND, 512], BF16)
    xtl = sb1("xtl", [128, ND, 512], BF16)
    sqs = sb1("sqs", [128, DM], F32)
    stf = sb1("stf", [128, NS, 8], F32)
    stb = sb1("stb", [128, NS, 1], BF16)
    nmrow = sb1("nmrow", [1, 512], BF16)
    einv2 = sb1("einv2", [128, NS, 4], F32)
    qh = sb1("qh", [128, NE, 512], BF16);  ql = sb1("ql", [128, NE, 512], BF16)
    kh = sb1("kh", [128, NE, 512], BF16);  kl = sb1("kl", [128, NE, 512], BF16)
    kth = sb1("kth", [128, NS, 512], BF16); ktl = sb1("ktl", [128, NS, 512], BF16)
    vth = sb1("vth", [128, NS, 512], BF16); vtl = sb1("vtl", [128, NS, 512], BF16)
    vsh = sb1("vsh", [128, NS, 512], BF16); vsl = sb1("vsl", [128, NS, 512], BF16)
    sch = sb1("sch", [128, NS, 512], BF16); scl_ = sb1("scl_", [128, NS, 512], BF16)
    scm32 = sb1("scm32", [128, 512], F32)
    rc32 = sb1("rc32", [128, 512], F32)
    ra32 = sb1("ra32", [128, 512], F32)
    vt32 = sb1("vt32", [128, 512], F32)
    blsb = sb1("blsb", [128, NS, 512], F32)
    blth = sb1("blth", [128, NE, 512], BF16); bltl = sb1("bltl", [128, NE, 512], BF16)
    gl3 = sb1("gl3", [128, NS, 8], F32)
    giv = sb1("giv", [128, NS, 4], F32)
    wsl_h = sb1("wsl_h", [128, 4, 512], BF16)
    wsl_l = sb1("wsl_l", [128, 4, 512], BF16)

    tok = {
        "xsb_free": [None] * NS,
        "xcast": [None] * ND,
        "xth_cons": None,
        "S_ready": [None] * 3,
        "S_read": [None] * 3,
        "wslab": [("p_tensor", 0)] * 4,
    }
    wei_tok = ("st_wei", n_wei_stores)

    def load_wslab(src_h, src_l, r0, c0, c1, idx):
        par = idx % 4
        p.dma("sync", "d_wsl%d" % par, wsl_h[:, par, 0:c1 - c0], src_h[r0:r0 + 128, c0:c1],
              waits=[tok["wslab"][par], wei_tok])
        t2 = p.dma("sync", "d_wsl%d" % par, wsl_l[:, par, 0:c1 - c0], src_l[r0:r0 + 128, c0:c1])
        return t2, par

    def proj_dloop(src_h, src_l, fm, t_xt, wcnt, b0=0):
        groups = [None] * 4
        for d in range(ND):
            tw, par = load_wslab(src_h, src_l, d * 128, 0, 512, wcnt[0])
            wcnt[0] += 1
            for g in range(4):
                first = (d == 0)
                def mm(e, d=d, g=g, par=par, first=first, b0=b0):
                    o = PS[:, b0 + g, :]
                    if fm:
                        w_h = wsl_h[:, par, g * 128:(g + 1) * 128]
                        w_l = wsl_l[:, par, g * 128:(g + 1) * 128]
                        e.matmul(o, w_h, xth[:, d, :], start=first, stop=False)
                        e.matmul(o, w_l, xth[:, d, :], start=False, stop=False)
                        return e.matmul(o, w_h, xtl[:, d, :], start=False, stop=False)
                    else:
                        xh_t = xth[:, d, g * 128:(g + 1) * 128]
                        xl_t = xtl[:, d, g * 128:(g + 1) * 128]
                        e.matmul(o, xh_t, wsl_h[:, par, :], start=first, stop=False)
                        e.matmul(o, xl_t, wsl_h[:, par, :], start=False, stop=False)
                        return e.matmul(o, xh_t, wsl_l[:, par, :], start=False, stop=False)
                groups[g] = p.step("tensor", mm,
                                   waits=[tw, tok["xcast"][d],
                                          bfree[b0 + g] if d == 0 else None], serial=False)
            tok["wslab"][par] = groups[3]
        return groups

    def proj_finish(groups, dst_h, dst_l, csr_ap, fm, t_nm, t_stats, extra=None, b0=0):
        for g in range(4):
            def r1mm(e, g=g, b0=b0):
                o = PS[:, b0 + g, :]
                if fm:
                    return e.matmul(o, csr_ap[:, g * 128:(g + 1) * 128], nmrow[:],
                                    start=False, stop=True)
                else:
                    return e.matmul(o, nmrow[:, g * 128:(g + 1) * 128], csr_ap,
                                    start=False, stop=True)
            groups[g] = p.step("tensor", r1mm, waits=[groups[g], t_nm], serial=False)
        ev = [None] * 4
        for g in range(4):
            pg = PS[:, b0 + g, :]
            if extra is None:
                th_ = p.step("scalar", lambda e, g=g, pg=pg: e.activation(dst_h[:, g, :], pg,
                                                                          AF.Copy), waits=[groups[g]])
                tl_ = p.step("vector", lambda e, g=g, pg=pg: e.tensor_tensor(
                    dst_l[:, g, :], pg, dst_h[:, g, :], ALU.subtract), waits=[th_])
            else:
                sc1, (edh, edl), sc2 = extra
                t0_ = p.step("vector", lambda e, g=g, pg=pg: e.tensor_scalar_mul(
                    vt32[:], pg, sc1(g)), waits=[groups[g], t_stats])
                th_ = p.step("scalar", lambda e, g=g: e.activation(dst_h[:, g, :], vt32[:],
                                                                   AF.Copy), waits=[t0_])
                tl_ = p.step("vector", lambda e, g=g: e.tensor_tensor(
                    dst_l[:, g, :], vt32[:], dst_h[:, g, :], ALU.subtract), waits=[th_])
                t0b = p.step("vector", lambda e, g=g, pg=pg: e.tensor_scalar_mul(
                    rc32[:], pg, sc2(g)), waits=[groups[g], t_stats])
                th2 = p.step("scalar", lambda e, g=g: e.activation(edh[:, g, :], rc32[:],
                                                                   AF.Copy), waits=[t0b])
                tl_ = p.step("vector", lambda e, g=g: e.tensor_tensor(
                    edl[:, g, :], rc32[:], edh[:, g, :], ALU.subtract), waits=[th2, tl_])
            ev[g] = tl_
            bfree[b0 + g] = tl_
        return ev

    def proj_pass(dst_h, dst_l, src_h, src_l, csr_ap, fm, t_nm, t_xt, t_stats,
                  wcnt, extra=None, b0=0):
        groups = proj_dloop(src_h, src_l, fm, t_xt, wcnt, b0=b0)
        return proj_finish(groups, dst_h, dst_l, csr_ap, fm, t_nm, t_stats, extra=extra, b0=b0)

    wcnt = [0]

    def do_chunk(xsrc, xtsrc, base, is_comp, cx):
        t_xs = [None] * NS
        for ts in range(NS):
            t_xs[ts] = p.dma("sync", "d_xsb", xsb[:, ts, :],
                             xsrc[base + ts * 128:base + (ts + 1) * 128, :],
                             waits=[tok["xsb_free"][ts]])
        t_x = t_xs[NS - 1]
        for d in range(ND):
            par = d % 2
            tl = p.dma("sync", "d_xt%d" % par, xtst[:, par, :], xtsrc[d * 128:(d + 1) * 128, base:base + C],
                       waits=[tok["xcast"][d - 2] if d >= 2 else tok["xth_cons"]])
            th = p.step("vector", lambda e, d=d, par=par: e.tensor_copy(xth[:, d, :], xtst[:, par, :]),
                        waits=[tl, tok["xth_cons"] if d < 2 else None])
            tok["xcast"][d] = p.step("vector", lambda e, d=d, par=par: e.tensor_tensor(
                xtl[:, d, :], xtst[:, par, :], xth[:, d, :], ALU.subtract), waits=[th])
        t_xt = tok["xcast"][ND - 1]
        if not is_comp:
            q_groups = proj_dloop(s_w["qh"], s_w["ql"], True, t_xt, wcnt, b0=0)
            k0_groups = proj_dloop(s_w["kh"][0], s_w["kl"][0], True, t_xt, wcnt, b0=4)
        else:
            k0_groups = proj_dloop(s_w["kh"][0], s_w["kl"][0], True, t_xt, wcnt, b0=4)
            v0_groups = proj_dloop(s_w["vh"][0], s_w["vl"][0], False, t_xt, wcnt, b0=0)
        t_stats = None
        t_nm = None
        for s in range(NS):
            tq_ = p.step("scalar", lambda e, s=s: e.activation(sqs[:], xsb[:, s, :], AF.Square),
                         waits=[t_xs[s], t_stats])
            ta = p.step("vector", lambda e, s=s: e.reduce_sum(
                stf[:, s, 0:1], xsb[:, s, :], AX.X, negate=True), waits=[t_xs[s]])
            tb = p.step("vector", lambda e, s=s: e.reduce_sum(stf[:, s, 6:7], sqs[:], AX.X),
                        waits=[tq_])
            tc = p.step("vector", lambda e, s=s: e.tensor_scalar_mul(
                stf[:, s, 0:1], stf[:, s, 0:1], 1.0 / DM), waits=[ta])
            td = p.step("vector", lambda e, s=s: e.tensor_scalar(
                stf[:, s, 1:2], stf[:, s, 6:7], 1.0 / DM, EPS_LN, ALU.mult, ALU.add), waits=[tb])
            te = p.step("vector", lambda e, s=s: e.tensor_scalar_mul(
                stf[:, s, 2:3], stf[:, s, 0:1], stf[:, s, 0:1]), waits=[tc])
            tf = p.step("vector", lambda e, s=s: e.scalar_tensor_tensor(
                stf[:, s, 3:4], stf[:, s, 2:3], -1.0, stf[:, s, 1:2], ALU.mult, ALU.add),
                waits=[td, te])
            tg = p.step("vector", lambda e, s=s: e.reciprocal(stf[:, s, 4:5], stf[:, s, 3:4]),
                        waits=[tf])
            tn = p.step("scalar", lambda e, s=s: e.activation(
                stf[:, s, 4:5], stf[:, s, 4:5], AF.Sqrt), waits=[tg])
            for _ in range(2):
                n1 = p.step("scalar", lambda e, s=s: e.activation(
                    stf[:, s, 6:7], stf[:, s, 4:5], AF.Square), waits=[tn])
                n2 = p.step("vector", lambda e, s=s: e.tensor_tensor(
                    stf[:, s, 6:7], stf[:, s, 6:7], stf[:, s, 3:4], ALU.mult), waits=[n1])
                n3 = p.step("vector", lambda e, s=s: e.tensor_scalar(
                    stf[:, s, 6:7], stf[:, s, 6:7], -0.5, 1.5, ALU.mult, ALU.add), waits=[n2])
                tn = p.step("vector", lambda e, s=s: e.tensor_tensor(
                    stf[:, s, 4:5], stf[:, s, 4:5], stf[:, s, 6:7], ALU.mult), waits=[n3])
            ti = p.step("vector", lambda e, s=s: e.tensor_scalar_mul(
                stf[:, s, 5:6], stf[:, s, 4:5], stf[:, s, 4:5]), waits=[tn])
            for i in range(3):
                ti = p.step("vector", lambda e, s=s, i=i, ic=1 if is_comp else 0: e.tensor_tensor(
                    einv2[:, s, i:i + 1], evecs[:, ic, s, i:i + 1], stf[:, s, 5:6], ALU.mult),
                    waits=[ti])
            tj = p.step("vector", lambda e, s=s: e.tensor_copy(
                stb[:, s, 0:1], stf[:, s, 0:1]), waits=[tc])
            t_nm = p.dma("sync", "d_nm", nmrow[0:1, s * 128:(s + 1) * 128],
                         stb[:, s, 0:1], waits=[tj], allow_slow_non_contiguous=True)
            t_stats = ti
        t_qev = None
        if not is_comp:
            t_qev = proj_finish(q_groups, qh, ql, csr_q, True, t_nm, t_stats, b0=0)
            t_k0ev = proj_finish(k0_groups, kh, kl, csr_k[:, 0, :], True, t_nm, t_stats, b0=4)
        else:
            t_k0ev = proj_finish(k0_groups, kh, kl, csr_k[:, 0, :], True, t_nm, t_stats, b0=4)
        t_gate = None
        if not is_comp:
            for s in range(NS):
                def gmm(e, s=s):
                    o = PS[:, 7, 0:8]
                    for d in range(ND):
                        e.matmul(o, xth[:, d, s * 128:(s + 1) * 128], wgh[:, d, :],
                                 start=(d == 0), stop=False)
                        e.matmul(o, xtl[:, d, s * 128:(s + 1) * 128], wgh[:, d, :],
                                 start=False, stop=False)
                        e.matmul(o, xth[:, d, s * 128:(s + 1) * 128], wgl[:, d, :],
                                 start=False, stop=False)
                    return e.matmul(o, nmrow[:, s * 128:(s + 1) * 128], csr_g[:],
                                    start=False, stop=True)
                tg1 = p.step("tensor", gmm, waits=[t_xt, t_nm, bfree[7]], serial=False)
                tg2 = p.step("vector", lambda e, s=s: e.tensor_scalar_mul(
                    gl3[:, s, 0:3], PS[:, 7, 0:3], stf[:, s, 4:5]), waits=[tg1, t_stats])
                bfree[7] = tg2
                tg3 = p.step("vector", lambda e, s=s: e.reduce_max(
                    gl3[:, s, 4:5], gl3[:, s, 0:3], AX.X, negate=True), waits=[tg2])
                tg4 = p.step("scalar", lambda e, s=s: e.activation(
                    gl3[:, s, 0:3], gl3[:, s, 0:3], AF.Exp, bias=gl3[:, s, 4:5]), waits=[tg3])
                tg5 = p.step("vector", lambda e, s=s: e.reduce_sum(
                    gl3[:, s, 4:5], gl3[:, s, 0:3], AX.X), waits=[tg4])
                tg6 = p.step("vector", lambda e, s=s: e.reciprocal(
                    gl3[:, s, 5:6], gl3[:, s, 4:5]), waits=[tg5])
                tg7 = p.step("vector", lambda e, s=s: e.tensor_scalar_mul(
                    gl3[:, s, 0:3], gl3[:, s, 0:3], gl3[:, s, 5:6]), waits=[tg6])
                t_gate = p.step("vector", lambda e, s=s: e.tensor_scalar_mul(
                    giv[:, s, 0:3], gl3[:, s, 0:3], stf[:, s, 4:5]), waits=[tg7])
        for i in range(3):
            if i == 0:
                t_kev = t_k0ev
            else:
                t_kev = proj_pass(kh, kl, s_w["kh"][i], s_w["kl"][i], csr_k[:, i, :], True,
                                  t_nm, t_xt, t_stats, wcnt)
            t_ktev = [None] * NS
            tb4 = [bfree[4], bfree[5], bfree[6], bfree[7]]
            for hl, (src, dst) in enumerate(((kh, kth), (kl, ktl))):
                for ec in range(NE):
                    tts = []
                    for ss in range(4):
                        bank = 4 + ss
                        tts.append(p.step("tensor", lambda e, ec=ec, ss=ss, bank=bank, src=src:
                                          e.matmul(PS[:, bank, 0:128],
                                                   src[:, ec, ss * 128:(ss + 1) * 128], identb[:],
                                                   start=True, stop=True),
                                          waits=[t_kev[ec], tb4[ss]], serial=False))
                    tev = p.step("scalar", lambda e, ec=ec, dst=dst: e.activation(
                        dst[:, 0:4, ec * 128:(ec + 1) * 128].rearrange("p s j -> p (s j)")
                        if False else dst[:, 0:4, ec * 128:(ec + 1) * 128],
                        PS[:, 4:8, 0:128], AF.Copy), waits=tts)
                    for ss in range(4):
                        tb4[ss] = tev
                        t_ktev[ss] = tev
            bfree[4], bfree[5], bfree[6], bfree[7] = tb4
            t_scev = [None] * NS
            if not is_comp:
                for ss in range(NS):
                    bank = 6 + (ss % 2)
                    def smm(e, ss=ss, bank=bank):
                        o = PS[:, bank, :]
                        last = None
                        for ec in range(NE):
                            last = e.matmul(o, kh[:, ec, ss * 128:(ss + 1) * 128], qh[:, ec, :],
                                            start=(ec == 0), stop=False)
                            last = e.matmul(o, kl[:, ec, ss * 128:(ss + 1) * 128], qh[:, ec, :],
                                            start=False, stop=False)
                            last = e.matmul(o, kh[:, ec, ss * 128:(ss + 1) * 128], ql[:, ec, :],
                                            start=False, stop=(ec == NE - 1))
                        return last
                    tsc = p.step("tensor", smm, waits=t_kev + t_qev + [bfree[bank]], serial=False)
                    e1 = p.step("vector", lambda e, ss=ss, bank=bank, i=i: e.scalar_tensor_tensor(
                        scm32[:], PS[:, bank, :], stf[:, ss, 4:5], msk[:, i, ss, :],
                        ALU.mult, ALU.mult), waits=[tsc, t_stats])
                    bfree[bank] = e1
                    e2 = p.step("scalar", lambda e, ss=ss: e.activation(sch[:, ss, :], scm32[:],
                                                                        AF.Copy), waits=[e1])
                    t_scev[ss] = p.step("vector", lambda e, ss=ss: e.tensor_tensor(
                        scl_[:, ss, :], scm32[:], sch[:, ss, :], ALU.subtract), waits=[e2])
            if is_comp and i == 0:
                t_vev = proj_finish(v0_groups, vth, vtl, csr_v[:, 0, :], False, t_nm, t_stats,
                                    extra=(lambda g: stf[:, g, 4:5], (vsh, vsl),
                                           lambda g, i=i: einv2[:, g, i:i + 1]), b0=0)
            else:
                t_vev = proj_pass(vth, vtl, s_w["vh"][i], s_w["vl"][i], csr_v[:, i, :], False,
                                  t_nm, t_xt, t_stats, wcnt,
                                  extra=(lambda g: stf[:, g, 4:5], (vsh, vsl),
                                         lambda g, i=i: einv2[:, g, i:i + 1]))
            t_wc = []
            if not is_comp:
                for ts in range(NS):
                    bw = 6 if ts % 2 == 0 else 7
                    bc = 4 if ts % 2 == 0 else 5
                    def wmm(e, ts=ts, bw=bw):
                        o = PS[:, bw, :]
                        last = None
                        for ss in range(NS):
                            last = e.matmul(o, sch[:, ss, ts * 128:(ts + 1) * 128], vth[:, ss, :],
                                            start=(ss == 0), stop=False)
                            last = e.matmul(o, scl_[:, ss, ts * 128:(ts + 1) * 128], vth[:, ss, :],
                                            start=False, stop=False)
                            last = e.matmul(o, sch[:, ss, ts * 128:(ts + 1) * 128], vtl[:, ss, :],
                                            start=False, stop=(ss == NS - 1))
                        return last
                    tw_ = p.step("tensor", wmm, waits=t_scev + t_vev + [bfree[bw]], serial=False)
                    def cmm(e, ts=ts, bc=bc, i=i):
                        o = PS[:, bc, :]
                        last = None
                        for ec in range(NE):
                            last = e.matmul(o, qh[:, ec, ts * 128:(ts + 1) * 128], Sh[:, i, ec, :],
                                            start=(ec == 0), stop=False)
                            last = e.matmul(o, ql[:, ec, ts * 128:(ts + 1) * 128], Sh[:, i, ec, :],
                                            start=False, stop=False)
                            last = e.matmul(o, qh[:, ec, ts * 128:(ts + 1) * 128], Sl[:, i, ec, :],
                                            start=False, stop=(ec == NE - 1))
                        return last
                    tc_ = p.step("tensor", cmm,
                                 waits=t_qev + [tok["S_ready"][i], bfree[bc]], serial=False)
                    e1 = p.step("vector", lambda e, ts=ts, bc=bc, i=i: e.tensor_scalar_mul(
                        rc32[:], PS[:, bc, :], cvec[:, ts, i:i + 1]), waits=[tc_])
                    bfree[bc] = e1
                    e2 = p.step("vector", lambda e, bw=bw: e.tensor_tensor(
                        ra32[:], PS[:, bw, :], rc32[:], ALU.add), waits=[e1, tw_])
                    bfree[bw] = e2
                    if i == 0:
                        e3 = p.step("vector", lambda e, ts=ts, i=i: e.tensor_scalar_mul(
                            blsb[:, ts, :], ra32[:], giv[:, ts, i:i + 1]), waits=[e2, t_gate])
                    else:
                        e2b = p.step("vector", lambda e, ts=ts, i=i: e.tensor_scalar_mul(
                            ra32[:], ra32[:], giv[:, ts, i:i + 1]), waits=[e2, t_gate])
                        e3 = p.step("vector", lambda e, ts=ts: e.tensor_tensor(
                            blsb[:, ts, :], blsb[:, ts, :], ra32[:], ALU.add), waits=[e2b])
                    t_wc.append(e3)
                    tok["S_read"][i] = tc_
            t_sup = []
            for ek in range(NE):
                def supmm(e, ek=ek, i=i, cx=cx):
                    o = PS[:, 4 + ek, :]
                    last = None
                    fs = True
                    if cx != 0:
                        last = e.matmul(o, dgh[:, i, :], Sh[:, i, ek, :], start=True, stop=False)
                        last = e.matmul(o, dgh[:, i, :], Sl[:, i, ek, :], start=False, stop=False)
                        last = e.matmul(o, dgl[:, i, :], Sh[:, i, ek, :], start=False, stop=False)
                        fs = False
                    for ss in range(NS):
                        last = e.matmul(o, kth[:, ss, ek * 128:(ek + 1) * 128], vsh[:, ss, :],
                                        start=fs, stop=False)
                        fs = False
                        last = e.matmul(o, ktl[:, ss, ek * 128:(ek + 1) * 128], vsh[:, ss, :],
                                        start=False, stop=False)
                        last = e.matmul(o, kth[:, ss, ek * 128:(ek + 1) * 128], vsl[:, ss, :],
                                        start=False, stop=(ss == NS - 1))
                    return last
                t_sup.append(p.step("tensor", supmm,
                                    waits=t_ktev + t_vev + [tok["S_ready"][i], bfree[4 + ek]],
                                    serial=False))
            t_sev = None
            for ek in range(NE):
                e1 = p.step("scalar", lambda e, ek=ek, i=i: e.activation(
                    Sh[:, i, ek, :], PS[:, 4 + ek, :], AF.Copy), waits=[t_sup[ek], tok["S_read"][i]])
                t_sev = p.step("vector", lambda e, ek=ek, i=i: e.tensor_tensor(
                    Sl[:, i, ek, :], PS[:, 4 + ek, :], Sh[:, i, ek, :], ALU.subtract), waits=[e1])
                bfree[4 + ek] = t_sev
            tok["S_ready"][i] = t_sev
        if not is_comp:
            t_blev = [None] * NE
            tpar = [bfree[6], bfree[7]]
            for idx in range(16):
                ts, ec = idx // 4, idx % 4
                bank = 6 + (idx % 2)
                tt = p.step("tensor", lambda e, ts=ts, ec=ec, bank=bank: e.transpose(
                    PS[:, bank, 0:128], blsb[:, ts, ec * 128:(ec + 1) * 128], ident[:]),
                    waits=[t_wc[ts], tpar[idx % 2]], serial=False)
                th_ = p.step("vector", lambda e, ts=ts, ec=ec, bank=bank: e.tensor_copy(
                    blth[:, ec, ts * 128:(ts + 1) * 128], PS[:, bank, 0:128]), waits=[tt])
                tv = p.step("vector", lambda e, ts=ts, ec=ec, bank=bank: e.tensor_tensor(
                    bltl[:, ec, ts * 128:(ts + 1) * 128], PS[:, bank, 0:128],
                    blth[:, ec, ts * 128:(ts + 1) * 128], ALU.subtract), waits=[th_])
                tpar[idx % 2] = tv
                t_blev[ec] = tv
            bfree[6], bfree[7] = tpar
            t_r1 = None
            for dq in range(4):
                gmm = [None] * NS
                for ec in range(NE):
                    tw, par = load_wslab(s_w["oh"], s_w["ol"], ec * 128, dq * 512, (dq + 1) * 512,
                                         wcnt[0])
                    wcnt[0] += 1
                    for ts in range(NS):
                        def omm(e, ts=ts, ec=ec, par=par):
                            o = PS[:, ts, :]
                            e.matmul(o, blth[:, ec, ts * 128:(ts + 1) * 128], wsl_h[:, par, :],
                                     start=(ec == 0), stop=False)
                            e.matmul(o, bltl[:, ec, ts * 128:(ts + 1) * 128], wsl_h[:, par, :],
                                     start=False, stop=False)
                            return e.matmul(o, blth[:, ec, ts * 128:(ts + 1) * 128],
                                            wsl_l[:, par, :], start=False, stop=(ec == NE - 1))
                        gmm[ts] = p.step("tensor", omm,
                                         waits=[tw, t_blev[ec]] +
                                               ([bfree[ts]] if ec == 0 else []), serial=False)
                    tok["wslab"][par] = gmm[NS - 1]
                for ts in range(NS):
                    e2 = p.step("vector", lambda e, ts=ts, dq=dq: e.scalar_tensor_tensor(
                        xsb[:, ts, dq * 512:(dq + 1) * 512], PS[:, ts, :], sbc[:, 12:13],
                        xsb[:, ts, dq * 512:(dq + 1) * 512], ALU.mult, ALU.add),
                        waits=[gmm[ts], t_xs[ts]])
                    bfree[ts] = e2
                    if dq == 3:
                        tok["xsb_free"][ts] = p.dma(
                            "gpsimd", "st_r1", r1d[base + ts * 128:base + (ts + 1) * 128, :],
                            xsb[:, ts, :], waits=[e2])
        else:
            for ts in range(NS):
                tok["xsb_free"][ts] = t_stats
        tok["xcast"] = [None] * ND
        tok["xth_cons"] = ("p_tensor", p.ctr["p_tensor"])

    cx = 0
    for j in range(noch - 1, -1, -1):
        do_chunk(xo, xot, j * C, True, cx)
        cx += 1
    for j in range(nch - 1, -1, -1):
        do_chunk(x, xt, j * C, False, cx)
        cx += 1
    n_r1_stores = p.ctr["st_r1"]
    st1.close()
    ap_st.close()

    p.barrier()

    ffn_st = ExitStack()
    sbf = lambda nm, sh, dt: ffn_st.enter_context(nc.sbuf_tensor(nm, sh, dt))
    bth = sbf("bth", [128, ND, n_own], BF16)
    btl = sbf("btl", [128, ND, n_own], BF16)
    ssqP = sbf("ssqP", [128, n_own], F32)
    t_ssq0 = p.step("vector", lambda e: e.memset(ssqP[:], 0.0))

    st2 = ExitStack()
    sb2 = lambda nm, sh, dt: st2.enter_context(nc.sbuf_tensor(nm, sh, dt))
    r1sb = sb2("r1sb", [128, 2, DM], F32)
    sq2 = sb2("sq2", [128, DM], F32)
    b32 = sq2
    st2f = sb2("st2f", [128, NS, 8], F32)
    t_r1free = [None, None]
    t_blast = None
    r1tok = ("st_r1", n_r1_stores)
    for gsub in range(NTQ * NS):
        tc2, s = gsub // NS, gsub % NS
        pr2 = gsub % 2
        t_x = p.dma("sync", "d_r1%d" % pr2, r1sb[:, pr2, :],
                    r1d[gsub * 128:(gsub + 1) * 128, :],
                    waits=[t_r1free[pr2], r1tok])
        if True:
            tq_ = p.step("scalar", lambda e, pr2=pr2: e.activation(sq2[:], r1sb[:, pr2, :], AF.Square),
                         waits=[t_x, t_blast])
            ta = p.step("vector", lambda e, s=s, pr2=pr2: e.reduce_sum(st2f[:, s, 0:1], r1sb[:, pr2, :], AX.X),
                        waits=[t_x])
            tb = p.step("vector", lambda e, s=s: e.reduce_sum(st2f[:, s, 6:7], sq2[:], AX.X),
                        waits=[tq_])
            tc1 = p.step("vector", lambda e, s=s: e.tensor_scalar_mul(
                st2f[:, s, 0:1], st2f[:, s, 0:1], 1.0 / DM), waits=[ta])
            td = p.step("vector", lambda e, s=s: e.tensor_scalar(
                st2f[:, s, 1:2], st2f[:, s, 6:7], 1.0 / DM, EPS_LN, ALU.mult, ALU.add), waits=[tb])
            te = p.step("vector", lambda e, s=s: e.tensor_scalar_mul(
                st2f[:, s, 2:3], st2f[:, s, 0:1], st2f[:, s, 0:1]), waits=[tc1])
            tf = p.step("vector", lambda e, s=s: e.scalar_tensor_tensor(
                st2f[:, s, 3:4], st2f[:, s, 2:3], -1.0, st2f[:, s, 1:2], ALU.mult, ALU.add),
                waits=[td, te])
            tg = p.step("vector", lambda e, s=s: e.reciprocal(st2f[:, s, 4:5], st2f[:, s, 3:4]),
                        waits=[tf])
            tn = p.step("scalar", lambda e, s=s: e.activation(
                st2f[:, s, 4:5], st2f[:, s, 4:5], AF.Sqrt), waits=[tg])
            for _ in range(2):
                n1 = p.step("scalar", lambda e, s=s: e.activation(
                    st2f[:, s, 6:7], st2f[:, s, 4:5], AF.Square), waits=[tn])
                n2 = p.step("vector", lambda e, s=s: e.tensor_tensor(
                    st2f[:, s, 6:7], st2f[:, s, 6:7], st2f[:, s, 3:4], ALU.mult), waits=[n1])
                n3 = p.step("vector", lambda e, s=s: e.tensor_scalar(
                    st2f[:, s, 6:7], st2f[:, s, 6:7], -0.5, 1.5, ALU.mult, ALU.add), waits=[n2])
                tn = p.step("vector", lambda e, s=s: e.tensor_tensor(
                    st2f[:, s, 4:5], st2f[:, s, 4:5], st2f[:, s, 6:7], ALU.mult), waits=[n3])
            tbb = p.step("vector", lambda e, s=s, pr2=pr2: e.tensor_scalar(
                b32[:], r1sb[:, pr2, :], st2f[:, s, 0:1], st2f[:, s, 4:5],
                ALU.subtract, ALU.mult), waits=[tn, t_x, tb])
            tlast = None
            col = gsub * 128
            for d0 in range(0, ND, 4):
                tts = []
                for si in range(4):
                    dch = d0 + si
                    bank = 4 + si
                    tts.append(p.step("tensor", lambda e, dch=dch, bank=bank: e.transpose(
                        PS[:, bank, 0:128], b32[:, dch * 128:(dch + 1) * 128], ident[:]),
                        waits=[tbb, bfree[bank]], serial=False))
                th_ = p.step("scalar", lambda e, d0=d0, col=col: e.activation(
                    bth[:, d0:d0 + 4, col:col + 128], PS[:, 4:8, 0:128], AF.Copy),
                    waits=tts)
                tv = p.step("vector", lambda e, d0=d0, col=col: e.tensor_tensor(
                    btl[:, d0:d0 + 4, col:col + 128], PS[:, 4:8, 0:128],
                    bth[:, d0:d0 + 4, col:col + 128], ALU.subtract), waits=[th_])
                for si in range(4):
                    bfree[4 + si] = tv
                tlast = tv
            t_r1free[pr2] = tlast
            t_blast = tlast
    st2.close()

    p.barrier()

    st3 = ExitStack()
    sb3 = lambda nm, sh, dt: st3.enter_context(nc.sbuf_tensor(nm, sh, dt))
    QF = 1024
    NG = QF // 64
    wstq = sb3("wstq", [128, 2, QF], F32)
    wbb = sb3("wbb", [128, 2, QF], BF16)
    redq = sb3("redq", [128, NG], F32)
    sbq = sb3("sbq", [128, NG], BF16)
    lnq = sb3("lnq", [128, NG], F32)
    rcq = sb3("rcq", [128, 2, NG], F32)
    rtq = sb3("rtq", [128, QF], BF16)
    g1q = sb3("g1q", [128, QF], BF16)
    nqq = sb3("nqq", [128, QF], BF16)
    whq = rtq
    d1q = g1q
    wqv = sb3("wqv", [128, QF], BF16)
    ustg = sb3("ustg", [128, 2, 8, 128], U16)
    wsg = sb3("wsg", [128, 2, ND, 128], BF16)
    wsu = sb3("wsu", [128, 2, ND, 128], BF16)
    uslab = sb3("uslab", [128, 2, 2, ND, 128], U16)
    silu_s = sb3("silu_s", [128, 1024], F32)
    h32 = sb3("h32", [128, 1024], F32)
    hsq = silu_s
    hhs = sb3("hhs", [128, 1024], BF16)
    hls = sb3("hls", [128, 1024], BF16)
    qs = {"par": 0, "wst_free": [[], []], "wbb_free": [None, None],
          "rcq_free": [None, None], "ustg_free": [None, None],
          "tbank": [None, None, None, None], "last_store": None, "g1_free": None,
          "rtq_free": None}

    def quant_chain(load_src, dst_hbm_ap, transpose):
        par = qs["par"]
        qs["par"] ^= 1
        tl = p.dma("sync", "d_wq%d" % par, wstq[:, par, :], load_src,
                   waits=list(qs["wst_free"][par]))
        tcp = p.step("scalar", lambda e, par=par: e.activation(wbb[:, par, :], wstq[:, par, :],
                                                               AF.Copy),
                     waits=[tl, qs["wbb_free"][par]])
        qs["wst_free"][par] = [tcp]
        trd = p.step("vector", lambda e, par=par: e.reduce_sum(
            redq[:], wbb[:, par, :].rearrange("p (g k) -> p g k", k=64), AX.X,
            apply_absolute_value=True), waits=[tcp])
        tcl = p.step("vector", lambda e: e.tensor_scalar(
            sbq[:], redq[:], 1.0 / 64, 1e-8, ALU.mult, ALU.max), waits=[trd])
        tln = p.step("scalar", lambda e: e.activation(lnq[:], sbq[:], AF.Ln), waits=[tcl])
        tex = p.step("scalar", lambda e, par=par: e.activation(rcq[:, par, :], lnq[:], AF.Exp,
                                                               scale=-1.0),
                     waits=[tln, qs["rcq_free"][par]])
        rc_rep = bass.AP(rcq.ap().tensor, par * NG, [[2 * NG, 128], [1, NG], [0, 64]])
        sb_rep = bass.AP(sbq.ap().tensor, 0, [[NG, 128], [1, NG], [0, 64]])
        g3 = lambda t: t.rearrange("p (g k) -> p g k", k=64)
        trt = p.step("vector", lambda e, par=par: e.tensor_tensor(
            g3(rtq[:]), g3(wbb[:, par, :]), rc_rep, ALU.mult), waits=[tex, qs["rtq_free"]])
        qs["rcq_free"][par] = trt
        tgt = p.step("gpsimd", lambda e: e.tensor_scalar(g1q[:], rtq[:], 0.5, None, ALU.is_gt),
                     waits=[trt, qs["g1_free"]])
        tnq = p.step("gpsimd", lambda e: e.tensor_scalar(
            nqq[:], rtq[:], -0.5, None, ALU.is_lt), waits=[tgt])
        tnq = p.step("gpsimd", lambda e: e.tensor_tensor(
            nqq[:], nqq[:], g1q[:], ALU.subtract), waits=[tnq])
        twh = p.step("vector", lambda e: e.scalar_tensor_tensor(
            g3(whq[:]), g3(nqq[:]), -1.0, sb_rep, ALU.mult, ALU.mult), waits=[tnq])
        td1 = p.step("gpsimd", lambda e, par=par: e.tensor_tensor(
            d1q[:], whq[:], wbb[:, par, :], ALU.subtract), waits=[twh])
        tqv = p.step("gpsimd", lambda e, par=par: e.tensor_tensor(
            wqv[:], wbb[:, par, :], d1q[:], ALU.add), waits=[td1, qs["last_store"]])
        qs["g1_free"] = tqv
        qs["rtq_free"] = td1
        qs["wbb_free"][par] = tqv
        if not transpose:
            tst = p.dma("gpsimd", "st_wq", dst_hbm_ap, wqv[:].bitcast(U16), waits=[tqv])
        else:
            ev = []
            for half in range(2):
                tts = []
                for si in range(4):
                    sub = half * 4 + si
                    bank = 4 + si
                    tts.append(p.step("tensor", lambda e, sub=sub, bank=bank: e.matmul(
                        PS[:, bank, 0:128], wqv[:, sub * 128:(sub + 1) * 128], identb[:],
                        start=True, stop=True), waits=[tqv, qs["tbank"][si]], serial=False))
                tev = p.step("scalar", lambda e, half=half, par=par: e.activation(
                    ustg[:, par, half * 4:half * 4 + 4, :].bitcast(BF16), PS[:, 4:8, 0:128],
                    AF.Copy), waits=tts + [qs["ustg_free"][par]])
                for si in range(4):
                    qs["tbank"][si] = tev
                ev.append(tev)
            tst = p.dma("gpsimd", "st_wq", dst_hbm_ap, ustg[:, par, :, :], waits=ev)
            qs["ustg_free"][par] = tst
        qs["last_store"] = tst
        return tst

    t_hh_free = [None]
    t_ssq = [t_ssq0]
    t_uslab_free = [None, None]
    t_wsgu_free = [None, None]

    def emit_mm_item(jj, th2, wq_tok, tctx=[None]):
        par = jj % 2
        tcol = th2 * 1024
        if th2 == 0:
            l1 = p.dma("sync", "d_sl%d" % par, uslab[:, par, 0, :, :],
                       wgh_n[jj * 128:(jj + 1) * 128, :].rearrange("p (c n) -> p c n", n=128),
                       waits=[t_uslab_free[par], wq_tok])
            l2 = p.dma("sync", "d_sl%d" % par, uslab[:, par, 1, :, :],
                       wgh_n[Hs + jj * 128:Hs + (jj + 1) * 128, :]
                       .rearrange("p (c n) -> p c n", n=128))
            evs = []
            for gu_i in range(2):
                dst = wsg if gu_i == 0 else wsu
                for d0 in range(0, ND, 4):
                    tts = []
                    for si in range(4):
                        d = d0 + si
                        bank = 4 + si
                        tts.append(p.step("tensor", lambda e, gu_i=gu_i, d=d, bank=bank, par=par:
                                          e.matmul(PS[:, bank, 0:128],
                                                   uslab[:, par, gu_i, d, :].bitcast(BF16),
                                                   identb[:], start=True, stop=True),
                                          waits=[l2, qs["tbank"][si]], serial=False))
                    tev = p.step("scalar", lambda e, dst=dst, d0=d0, par=par: e.activation(
                        dst[:, par, d0:d0 + 4, :], PS[:, 4:8, 0:128], AF.Copy),
                        waits=tts + [t_wsgu_free[par]])
                    for si in range(4):
                        qs["tbank"][si] = tev
                    evs.append(tev)
            tctx[0] = evs
        evs = tctx[0]
        def gatemm(e, par=par, tcol=tcol):
            last = None
            for d in range(ND):
                lh = wsg[:, par, d, :]
                for half in range(2):
                    cs = tcol + half * 512
                    last = e.matmul(PS[:, half, :], lh, bth[:, d, cs:cs + 512],
                                    start=(d == 0), stop=False)
                    last = e.matmul(PS[:, half, :], lh, btl[:, d, cs:cs + 512],
                                    start=False, stop=(d == ND - 1))
            return last
        def upmm(e, par=par, tcol=tcol):
            last = None
            for d in range(ND):
                lh = wsu[:, par, d, :]
                for half in range(2):
                    cs = tcol + half * 512
                    last = e.matmul(PS[:, 2 + half, :], lh, bth[:, d, cs:cs + 512],
                                    start=(d == 0), stop=False)
                    last = e.matmul(PS[:, 2 + half, :], lh, btl[:, d, cs:cs + 512],
                                    start=False, stop=(d == ND - 1))
            return last
        tg_ = p.step("tensor", gatemm, waits=evs + [bfree[0], bfree[1]], serial=False)
        tu_ = p.step("tensor", upmm, waits=evs + [bfree[2], bfree[3]], serial=False)
        t_uslab_free[par] = tu_
        t_wsgu_free[par] = tu_
        return (jj, tcol, tg_, tu_)

    def emit_evac(ctx):
        jj, tcol, tg_, tu_ = ctx
        hrow = jj * 128
        e1 = p.step("scalar", lambda e: e.activation(silu_s[:], PS[:, 0:2, :], AF.Silu),
                    waits=[tg_, t_ssq[0]])
        e2 = p.step("vector", lambda e: e.tensor_tensor(h32[:], silu_s[:], PS[:, 2:4, :],
                                                        ALU.mult), waits=[e1, tu_])
        bfree[0] = e2; bfree[1] = e2; bfree[2] = e2; bfree[3] = e2
        e3 = p.step("scalar", lambda e: e.activation(hsq[:], h32[:], AF.Square), waits=[e2])
        t_ssq[0] = p.step("vector", lambda e, tcol=tcol: e.tensor_tensor(
            ssqP[:, tcol:tcol + 1024], ssqP[:, tcol:tcol + 1024], hsq[:], ALU.add),
            waits=[e3, t_ssq[0]])
        e5 = p.step("scalar", lambda e: e.activation(hhs[:], h32[:], AF.Copy),
                    waits=[e2, t_hh_free[0]])
        e6 = p.step("vector", lambda e: e.tensor_tensor(hls[:], h32[:], hhs[:], ALU.subtract),
                    waits=[e5])
        p.dma("sync", "st_h", hhi[hrow:hrow + 128, tcol:tcol + 1024], hhs[:], waits=[e6])
        t_hh_free[0] = p.dma("sync", "st_h", hlo[hrow:hrow + 128, tcol:tcol + 1024],
                             hls[:], waits=[e6])
        return tu_

    NBP = Hs // 1024
    NJT = 1024 // 128
    NTH = max(1, n_own // 1024)
    NWPC = (DM // 128) * (Hs // 1024)
    wp_i = [0]
    wq_tok_of = {}
    pend = []
    for bp in range(NBP + 1):
        if bp < NBP:
            qthunks = []
            for jr in range(8 * bp, 8 * (bp + 1)):
                for dh in range(2):
                    qthunks.append((wgun[jr * 128:(jr + 1) * 128, dh * 1024:(dh + 1) * 1024],
                                    wgh_n[jr * 128:(jr + 1) * 128, dh * 1024:(dh + 1) * 1024],
                                    False))
                    qthunks.append((wgun[Hs + jr * 128:Hs + (jr + 1) * 128, dh * 1024:(dh + 1) * 1024],
                                    wgh_n[Hs + jr * 128:Hs + (jr + 1) * 128, dh * 1024:(dh + 1) * 1024],
                                    False))
            w0 = (NWPC * bp) // NBP
            w1 = (NWPC * (bp + 1)) // NBP
            for wi in range(w0, w1):
                er, h8 = wi // (Hs // 1024), wi % (Hs // 1024)
                qthunks.append((wpn[er * 128:(er + 1) * 128, h8 * 1024:(h8 + 1) * 1024],
                                wph[h8 * 1024:(h8 + 1) * 1024, er * 128:(er + 1) * 128]
                                .rearrange("(s p) j -> p s j", p=128),
                                True))
        else:
            qthunks = []
        items = [(jj, th2) for jj in range((bp - 1) * NJT, bp * NJT) for th2 in range(NTH)]             if bp > 0 else []
        nq, ni = len(qthunks), len(items)
        qi = ii = 0
        last_st = None
        prev_item = None
        while qi < nq or ii < ni:
            if qi < nq and (ii >= ni or qi * max(ni, 1) <= ii * nq):
                last_st = quant_chain(*qthunks[qi]); qi += 1
            else:
                jj, th2 = items[ii]
                if th2 == 0:
                    cur_tctx = [None]
                if prev_item is not None:
                    emit_evac(prev_item)
                prev_item = emit_mm_item(jj, th2, wq_tok_of[bp - 1], tctx=cur_tctx)
                ii += 1
        if prev_item is not None:
            emit_evac(prev_item)
        if bp < NBP:
            wq_tok_of[bp] = ("st_wq", p.ctr["st_wq"])
    n_h_stores = p.ctr["st_h"]
    n_wp_stores = p.ctr["st_wq"]
    st3.close()

    p.barrier()

    st3b = ExitStack()
    rowt = st3b.enter_context(nc.sbuf_tensor("rowt", [1, 4, 512], F32))
    for tq in range(NTQ):
        tm = p.step("tensor", lambda e, tq=tq: e.matmul(
            PS[0:1, 7, :], ones1fs[:], ssqP[:, tq * 512:(tq + 1) * 512],
            start=True, stop=True), waits=[bfree[7]], serial=False)
        t1 = p.step("vector", lambda e: e.tensor_scalar(
            rowt[:, 0, :], PS[0:1, 7, :], 1.0 / Hs, EPS_RMS, ALU.mult, ALU.add), waits=[tm])
        bfree[7] = t1
        t2 = p.step("vector", lambda e: e.reciprocal(rowt[:, 1, :], rowt[:, 0, :]), waits=[t1])
        t3 = p.step("scalar", lambda e: e.activation(rowt[:, 1, :], rowt[:, 1, :], AF.Sqrt),
                    waits=[t2])
        tn = t3
        for _ in range(2):
            n1 = p.step("scalar", lambda e: e.activation(rowt[:, 2, :], rowt[:, 1, :], AF.Square),
                        waits=[tn])
            n2 = p.step("vector", lambda e: e.tensor_tensor(rowt[:, 2, :], rowt[:, 2, :],
                                                            rowt[:, 0, :], ALU.mult), waits=[n1])
            n3 = p.step("vector", lambda e: e.tensor_scalar(rowt[:, 2, :], rowt[:, 2, :],
                                                            -0.5, 1.5, ALU.mult, ALU.add),
                        waits=[n2])
            tn = p.step("vector", lambda e: e.tensor_tensor(rowt[:, 1, :], rowt[:, 1, :],
                                                            rowt[:, 2, :], ALU.mult), waits=[n3])
        for s in range(NS):
            tt = p.step("tensor", lambda e, s=s: e.transpose(
                PS[:, 6, 0:1], rowt[0:1, 1, s * 128:(s + 1) * 128], onesrs[0:1, 0:1]),
                waits=[tn, bfree[6]], serial=False)
            tv = p.step("vector", lambda e, tq=tq, s=s: e.tensor_copy(
                invr[:, tq, s:s + 1], PS[:, 6, 0:1]), waits=[tt])
            bfree[6] = tv
    st3b.close()

    p.barrier()

    st4 = ExitStack()
    sb4 = lambda nm, sh, dt: st4.enter_context(nc.sbuf_tensor(nm, sh, dt))
    hts_h = sb4("hts_h", [128, 4, 512], BF16)
    hts_l = sb4("hts_l", [128, 4, 512], BF16)
    wps = sb4("wps", [128, 4, 1024], U16)
    o32s = sb4("o32s", [128, 1024], F32)
    r1s = sb4("r1s", [128, 1024], F32)
    ots = sb4("ots", [128, 1024], F32)
    htok = ("st_h", n_h_stores)
    wptok = ("st_wq", n_wp_stores)
    r1tok = ("st_r1", n_r1_stores)
    t_out_dma = None
    t_hfree = [None, None, None, None]
    t_r1s_free = None
    for tp in range(NTQ):
        for eh in range(2):
            gtoks = [None] * NS
            for hc in range(NHC):
                par = hc % 4
                ta_ = p.dma("sync", "d_hh%d" % par, hts_h[:, par, :],
                      hhi[hc * 128:(hc + 1) * 128, tp * 512:(tp + 1) * 512],
                      waits=[t_hfree[par], htok])
                tb_ = p.dma("gpsimd", "d_hl%d" % par, hts_l[:, par, :],
                      hlo[hc * 128:(hc + 1) * 128, tp * 512:(tp + 1) * 512],
                      waits=[t_hfree[par], htok])
                tl3 = p.dma("sync", "d_hw%d" % par, wps[:, par, :],
                            wph[hc * 128:(hc + 1) * 128, eh * 1024:(eh + 1) * 1024],
                            waits=[t_hfree[par], wptok])
                for ts in range(NS):
                    def rmm(e, ts=ts, par=par, hc=hc):
                        last = None
                        for half in range(2):
                            o = PS[:, ts * 2 + half, :]
                            wslab = wps[:, par, half * 512:(half + 1) * 512].bitcast(BF16)
                            last = e.matmul(o, hts_h[:, par, ts * 128:(ts + 1) * 128], wslab,
                                            start=(hc == 0), stop=False)
                            last = e.matmul(o, hts_l[:, par, ts * 128:(ts + 1) * 128], wslab,
                                            start=False, stop=(hc == NHC - 1))
                        return last
                    gtoks[ts] = p.step("tensor", rmm,
                                       waits=[ta_, tb_, tl3] + ([bfree[ts * 2], bfree[ts * 2 + 1]]
                                                                if hc == 0 else []), serial=False)
                t_hfree[par] = gtoks[NS - 1]
            for ts in range(NS):
                rows = tp * 512 + ts * 128
                e1 = p.step("vector", lambda e, ts=ts, tp=tp: e.tensor_scalar_mul(
                    o32s[:], PS[:, ts * 2:ts * 2 + 2, :], invr[:, tp, ts:ts + 1]),
                    waits=[gtoks[ts]])
                bfree[ts * 2] = e1
                bfree[ts * 2 + 1] = e1
                tlr = p.dma("sync", "d_r1s", r1s[:], r1d[rows:rows + 128, eh * 1024:(eh + 1) * 1024],
                            waits=[t_r1s_free, r1tok])
                e2 = p.step("vector", lambda e: e.tensor_tensor(ots[:], o32s[:], r1s[:], ALU.add),
                            waits=[e1, tlr, t_out_dma])
                t_r1s_free = e2
                t_out_dma = p.dma("gpsimd", "st_out", out[rows:rows + 128, eh * 1024:(eh + 1) * 1024],
                                  ots[:], waits=[e2])
    p.step("gpsimd", lambda e: None, waits=[t_out_dma], serial=False)
    st4.close()
    ffn_st.close()

    p.emit_all()
    st.close()
    return nc


def _host_consts():
    import ml_dtypes
    BF = ml_dtypes.bfloat16
    sig = np.arange(C)[:, None]
    ta = np.arange(C)[None, :]
    expnt = np.where(sig > ta, (sig - ta - 1).astype(np.float32),
                     np.float32(1e30)).astype(np.float32)
    expn_in = np.ascontiguousarray(expnt.reshape(4, 128, C))
    tauv_in = np.ascontiguousarray((C - 1 - np.arange(C, dtype=np.float32)).reshape(4, 128, 1))
    sigv_in = np.ascontiguousarray(np.arange(C, dtype=np.float32).reshape(4, 128, 1))
    id32_in = np.eye(128, dtype=np.float32)
    seg_in = np.zeros((128, 2), dtype=np.float32)
    seg_in[0:64, 0] = 1.0 / 64
    seg_in[64:128, 1] = 1.0 / 64
    segt_in = np.zeros((2, 128), dtype=np.float32)
    segt_in[0, 0:64] = 1.0
    segt_in[1, 64:128] = 1.0
    return dict(expn=expn_in, tauv=tauv_in, sigv=sigv_in, id32=id32_in,
                idbf=id32_in.astype(BF), seg=seg_in.astype(BF),
                segtb=segt_in.astype(BF), segtf=segt_in,
                onesr=np.ones((1, 128), dtype=np.float32),
                ones1f=np.ones((128, 1), dtype=np.float32),
                ones1b=np.ones((128, 1), dtype=np.float32).astype(BF))


def kernel(L0_attn_x, L0_attn_Wq, L0_attn_Wk, L0_attn_Wv, L0_attn_Wo, L0_attn_Wg,
           L0_attn_decay_logits, L0_attn_out_scale, L0_ffn_w_gate_up, L0_ffn_w_proj):
    xfull = np.ascontiguousarray(np.asarray(L0_attn_x, dtype=np.float32))
    B, T, _ = xfull.shape
    half = T // 2
    wq_in = np.ascontiguousarray(np.asarray(L0_attn_Wq, np.float32))
    wk_in = np.ascontiguousarray(np.asarray(L0_attn_Wk, np.float32))
    wv_in = np.ascontiguousarray(np.asarray(L0_attn_Wv, np.float32))
    wo_in = np.ascontiguousarray(np.asarray(L0_attn_Wo, np.float32))
    wg_in = np.zeros((DM, 8), dtype=np.float32)
    wg_in[:, 0:3] = np.asarray(L0_attn_Wg, np.float32)
    wgun_in = np.ascontiguousarray(np.asarray(L0_ffn_w_gate_up, np.float32))
    wpn_in = np.ascontiguousarray(np.asarray(L0_ffn_w_proj, np.float32))
    dl = np.asarray(L0_attn_decay_logits, np.float32)
    osc = np.float32(L0_attn_out_scale)
    consts = _host_consts()

    nc = build_nc(half, half, wpn_in.shape[1])
    in_maps = []
    for c in range(8):
        b, hf = c // 2, c % 2
        t0 = hf * half
        xc = np.ascontiguousarray(xfull[b, t0:t0 + half])
        xtc = np.ascontiguousarray(xfull[b, t0:t0 + half].T)
        if hf == 0:
            xoc = np.ascontiguousarray(xfull[b, half:])
            xotc = np.ascontiguousarray(xfull[b, half:].T)
            smask = 1.0
        else:
            xoc, xotc, smask = xc, xtc, 0.0
        cstv = np.zeros((1, 16), dtype=np.float32)
        cstv[0, 0:3] = dl
        cstv[0, 3] = osc
        cstv[0, 4] = smask
        cstv[0, 5] = float(C)
        m = dict(x=xc, xt=xtc, xo=xoc, xot=xotc, wq=wq_in, wk=wk_in, wv=wv_in,
                 wo=wo_in, wg=wg_in, cst=cstv, wgun=wgun_in, wpn=wpn_in, **consts)
        in_maps.append(m)
    res = run_bass_kernel_spmd(nc, in_maps, core_ids=list(range(8)))
    outv = np.empty_like(xfull)
    for c in range(8):
        b, hf = c // 2, c % 2
        outv[b, hf * half:(hf + 1) * half] = res.results[c]["out"]
    return outv


# version 21
# speedup vs baseline: 1.7196x; 1.7196x over previous
import numpy as np
import concourse.bass as bass
import concourse.mybir as mybir
from concourse.bass_utils import run_bass_kernel_spmd
from contextlib import ExitStack

F32 = mybir.dt.float32
BF16 = mybir.dt.bfloat16
U16 = mybir.dt.uint16
I32 = mybir.dt.int32
AF = mybir.ActivationFunctionType
ALU = mybir.AluOpType
AX = mybir.AxisListType

C = 512
D = 512
DM = 2048
NS = 4
ND = 16
NE = 4
EPS_LN = 1e-5
EPS_RMS = 1.1920929e-7


class PB:
    ENGS = ("sync", "gpsimd", "scalar", "vector", "tensor")

    def __init__(self, nc, stack):
        self.nc = nc
        self.stack = stack
        self.q = {e: [] for e in self.ENGS}
        self.sems = {}
        self.ctr = {}
        self.fence = {e: [] for e in self.ENGS}
        for e in self.ENGS:
            self.sem("p_" + e)

    def sem(self, name):
        if name not in self.sems:
            self.sems[name] = self.stack.enter_context(self.nc.semaphore(name))
            self.ctr[name] = 0
        return self.sems[name]

    def step(self, eng, fn, waits=(), amt=1, semname=None, serial=None):
        sn = semname or ("p_" + eng)
        self.sem(sn)
        wl = {}
        if serial is None:
            serial = eng in ("vector", "scalar")
        if serial and self.ctr["p_" + eng] > 0:
            wl["p_" + eng] = self.ctr["p_" + eng]
        for w in list(waits) + self.fence[eng]:
            if w is None:
                continue
            s, v = w
            if v > wl.get(s, -1):
                wl[s] = v
        self.fence[eng] = []
        self.ctr[sn] += amt
        tok = (sn, self.ctr[sn])
        self.q[eng].append((sorted(wl.items()), fn, sn, amt))
        return tok

    def dma(self, eng, sem, out_ap, in_ap, waits=(), **kw):
        self.sem(sem)
        prev = self.ctr[sem]
        w = list(waits) + ([(sem, prev)] if prev > 0 else [])
        return self.step(eng, lambda e: e.dma_start(out_ap, in_ap, **kw),
                         waits=w, amt=16, semname=sem, serial=False)

    def barrier(self):
        toks = [self.step(e, lambda en: None,
                          waits=[("p_" + e, self.ctr["p_" + e])]) for e in self.ENGS]
        for e in self.ENGS:
            self.fence[e] = list(toks)

    def emit_all(self):
        nc = self.nc
        with nc.Block() as block:
            def mk(eng_name):
                def run(e):
                    for wl, fn, sn, amt in self.q[eng_name]:
                        for s, v in wl:
                            e.wait_ge(self.sems[s], v)
                        inst = fn(e)
                        if inst is not None:
                            inst.then_inc(self.sems[sn], amt)
                        else:
                            e.nop(nofuse=True).then_inc(self.sems[sn], amt)
                return run
            block.sync(mk("sync"))
            block.gpsimd(mk("gpsimd"))
            block.scalar(mk("scalar"))
            block.vector(mk("vector"))
            block.tensor(mk("tensor"))


def build_nc(n_own, n_oth, Hs):
    H2s = 2 * Hs
    nch = n_own // C
    noch = n_oth // C
    NHC = Hs // 128
    NTQ = n_own // C
    NGG = Hs // 256
    NPE = DM // 256

    nc = bass.Bass("TRN2", target_bir_lowering=False)
    din = lambda name, shape, dt=F32: nc.dram_tensor(name, shape, dt, kind="ExternalInput").ap()

    x = din("x", [n_own, DM]);        xt = din("xt", [DM, n_own])
    xo = din("xo", [n_oth, DM]);      xot = din("xot", [DM, n_oth])
    wq = din("wq", [DM, D]);          wk = din("wk", [3, DM, D])
    wv = din("wv", [3, DM, D]);       wo = din("wo", [D, DM])
    wg = din("wg", [DM, 8]);          cst = din("cst", [1, 16])
    wgun = din("wgun", [H2s, DM]);    wpn = din("wpn", [DM, Hs])
    expn = din("expn", [4, 128, C]);  tauv = din("tauv", [4, 128, 1])
    sigv = din("sigv", [4, 128, 1])
    id32 = din("id32", [128, 128]);   idbf = din("idbf", [128, 128], BF16)
    seg = din("seg", [128, 2], BF16); segtb = din("segtb", [2, 128], BF16)
    segtf = din("segtf", [2, 128]);   onesr = din("onesr", [1, 128])
    ones1f = din("ones1f", [128, 1]); ones1b = din("ones1b", [128, 1], BF16)

    out = nc.dram_tensor("out", [n_own, DM], F32, kind="ExternalOutput").ap()

    r1d = nc.dram_tensor("r1d", [n_own, DM], F32, kind="Internal").ap()
    hhi = nc.dram_tensor("hhi", [Hs, n_own], BF16, kind="Internal").ap()
    hlo = nc.dram_tensor("hlo", [Hs, n_own], BF16, kind="Internal").ap()
    wph = nc.dram_tensor("wph", [Hs, DM], U16, kind="Internal").ap()
    wgh_n = nc.dram_tensor("wgh_n", [H2s, DM], U16, kind="Internal").ap()
    s_w = {}
    for nm, shp in [("q", [DM, D]), ("k", [3, DM, D]), ("v", [3, DM, D]), ("o", [D, DM])]:
        s_w[nm + "h"] = nc.dram_tensor("s_w" + nm + "h", shp, BF16, kind="Internal").ap()
        s_w[nm + "l"] = nc.dram_tensor("s_w" + nm + "l", shp, BF16, kind="Internal").ap()

    st = ExitStack()
    p = PB(nc, st)
    sbt = lambda nm, sh, dt: st.enter_context(nc.sbuf_tensor(nm, sh, dt))

    PS = st.enter_context(nc.psum_tensor("PS", [128, 8, 512], F32))
    bfree = [None] * 8

    csts = sbt("csts", [1, 16], F32)
    srow = sbt("srow", [1, 16], F32)
    srob = sbt("srob", [1, 4], BF16)
    sbc = sbt("sbc", [128, 16], F32)
    ident = sbt("ident", [128, 128], F32)
    identb = sbt("identb", [128, 128], BF16)
    segs = sbt("segs", [128, 2], BF16)
    segtbs = sbt("segtbs", [2, 128], BF16)
    segtfs = sbt("segtfs", [2, 128], F32)
    onesrs = sbt("onesrs", [1, 128], F32)
    ones1fs = sbt("ones1fs", [128, 1], F32)
    ones1bs = sbt("ones1bs", [128, 1], BF16)
    invr = sbt("invr", [128, NTQ, 4], F32)
    ap_st = ExitStack()
    apt = lambda nm, sh, dt: ap_st.enter_context(nc.sbuf_tensor(nm, sh, dt))
    dgh = apt("dgh", [128, 3, 128], BF16)
    dgl = apt("dgl", [128, 3, 128], BF16)
    cvec = apt("cvec", [128, 4, 4], F32)
    evecs = apt("evecs", [128, 2, 4, 4], F32)
    msk = apt("msk", [128, 3, 4, 512], BF16)
    tauvs = apt("tauvs", [128, 4, 1], F32)
    sigvs = apt("sigvs", [128, 4, 1], F32)
    csr_q = apt("csr_q", [1, 512], BF16)
    csr_k = apt("csr_k", [1, 3, 512], BF16)
    csr_v = apt("csr_v", [1, 3, 512], BF16)
    csr_g = apt("csr_g", [1, 8], BF16)
    wgh = apt("wgh", [128, ND, 8], BF16)
    wgl = apt("wgl", [128, ND, 8], BF16)
    Sh = apt("Sh", [128, 3, 4, 512], BF16)
    Sl = apt("Sl", [128, 3, 4, 512], BF16)

    st0a = ExitStack()
    expns = st0a.enter_context(nc.sbuf_tensor("expns", [128, 4, 512], F32))
    t_cst = p.dma("sync", "d_cst", csts[:], cst[:, :])
    for dst_, src_ in [(ident, id32), (identb, idbf), (segs, seg), (segtbs, segtb),
                       (segtfs, segtf), (onesrs, onesr), (ones1fs, ones1f), (ones1bs, ones1b)]:
        p.dma("sync", "d_cst", dst_[:], src_[:, :])
    p.dma("sync", "d_cst", tauvs[:], tauv.rearrange("s p o -> p s o"), allow_slow_non_contiguous=True)
    p.dma("sync", "d_cst", sigvs[:], sigv.rearrange("s p o -> p s o"), allow_slow_non_contiguous=True)
    t_ldall = p.dma("sync", "d_cst", expns[:], expn.rearrange("s p t -> p s t"))

    t0m = p.step("vector", lambda e: e.memset(srow[:], 0.0))
    t1 = p.step("scalar", lambda e: e.activation(srow[:, 14:15], csts[:, 0:1], AF.Sigmoid), waits=[t_ldall, t0m])
    t1 = p.step("scalar", lambda e: e.activation(srow[:, 0:1], srow[:, 14:15], AF.Ln), waits=[t1])
    t1 = p.step("scalar", lambda e: e.activation(srow[:, 14:15], csts[:, 1:2], AF.Sigmoid), waits=[t1])
    t1 = p.step("scalar", lambda e: e.activation(srow[:, 1:2], srow[:, 14:15], AF.Ln), waits=[t1])
    t1 = p.step("scalar", lambda e: e.activation(srow[:, 14:15], csts[:, 2:3], AF.Sigmoid), waits=[t1])
    t1 = p.step("scalar", lambda e: e.activation(srow[:, 2:3], srow[:, 14:15], AF.Ln), waits=[t1])
    tds = []
    for i in range(3):
        tds.append(p.step("scalar", lambda e, i=i: e.activation(
            srow[:, 3 + i:4 + i], csts[:, 5:6], AF.Exp, scale=srow[:, i:i + 1]), waits=[t1]))
    t2 = p.step("vector", lambda e: e.tensor_copy(srob[:, 0:3], srow[:, 3:6]), waits=tds)
    t2 = p.step("vector", lambda e: e.tensor_copy(srow[:, 6:9], srob[:, 0:3]), waits=[t2])
    t2 = p.step("vector", lambda e: e.tensor_tensor(srow[:, 9:12], srow[:, 3:6], srow[:, 6:9],
                                                    ALU.subtract), waits=[t2])
    t2 = p.step("vector", lambda e: e.tensor_copy(srow[:, 12:14], csts[:, 3:5]), waits=[t2])
    t3 = p.step("tensor", lambda e: e.matmul(PS[:, 7, 0:16], onesrs[:], srow[:, 0:16],
                                             start=True, stop=True), waits=[t2], serial=False)
    t_sbc = p.step("vector", lambda e: e.tensor_copy(sbc[:], PS[:, 7, 0:16]), waits=[t3])
    bfree[7] = t_sbc
    for i in range(3):
        p.step("vector", lambda e, i=i: e.tensor_scalar_mul(dgh[:, i, :], identb[:],
                                                            sbc[:, 6 + i:7 + i]), waits=[t_sbc])
        p.step("vector", lambda e, i=i: e.tensor_scalar_mul(dgl[:, i, :], identb[:],
                                                            sbc[:, 9 + i:10 + i]), waits=[t_sbc])
    for i in range(3):
        for s in range(4):
            p.step("scalar", lambda e, i=i, s=s: e.activation(
                cvec[:, s, i:i + 1], tauvs[:, s, :], AF.Exp, scale=sbc[:, i:i + 1]), waits=[t_sbc])
            te = p.step("scalar", lambda e, i=i, s=s: e.activation(
                evecs[:, 0, s, i:i + 1], sigvs[:, s, :], AF.Exp, scale=sbc[:, i:i + 1]), waits=[t_sbc])
            p.step("vector", lambda e, i=i, s=s: e.tensor_scalar_mul(
                evecs[:, 1, s, i:i + 1], evecs[:, 0, s, i:i + 1], sbc[:, 13:14]), waits=[te])
    for i in range(3):
        for s in range(4):
            p.step("scalar", lambda e, i=i, s=s: e.activation(
                msk[:, i, s, :], expns[:, s, :], AF.Exp, scale=sbc[:, i:i + 1]), waits=[t_sbc])

    p.barrier()
    st0a.close()

    with ExitStack() as st0:
        wst = st0.enter_context(nc.sbuf_tensor("wst", [128, 2, 512], F32))
        whst = st0.enter_context(nc.sbuf_tensor("whst", [128, 2, 512], BF16))
        wlst = st0.enter_context(nc.sbuf_tensor("wlst", [128, 2, 512], BF16))
        wcons = [None, None]

        def prep_weight(src_ap, dsth, dstl, ntile, csum_to, tile_sl):
            tcs = None
            first = [True]
            for t in range(ntile):
                par = t % 2
                tl = p.dma("sync", "d_w%d" % par, wst[:, par, :], tile_sl(src_ap, t),
                           waits=wcons[par] or [])
                th = p.step("vector", lambda e, par=par: e.tensor_copy(whst[:, par, :], wst[:, par, :]),
                            waits=[tl])
                tlo = p.step("vector", lambda e, par=par: e.tensor_tensor(
                    wlst[:, par, :], wst[:, par, :], whst[:, par, :], ALU.subtract), waits=[th])
                if csum_to is not None:
                    def csmm(e, par=par, t=t, f=first[0]):
                        e.matmul(PS[0:1, 6, :], ones1bs[:], whst[:, par, :], start=f, stop=False)
                        return e.matmul(PS[0:1, 6, :], ones1bs[:], wlst[:, par, :],
                                        start=False, stop=(t == ntile - 1))
                    tcs = p.step("tensor", csmm, waits=[tlo, bfree[6] if first[0] else None],
                                 serial=False)
                    first[0] = False
                d1 = p.dma("gpsimd", "st_wei", tile_sl(dsth, t), whst[:, par, :], waits=[tlo])
                d2 = p.dma("gpsimd", "st_wei", tile_sl(dstl, t), wlst[:, par, :], waits=[tlo])
                wcons[par] = [d2, tcs] if csum_to is not None else [d2]
            if csum_to is not None:
                tv = p.step("vector", lambda e: e.tensor_copy(csum_to, PS[0:1, 6, :]), waits=[tcs])
                bfree[6] = tv

        rowtile = lambda ap_, t: ap_[t * 128:(t + 1) * 128, :]
        wo_tile = lambda ap_, t: ap_[(t // 4) * 128:(t // 4 + 1) * 128, (t % 4) * 512:(t % 4 + 1) * 512]
        prep_weight(wq, s_w["qh"], s_w["ql"], ND, csr_q[:, :], rowtile)
        for i in range(3):
            prep_weight(wk[i], s_w["kh"][i], s_w["kl"][i], ND, csr_k[:, i, :], rowtile)
            prep_weight(wv[i], s_w["vh"][i], s_w["vl"][i], ND, csr_v[:, i, :], rowtile)
        prep_weight(wo, s_w["oh"], s_w["ol"], 16, None, wo_tile)
        tcs = None
        for t in range(ND):
            par = t % 2
            tl = p.dma("sync", "d_w%d" % par, wst[:, par, 0:8], wg[t * 128:(t + 1) * 128, :],
                       waits=wcons[par] or [])
            th = p.step("vector", lambda e, par=par, t=t: e.tensor_copy(wgh[:, t, :], wst[:, par, 0:8]),
                        waits=[tl])
            tlo = p.step("vector", lambda e, par=par, t=t: e.tensor_tensor(
                wgl[:, t, :], wst[:, par, 0:8], wgh[:, t, :], ALU.subtract), waits=[th])
            def gcs(e, t=t):
                e.matmul(PS[0:1, 6, 0:8], ones1bs[:], wgh[:, t, :], start=(t == 0), stop=False)
                return e.matmul(PS[0:1, 6, 0:8], ones1bs[:], wgl[:, t, :],
                                start=False, stop=(t == ND - 1))
            tcs = p.step("tensor", gcs, waits=[tlo, bfree[6] if t == 0 else None], serial=False)
            wcons[par] = [tcs]
        tv = p.step("vector", lambda e: e.tensor_copy(csr_g[:], PS[0:1, 6, 0:8]), waits=[tcs])
        bfree[6] = tv
        n_wei_stores = p.ctr["st_wei"]

    p.barrier()

    st1 = ExitStack()
    sb1 = lambda nm, sh, dt: st1.enter_context(nc.sbuf_tensor(nm, sh, dt))
    xsb = sb1("xsb", [128, NS, DM], F32)
    xtst = sb1("xtst", [128, 2, 512], F32)
    xth = sb1("xth", [128, ND, 512], BF16)
    xtl = sb1("xtl", [128, ND, 512], BF16)
    sqs = sb1("sqs", [128, DM], F32)
    stf = sb1("stf", [128, NS, 8], F32)
    stb = sb1("stb", [128, NS, 1], BF16)
    nmrow = sb1("nmrow", [1, 512], BF16)
    einv2 = sb1("einv2", [128, NS, 4], F32)
    qh = sb1("qh", [128, NE, 512], BF16);  ql = sb1("ql", [128, NE, 512], BF16)
    kh = sb1("kh", [128, NE, 512], BF16);  kl = sb1("kl", [128, NE, 512], BF16)
    kth = sb1("kth", [128, NS, 512], BF16); ktl = sb1("ktl", [128, NS, 512], BF16)
    vth = sb1("vth", [128, NS, 512], BF16); vtl = sb1("vtl", [128, NS, 512], BF16)
    vsh = sb1("vsh", [128, NS, 512], BF16); vsl = sb1("vsl", [128, NS, 512], BF16)
    sch = sb1("sch", [128, NS, 512], BF16); scl_ = sb1("scl_", [128, NS, 512], BF16)
    scm32 = sb1("scm32", [128, 512], F32)
    rc32 = sb1("rc32", [128, 512], F32)
    ra32 = sb1("ra32", [128, 512], F32)
    vt32 = sb1("vt32", [128, 512], F32)
    blsb = sb1("blsb", [128, NS, 512], F32)
    blth = sb1("blth", [128, NE, 512], BF16); bltl = sb1("bltl", [128, NE, 512], BF16)
    gl3 = sb1("gl3", [128, NS, 8], F32)
    giv = sb1("giv", [128, NS, 4], F32)
    wsl_h = sb1("wsl_h", [128, 4, 512], BF16)
    wsl_l = sb1("wsl_l", [128, 4, 512], BF16)

    tok = {
        "xsb_free": [None] * NS,
        "xcast": [None] * ND,
        "xth_cons": None,
        "S_ready": [None] * 3,
        "S_read": [None] * 3,
        "wslab": [("p_tensor", 0)] * 4,
    }
    wei_tok = ("st_wei", n_wei_stores)

    def load_wslab(src_h, src_l, r0, c0, c1, idx):
        par = idx % 4
        p.dma("sync", "d_wsl%d" % par, wsl_h[:, par, 0:c1 - c0], src_h[r0:r0 + 128, c0:c1],
              waits=[tok["wslab"][par], wei_tok])
        t2 = p.dma("sync", "d_wsl%d" % par, wsl_l[:, par, 0:c1 - c0], src_l[r0:r0 + 128, c0:c1])
        return t2, par

    def proj_dloop(src_h, src_l, fm, t_xt, wcnt, b0=0):
        groups = [None] * 4
        for d in range(ND):
            tw, par = load_wslab(src_h, src_l, d * 128, 0, 512, wcnt[0])
            wcnt[0] += 1
            for g in range(4):
                first = (d == 0)
                def mm(e, d=d, g=g, par=par, first=first, b0=b0):
                    o = PS[:, b0 + g, :]
                    if fm:
                        w_h = wsl_h[:, par, g * 128:(g + 1) * 128]
                        w_l = wsl_l[:, par, g * 128:(g + 1) * 128]
                        e.matmul(o, w_h, xth[:, d, :], start=first, stop=False)
                        e.matmul(o, w_l, xth[:, d, :], start=False, stop=False)
                        return e.matmul(o, w_h, xtl[:, d, :], start=False, stop=False)
                    else:
                        xh_t = xth[:, d, g * 128:(g + 1) * 128]
                        xl_t = xtl[:, d, g * 128:(g + 1) * 128]
                        e.matmul(o, xh_t, wsl_h[:, par, :], start=first, stop=False)
                        e.matmul(o, xl_t, wsl_h[:, par, :], start=False, stop=False)
                        return e.matmul(o, xh_t, wsl_l[:, par, :], start=False, stop=False)
                groups[g] = p.step("tensor", mm,
                                   waits=[tw, tok["xcast"][d],
                                          bfree[b0 + g] if d == 0 else None], serial=False)
            tok["wslab"][par] = groups[3]
        return groups

    def proj_finish(groups, dst_h, dst_l, csr_ap, fm, t_nm, t_stats, extra=None, b0=0):
        for g in range(4):
            def r1mm(e, g=g, b0=b0):
                o = PS[:, b0 + g, :]
                if fm:
                    return e.matmul(o, csr_ap[:, g * 128:(g + 1) * 128], nmrow[:],
                                    start=False, stop=True)
                else:
                    return e.matmul(o, nmrow[:, g * 128:(g + 1) * 128], csr_ap,
                                    start=False, stop=True)
            groups[g] = p.step("tensor", r1mm, waits=[groups[g], t_nm], serial=False)
        ev = [None] * 4
        for g in range(4):
            pg = PS[:, b0 + g, :]
            if extra is None:
                th_ = p.step("scalar", lambda e, g=g, pg=pg: e.activation(dst_h[:, g, :], pg,
                                                                          AF.Copy), waits=[groups[g]])
                tl_ = p.step("vector", lambda e, g=g, pg=pg: e.tensor_tensor(
                    dst_l[:, g, :], pg, dst_h[:, g, :], ALU.subtract), waits=[th_])
            else:
                sc1, (edh, edl), sc2 = extra
                t0_ = p.step("vector", lambda e, g=g, pg=pg: e.tensor_scalar_mul(
                    vt32[:], pg, sc1(g)), waits=[groups[g], t_stats])
                th_ = p.step("scalar", lambda e, g=g: e.activation(dst_h[:, g, :], vt32[:],
                                                                   AF.Copy), waits=[t0_])
                tl_ = p.step("vector", lambda e, g=g: e.tensor_tensor(
                    dst_l[:, g, :], vt32[:], dst_h[:, g, :], ALU.subtract), waits=[th_])
                t0b = p.step("vector", lambda e, g=g, pg=pg: e.tensor_scalar_mul(
                    rc32[:], pg, sc2(g)), waits=[groups[g], t_stats])
                th2 = p.step("scalar", lambda e, g=g: e.activation(edh[:, g, :], rc32[:],
                                                                   AF.Copy), waits=[t0b])
                tl_ = p.step("vector", lambda e, g=g: e.tensor_tensor(
                    edl[:, g, :], rc32[:], edh[:, g, :], ALU.subtract), waits=[th2, tl_])
            ev[g] = tl_
            bfree[b0 + g] = tl_
        return ev

    def proj_pass(dst_h, dst_l, src_h, src_l, csr_ap, fm, t_nm, t_xt, t_stats,
                  wcnt, extra=None, b0=0):
        groups = proj_dloop(src_h, src_l, fm, t_xt, wcnt, b0=b0)
        return proj_finish(groups, dst_h, dst_l, csr_ap, fm, t_nm, t_stats, extra=extra, b0=b0)

    wcnt = [0]

    def do_chunk(xsrc, xtsrc, base, is_comp, cx):
        t_xs = [None] * NS
        for ts in range(NS):
            t_xs[ts] = p.dma("sync", "d_xsb", xsb[:, ts, :],
                             xsrc[base + ts * 128:base + (ts + 1) * 128, :],
                             waits=[tok["xsb_free"][ts]])
        t_x = t_xs[NS - 1]
        for d in range(ND):
            par = d % 2
            tl = p.dma("sync", "d_xt%d" % par, xtst[:, par, :], xtsrc[d * 128:(d + 1) * 128, base:base + C],
                       waits=[tok["xcast"][d - 2] if d >= 2 else tok["xth_cons"]])
            th = p.step("vector", lambda e, d=d, par=par: e.tensor_copy(xth[:, d, :], xtst[:, par, :]),
                        waits=[tl, tok["xth_cons"] if d < 2 else None])
            tok["xcast"][d] = p.step("vector", lambda e, d=d, par=par: e.tensor_tensor(
                xtl[:, d, :], xtst[:, par, :], xth[:, d, :], ALU.subtract), waits=[th])
        t_xt = tok["xcast"][ND - 1]
        if not is_comp:
            q_groups = proj_dloop(s_w["qh"], s_w["ql"], True, t_xt, wcnt, b0=0)
            k0_groups = proj_dloop(s_w["kh"][0], s_w["kl"][0], True, t_xt, wcnt, b0=4)
        else:
            k0_groups = proj_dloop(s_w["kh"][0], s_w["kl"][0], True, t_xt, wcnt, b0=4)
            v0_groups = proj_dloop(s_w["vh"][0], s_w["vl"][0], False, t_xt, wcnt, b0=0)
        t_stats = None
        t_nm = None
        for s in range(NS):
            tq_ = p.step("scalar", lambda e, s=s: e.activation(sqs[:], xsb[:, s, :], AF.Square),
                         waits=[t_xs[s], t_stats])
            ta = p.step("vector", lambda e, s=s: e.reduce_sum(
                stf[:, s, 0:1], xsb[:, s, :], AX.X, negate=True), waits=[t_xs[s]])
            tb = p.step("vector", lambda e, s=s: e.reduce_sum(stf[:, s, 6:7], sqs[:], AX.X),
                        waits=[tq_])
            tc = p.step("vector", lambda e, s=s: e.tensor_scalar_mul(
                stf[:, s, 0:1], stf[:, s, 0:1], 1.0 / DM), waits=[ta])
            td = p.step("vector", lambda e, s=s: e.tensor_scalar(
                stf[:, s, 1:2], stf[:, s, 6:7], 1.0 / DM, EPS_LN, ALU.mult, ALU.add), waits=[tb])
            te = p.step("vector", lambda e, s=s: e.tensor_scalar_mul(
                stf[:, s, 2:3], stf[:, s, 0:1], stf[:, s, 0:1]), waits=[tc])
            tf = p.step("vector", lambda e, s=s: e.scalar_tensor_tensor(
                stf[:, s, 3:4], stf[:, s, 2:3], -1.0, stf[:, s, 1:2], ALU.mult, ALU.add),
                waits=[td, te])
            tg = p.step("vector", lambda e, s=s: e.reciprocal(stf[:, s, 4:5], stf[:, s, 3:4]),
                        waits=[tf])
            tn = p.step("scalar", lambda e, s=s: e.activation(
                stf[:, s, 4:5], stf[:, s, 4:5], AF.Sqrt), waits=[tg])
            for _ in range(2):
                n1 = p.step("scalar", lambda e, s=s: e.activation(
                    stf[:, s, 6:7], stf[:, s, 4:5], AF.Square), waits=[tn])
                n2 = p.step("vector", lambda e, s=s: e.tensor_tensor(
                    stf[:, s, 6:7], stf[:, s, 6:7], stf[:, s, 3:4], ALU.mult), waits=[n1])
                n3 = p.step("vector", lambda e, s=s: e.tensor_scalar(
                    stf[:, s, 6:7], stf[:, s, 6:7], -0.5, 1.5, ALU.mult, ALU.add), waits=[n2])
                tn = p.step("vector", lambda e, s=s: e.tensor_tensor(
                    stf[:, s, 4:5], stf[:, s, 4:5], stf[:, s, 6:7], ALU.mult), waits=[n3])
            ti = p.step("vector", lambda e, s=s: e.tensor_scalar_mul(
                stf[:, s, 5:6], stf[:, s, 4:5], stf[:, s, 4:5]), waits=[tn])
            for i in range(3):
                ti = p.step("vector", lambda e, s=s, i=i, ic=1 if is_comp else 0: e.tensor_tensor(
                    einv2[:, s, i:i + 1], evecs[:, ic, s, i:i + 1], stf[:, s, 5:6], ALU.mult),
                    waits=[ti])
            tj = p.step("vector", lambda e, s=s: e.tensor_copy(
                stb[:, s, 0:1], stf[:, s, 0:1]), waits=[tc])
            t_nm = p.dma("sync", "d_nm", nmrow[0:1, s * 128:(s + 1) * 128],
                         stb[:, s, 0:1], waits=[tj], allow_slow_non_contiguous=True)
            t_stats = ti
        t_qev = None
        if not is_comp:
            t_qev = proj_finish(q_groups, qh, ql, csr_q, True, t_nm, t_stats, b0=0)
            t_k0ev = proj_finish(k0_groups, kh, kl, csr_k[:, 0, :], True, t_nm, t_stats, b0=4)
        else:
            t_k0ev = proj_finish(k0_groups, kh, kl, csr_k[:, 0, :], True, t_nm, t_stats, b0=4)
        t_gate = None
        if not is_comp:
            for s in range(NS):
                def gmm(e, s=s):
                    o = PS[:, 7, 0:8]
                    for d in range(ND):
                        e.matmul(o, xth[:, d, s * 128:(s + 1) * 128], wgh[:, d, :],
                                 start=(d == 0), stop=False)
                        e.matmul(o, xtl[:, d, s * 128:(s + 1) * 128], wgh[:, d, :],
                                 start=False, stop=False)
                        e.matmul(o, xth[:, d, s * 128:(s + 1) * 128], wgl[:, d, :],
                                 start=False, stop=False)
                    return e.matmul(o, nmrow[:, s * 128:(s + 1) * 128], csr_g[:],
                                    start=False, stop=True)
                tg1 = p.step("tensor", gmm, waits=[t_xt, t_nm, bfree[7]], serial=False)
                tg2 = p.step("vector", lambda e, s=s: e.tensor_scalar_mul(
                    gl3[:, s, 0:3], PS[:, 7, 0:3], stf[:, s, 4:5]), waits=[tg1, t_stats])
                bfree[7] = tg2
                tg3 = p.step("vector", lambda e, s=s: e.reduce_max(
                    gl3[:, s, 4:5], gl3[:, s, 0:3], AX.X, negate=True), waits=[tg2])
                tg4 = p.step("scalar", lambda e, s=s: e.activation(
                    gl3[:, s, 0:3], gl3[:, s, 0:3], AF.Exp, bias=gl3[:, s, 4:5]), waits=[tg3])
                tg5 = p.step("vector", lambda e, s=s: e.reduce_sum(
                    gl3[:, s, 4:5], gl3[:, s, 0:3], AX.X), waits=[tg4])
                tg6 = p.step("vector", lambda e, s=s: e.reciprocal(
                    gl3[:, s, 5:6], gl3[:, s, 4:5]), waits=[tg5])
                tg7 = p.step("vector", lambda e, s=s: e.tensor_scalar_mul(
                    gl3[:, s, 0:3], gl3[:, s, 0:3], gl3[:, s, 5:6]), waits=[tg6])
                t_gate = p.step("vector", lambda e, s=s: e.tensor_scalar_mul(
                    giv[:, s, 0:3], gl3[:, s, 0:3], stf[:, s, 4:5]), waits=[tg7])
        for i in range(3):
            if i == 0:
                t_kev = t_k0ev
            else:
                t_kev = proj_pass(kh, kl, s_w["kh"][i], s_w["kl"][i], csr_k[:, i, :], True,
                                  t_nm, t_xt, t_stats, wcnt)
            t_ktev = [None] * NS
            tb4 = [bfree[4], bfree[5], bfree[6], bfree[7]]
            for hl, (src, dst) in enumerate(((kh, kth), (kl, ktl))):
                for ec in range(NE):
                    tts = []
                    for ss in range(4):
                        bank = 4 + ss
                        tts.append(p.step("tensor", lambda e, ec=ec, ss=ss, bank=bank, src=src:
                                          e.matmul(PS[:, bank, 0:128],
                                                   src[:, ec, ss * 128:(ss + 1) * 128], identb[:],
                                                   start=True, stop=True),
                                          waits=[t_kev[ec], tb4[ss]], serial=False))
                    tev = p.step("scalar", lambda e, ec=ec, dst=dst: e.activation(
                        dst[:, 0:4, ec * 128:(ec + 1) * 128].rearrange("p s j -> p (s j)")
                        if False else dst[:, 0:4, ec * 128:(ec + 1) * 128],
                        PS[:, 4:8, 0:128], AF.Copy), waits=tts)
                    for ss in range(4):
                        tb4[ss] = tev
                        t_ktev[ss] = tev
            bfree[4], bfree[5], bfree[6], bfree[7] = tb4
            t_scev = [None] * NS
            if not is_comp:
                for ss in range(NS):
                    bank = 6 + (ss % 2)
                    def smm(e, ss=ss, bank=bank):
                        o = PS[:, bank, :]
                        last = None
                        for ec in range(NE):
                            last = e.matmul(o, kh[:, ec, ss * 128:(ss + 1) * 128], qh[:, ec, :],
                                            start=(ec == 0), stop=False)
                            last = e.matmul(o, kl[:, ec, ss * 128:(ss + 1) * 128], qh[:, ec, :],
                                            start=False, stop=False)
                            last = e.matmul(o, kh[:, ec, ss * 128:(ss + 1) * 128], ql[:, ec, :],
                                            start=False, stop=(ec == NE - 1))
                        return last
                    tsc = p.step("tensor", smm, waits=t_kev + t_qev + [bfree[bank]], serial=False)
                    e1 = p.step("vector", lambda e, ss=ss, bank=bank, i=i: e.scalar_tensor_tensor(
                        scm32[:], PS[:, bank, :], stf[:, ss, 4:5], msk[:, i, ss, :],
                        ALU.mult, ALU.mult), waits=[tsc, t_stats])
                    bfree[bank] = e1
                    e2 = p.step("scalar", lambda e, ss=ss: e.activation(sch[:, ss, :], scm32[:],
                                                                        AF.Copy), waits=[e1])
                    t_scev[ss] = p.step("vector", lambda e, ss=ss: e.tensor_tensor(
                        scl_[:, ss, :], scm32[:], sch[:, ss, :], ALU.subtract), waits=[e2])
            if is_comp and i == 0:
                t_vev = proj_finish(v0_groups, vth, vtl, csr_v[:, 0, :], False, t_nm, t_stats,
                                    extra=(lambda g: stf[:, g, 4:5], (vsh, vsl),
                                           lambda g, i=i: einv2[:, g, i:i + 1]), b0=0)
            else:
                t_vev = proj_pass(vth, vtl, s_w["vh"][i], s_w["vl"][i], csr_v[:, i, :], False,
                                  t_nm, t_xt, t_stats, wcnt,
                                  extra=(lambda g: stf[:, g, 4:5], (vsh, vsl),
                                         lambda g, i=i: einv2[:, g, i:i + 1]))
            t_wc = []
            if not is_comp:
                for ts in range(NS):
                    bw = 6 if ts % 2 == 0 else 7
                    bc = 4 if ts % 2 == 0 else 5
                    def wmm(e, ts=ts, bw=bw):
                        o = PS[:, bw, :]
                        last = None
                        for ss in range(NS):
                            last = e.matmul(o, sch[:, ss, ts * 128:(ts + 1) * 128], vth[:, ss, :],
                                            start=(ss == 0), stop=False)
                            last = e.matmul(o, scl_[:, ss, ts * 128:(ts + 1) * 128], vth[:, ss, :],
                                            start=False, stop=False)
                            last = e.matmul(o, sch[:, ss, ts * 128:(ts + 1) * 128], vtl[:, ss, :],
                                            start=False, stop=(ss == NS - 1))
                        return last
                    tw_ = p.step("tensor", wmm, waits=t_scev + t_vev + [bfree[bw]], serial=False)
                    def cmm(e, ts=ts, bc=bc, i=i):
                        o = PS[:, bc, :]
                        last = None
                        for ec in range(NE):
                            last = e.matmul(o, qh[:, ec, ts * 128:(ts + 1) * 128], Sh[:, i, ec, :],
                                            start=(ec == 0), stop=False)
                            last = e.matmul(o, ql[:, ec, ts * 128:(ts + 1) * 128], Sh[:, i, ec, :],
                                            start=False, stop=False)
                            last = e.matmul(o, qh[:, ec, ts * 128:(ts + 1) * 128], Sl[:, i, ec, :],
                                            start=False, stop=(ec == NE - 1))
                        return last
                    tc_ = p.step("tensor", cmm,
                                 waits=t_qev + [tok["S_ready"][i], bfree[bc]], serial=False)
                    e1 = p.step("vector", lambda e, ts=ts, bc=bc, i=i: e.tensor_scalar_mul(
                        rc32[:], PS[:, bc, :], cvec[:, ts, i:i + 1]), waits=[tc_])
                    bfree[bc] = e1
                    e2 = p.step("vector", lambda e, bw=bw: e.tensor_tensor(
                        ra32[:], PS[:, bw, :], rc32[:], ALU.add), waits=[e1, tw_])
                    bfree[bw] = e2
                    if i == 0:
                        e3 = p.step("vector", lambda e, ts=ts, i=i: e.tensor_scalar_mul(
                            blsb[:, ts, :], ra32[:], giv[:, ts, i:i + 1]), waits=[e2, t_gate])
                    else:
                        e2b = p.step("vector", lambda e, ts=ts, i=i: e.tensor_scalar_mul(
                            ra32[:], ra32[:], giv[:, ts, i:i + 1]), waits=[e2, t_gate])
                        e3 = p.step("vector", lambda e, ts=ts: e.tensor_tensor(
                            blsb[:, ts, :], blsb[:, ts, :], ra32[:], ALU.add), waits=[e2b])
                    t_wc.append(e3)
                    tok["S_read"][i] = tc_
            t_sup = []
            for ek in range(NE):
                def supmm(e, ek=ek, i=i, cx=cx):
                    o = PS[:, 4 + ek, :]
                    last = None
                    fs = True
                    if cx != 0:
                        last = e.matmul(o, dgh[:, i, :], Sh[:, i, ek, :], start=True, stop=False)
                        last = e.matmul(o, dgh[:, i, :], Sl[:, i, ek, :], start=False, stop=False)
                        last = e.matmul(o, dgl[:, i, :], Sh[:, i, ek, :], start=False, stop=False)
                        fs = False
                    for ss in range(NS):
                        last = e.matmul(o, kth[:, ss, ek * 128:(ek + 1) * 128], vsh[:, ss, :],
                                        start=fs, stop=False)
                        fs = False
                        last = e.matmul(o, ktl[:, ss, ek * 128:(ek + 1) * 128], vsh[:, ss, :],
                                        start=False, stop=False)
                        last = e.matmul(o, kth[:, ss, ek * 128:(ek + 1) * 128], vsl[:, ss, :],
                                        start=False, stop=(ss == NS - 1))
                    return last
                t_sup.append(p.step("tensor", supmm,
                                    waits=t_ktev + t_vev + [tok["S_ready"][i], bfree[4 + ek]],
                                    serial=False))
            t_sev = None
            for ek in range(NE):
                e1 = p.step("scalar", lambda e, ek=ek, i=i: e.activation(
                    Sh[:, i, ek, :], PS[:, 4 + ek, :], AF.Copy), waits=[t_sup[ek], tok["S_read"][i]])
                t_sev = p.step("vector", lambda e, ek=ek, i=i: e.tensor_tensor(
                    Sl[:, i, ek, :], PS[:, 4 + ek, :], Sh[:, i, ek, :], ALU.subtract), waits=[e1])
                bfree[4 + ek] = t_sev
            tok["S_ready"][i] = t_sev
        if not is_comp:
            t_blev = [None] * NE
            tpar = [bfree[6], bfree[7]]
            for idx in range(16):
                ts, ec = idx // 4, idx % 4
                bank = 6 + (idx % 2)
                tt = p.step("tensor", lambda e, ts=ts, ec=ec, bank=bank: e.transpose(
                    PS[:, bank, 0:128], blsb[:, ts, ec * 128:(ec + 1) * 128], ident[:]),
                    waits=[t_wc[ts], tpar[idx % 2]], serial=False)
                th_ = p.step("vector", lambda e, ts=ts, ec=ec, bank=bank: e.tensor_copy(
                    blth[:, ec, ts * 128:(ts + 1) * 128], PS[:, bank, 0:128]), waits=[tt])
                tv = p.step("vector", lambda e, ts=ts, ec=ec, bank=bank: e.tensor_tensor(
                    bltl[:, ec, ts * 128:(ts + 1) * 128], PS[:, bank, 0:128],
                    blth[:, ec, ts * 128:(ts + 1) * 128], ALU.subtract), waits=[th_])
                tpar[idx % 2] = tv
                t_blev[ec] = tv
            bfree[6], bfree[7] = tpar
            t_r1 = None
            for dq in range(4):
                gmm = [None] * NS
                for ec in range(NE):
                    tw, par = load_wslab(s_w["oh"], s_w["ol"], ec * 128, dq * 512, (dq + 1) * 512,
                                         wcnt[0])
                    wcnt[0] += 1
                    for ts in range(NS):
                        def omm(e, ts=ts, ec=ec, par=par):
                            o = PS[:, ts, :]
                            e.matmul(o, blth[:, ec, ts * 128:(ts + 1) * 128], wsl_h[:, par, :],
                                     start=(ec == 0), stop=False)
                            e.matmul(o, bltl[:, ec, ts * 128:(ts + 1) * 128], wsl_h[:, par, :],
                                     start=False, stop=False)
                            return e.matmul(o, blth[:, ec, ts * 128:(ts + 1) * 128],
                                            wsl_l[:, par, :], start=False, stop=(ec == NE - 1))
                        gmm[ts] = p.step("tensor", omm,
                                         waits=[tw, t_blev[ec]] +
                                               ([bfree[ts]] if ec == 0 else []), serial=False)
                    tok["wslab"][par] = gmm[NS - 1]
                for ts in range(NS):
                    e2 = p.step("vector", lambda e, ts=ts, dq=dq: e.scalar_tensor_tensor(
                        xsb[:, ts, dq * 512:(dq + 1) * 512], PS[:, ts, :], sbc[:, 12:13],
                        xsb[:, ts, dq * 512:(dq + 1) * 512], ALU.mult, ALU.add),
                        waits=[gmm[ts], t_xs[ts]])
                    bfree[ts] = e2
                    if dq == 3:
                        tok["xsb_free"][ts] = p.dma(
                            "gpsimd", "st_r1", r1d[base + ts * 128:base + (ts + 1) * 128, :],
                            xsb[:, ts, :], waits=[e2])
        else:
            for ts in range(NS):
                tok["xsb_free"][ts] = t_stats
        tok["xcast"] = [None] * ND
        tok["xth_cons"] = ("p_tensor", p.ctr["p_tensor"])

    cx = 0
    for j in range(noch - 1, -1, -1):
        do_chunk(xo, xot, j * C, True, cx)
        cx += 1
    for j in range(nch - 1, -1, -1):
        do_chunk(x, xt, j * C, False, cx)
        cx += 1
    n_r1_stores = p.ctr["st_r1"]
    st1.close()
    ap_st.close()

    p.barrier()

    ffn_st = ExitStack()
    sbf = lambda nm, sh, dt: ffn_st.enter_context(nc.sbuf_tensor(nm, sh, dt))
    bth = sbf("bth", [128, ND, n_own], BF16)
    btl = sbf("btl", [128, ND, n_own], BF16)
    ssqP = sbf("ssqP", [128, n_own], F32)
    t_ssq0 = p.step("vector", lambda e: e.memset(ssqP[:], 0.0))

    st2 = ExitStack()
    sb2 = lambda nm, sh, dt: st2.enter_context(nc.sbuf_tensor(nm, sh, dt))
    r1sb = sb2("r1sb", [128, 2, DM], F32)
    sq2 = sb2("sq2", [128, DM], F32)
    b32 = sq2
    st2f = sb2("st2f", [128, NS, 8], F32)
    t_r1free = [None, None]
    t_blast = None
    r1tok = ("st_r1", n_r1_stores)
    for gsub in range(NTQ * NS):
        tc2, s = gsub // NS, gsub % NS
        pr2 = gsub % 2
        t_x = p.dma("sync", "d_r1%d" % pr2, r1sb[:, pr2, :],
                    r1d[gsub * 128:(gsub + 1) * 128, :],
                    waits=[t_r1free[pr2], r1tok])
        if True:
            tq_ = p.step("scalar", lambda e, pr2=pr2: e.activation(sq2[:], r1sb[:, pr2, :], AF.Square),
                         waits=[t_x, t_blast])
            ta = p.step("vector", lambda e, s=s, pr2=pr2: e.reduce_sum(st2f[:, s, 0:1], r1sb[:, pr2, :], AX.X),
                        waits=[t_x])
            tb = p.step("vector", lambda e, s=s: e.reduce_sum(st2f[:, s, 6:7], sq2[:], AX.X),
                        waits=[tq_])
            tc1 = p.step("vector", lambda e, s=s: e.tensor_scalar_mul(
                st2f[:, s, 0:1], st2f[:, s, 0:1], 1.0 / DM), waits=[ta])
            td = p.step("vector", lambda e, s=s: e.tensor_scalar(
                st2f[:, s, 1:2], st2f[:, s, 6:7], 1.0 / DM, EPS_LN, ALU.mult, ALU.add), waits=[tb])
            te = p.step("vector", lambda e, s=s: e.tensor_scalar_mul(
                st2f[:, s, 2:3], st2f[:, s, 0:1], st2f[:, s, 0:1]), waits=[tc1])
            tf = p.step("vector", lambda e, s=s: e.scalar_tensor_tensor(
                st2f[:, s, 3:4], st2f[:, s, 2:3], -1.0, st2f[:, s, 1:2], ALU.mult, ALU.add),
                waits=[td, te])
            tg = p.step("vector", lambda e, s=s: e.reciprocal(st2f[:, s, 4:5], st2f[:, s, 3:4]),
                        waits=[tf])
            tn = p.step("scalar", lambda e, s=s: e.activation(
                st2f[:, s, 4:5], st2f[:, s, 4:5], AF.Sqrt), waits=[tg])
            for _ in range(2):
                n1 = p.step("scalar", lambda e, s=s: e.activation(
                    st2f[:, s, 6:7], st2f[:, s, 4:5], AF.Square), waits=[tn])
                n2 = p.step("vector", lambda e, s=s: e.tensor_tensor(
                    st2f[:, s, 6:7], st2f[:, s, 6:7], st2f[:, s, 3:4], ALU.mult), waits=[n1])
                n3 = p.step("vector", lambda e, s=s: e.tensor_scalar(
                    st2f[:, s, 6:7], st2f[:, s, 6:7], -0.5, 1.5, ALU.mult, ALU.add), waits=[n2])
                tn = p.step("vector", lambda e, s=s: e.tensor_tensor(
                    st2f[:, s, 4:5], st2f[:, s, 4:5], st2f[:, s, 6:7], ALU.mult), waits=[n3])
            tbb = p.step("vector", lambda e, s=s, pr2=pr2: e.tensor_scalar(
                b32[:], r1sb[:, pr2, :], st2f[:, s, 0:1], st2f[:, s, 4:5],
                ALU.subtract, ALU.mult), waits=[tn, t_x, tb])
            tlast = None
            col = gsub * 128
            for d0 in range(0, ND, 4):
                tts = []
                for si in range(4):
                    dch = d0 + si
                    bank = 4 + si
                    tts.append(p.step("tensor", lambda e, dch=dch, bank=bank: e.transpose(
                        PS[:, bank, 0:128], b32[:, dch * 128:(dch + 1) * 128], ident[:]),
                        waits=[tbb, bfree[bank]], serial=False))
                th_ = p.step("scalar", lambda e, d0=d0, col=col: e.activation(
                    bth[:, d0:d0 + 4, col:col + 128], PS[:, 4:8, 0:128], AF.Copy),
                    waits=tts)
                tv = p.step("vector", lambda e, d0=d0, col=col: e.tensor_tensor(
                    btl[:, d0:d0 + 4, col:col + 128], PS[:, 4:8, 0:128],
                    bth[:, d0:d0 + 4, col:col + 128], ALU.subtract), waits=[th_])
                for si in range(4):
                    bfree[4 + si] = tv
                tlast = tv
            t_r1free[pr2] = tlast
            t_blast = tlast
    st2.close()

    p.barrier()

    st3 = ExitStack()
    sb3 = lambda nm, sh, dt: st3.enter_context(nc.sbuf_tensor(nm, sh, dt))
    QF = 1024
    NG = QF // 64
    wstq = sb3("wstq", [128, 2, QF], F32)
    wbb = sb3("wbb", [128, 2, QF], BF16)
    redq = sb3("redq", [128, NG], F32)
    sbq = sb3("sbq", [128, NG], BF16)
    lnq = sb3("lnq", [128, NG], F32)
    rcq = sb3("rcq", [128, 2, NG], F32)
    rtq = sb3("rtq", [128, QF], BF16)
    g1q = sb3("g1q", [128, QF], BF16)
    nqq = sb3("nqq", [128, QF], BF16)
    whq = rtq
    d1q = g1q
    wqv = sb3("wqv", [128, QF], BF16)
    ustg = sb3("ustg", [128, 2, 8, 128], U16)
    wsg = sb3("wsg", [128, 2, ND, 128], BF16)
    wsu = sb3("wsu", [128, 2, ND, 128], BF16)
    uslab = sb3("uslab", [128, 2, 2, ND, 128], U16)
    silu_s = sb3("silu_s", [128, 1024], F32)
    h32 = sb3("h32", [128, 1024], F32)
    hsq = silu_s
    hhs = sb3("hhs", [128, 1024], BF16)
    hls = sb3("hls", [128, 1024], BF16)
    qs = {"par": 0, "wst_free": [[], []], "wbb_free": [None, None],
          "rcq_free": [None, None], "ustg_free": [None, None],
          "tbank": [None, None, None, None], "last_store": None, "g1_free": None,
          "rtq_free": None}

    def quant_chain(load_src, dst_hbm_ap, transpose):
        par = qs["par"]
        qs["par"] ^= 1
        tl = p.dma("sync", "d_wq%d" % par, wstq[:, par, :], load_src,
                   waits=list(qs["wst_free"][par]))
        tcp = p.step("scalar", lambda e, par=par: e.activation(wbb[:, par, :], wstq[:, par, :],
                                                               AF.Copy),
                     waits=[tl, qs["wbb_free"][par]])
        qs["wst_free"][par] = [tcp]
        trd = p.step("vector", lambda e, par=par: e.reduce_sum(
            redq[:], wbb[:, par, :].rearrange("p (g k) -> p g k", k=64), AX.X,
            apply_absolute_value=True), waits=[tcp])
        tcl = p.step("vector", lambda e: e.tensor_scalar(
            sbq[:], redq[:], 1.0 / 64, 1e-8, ALU.mult, ALU.max), waits=[trd])
        tln = p.step("scalar", lambda e: e.activation(lnq[:], sbq[:], AF.Ln), waits=[tcl])
        tex = p.step("scalar", lambda e, par=par: e.activation(rcq[:, par, :], lnq[:], AF.Exp,
                                                               scale=-1.0),
                     waits=[tln, qs["rcq_free"][par]])
        rc_rep = bass.AP(rcq.ap().tensor, par * NG, [[2 * NG, 128], [1, NG], [0, 64]])
        sb_rep = bass.AP(sbq.ap().tensor, 0, [[NG, 128], [1, NG], [0, 64]])
        g3 = lambda t: t.rearrange("p (g k) -> p g k", k=64)
        trt = p.step("vector", lambda e, par=par: e.tensor_tensor(
            g3(rtq[:]), g3(wbb[:, par, :]), rc_rep, ALU.mult), waits=[tex, qs["rtq_free"]])
        qs["rcq_free"][par] = trt
        tgt = p.step("vector", lambda e: e.tensor_scalar(g1q[:], rtq[:], 0.5, None, ALU.is_gt),
                     waits=[trt])
        tnq = p.step("vector", lambda e: e.scalar_tensor_tensor(
            nqq[:], rtq[:], -0.5, g1q[:], ALU.is_lt, ALU.subtract), waits=[tgt])
        twh = p.step("vector", lambda e: e.scalar_tensor_tensor(
            g3(whq[:]), g3(nqq[:]), -1.0, sb_rep, ALU.mult, ALU.mult), waits=[tnq])
        td1 = p.step("vector", lambda e, par=par: e.tensor_tensor(
            d1q[:], whq[:], wbb[:, par, :], ALU.subtract), waits=[twh])
        tqv = p.step("vector", lambda e, par=par: e.tensor_tensor(
            wqv[:], wbb[:, par, :], d1q[:], ALU.add), waits=[td1, qs["last_store"]])
        qs["wbb_free"][par] = tqv
        if not transpose:
            tst = p.dma("gpsimd", "st_wq", dst_hbm_ap, wqv[:].bitcast(U16), waits=[tqv])
        else:
            ev = []
            for half in range(2):
                tts = []
                for si in range(4):
                    sub = half * 4 + si
                    bank = 4 + si
                    tts.append(p.step("tensor", lambda e, sub=sub, bank=bank: e.matmul(
                        PS[:, bank, 0:128], wqv[:, sub * 128:(sub + 1) * 128], identb[:],
                        start=True, stop=True), waits=[tqv, qs["tbank"][si]], serial=False))
                tev = p.step("scalar", lambda e, half=half, par=par: e.activation(
                    ustg[:, par, half * 4:half * 4 + 4, :].bitcast(BF16), PS[:, 4:8, 0:128],
                    AF.Copy), waits=tts + [qs["ustg_free"][par]])
                for si in range(4):
                    qs["tbank"][si] = tev
                ev.append(tev)
            tst = p.dma("gpsimd", "st_wq", dst_hbm_ap, ustg[:, par, :, :], waits=ev)
            qs["ustg_free"][par] = tst
        qs["last_store"] = tst
        return tst

    t_hh_free = [None]
    t_ssq = [t_ssq0]
    t_uslab_free = [None, None]
    t_wsgu_free = [None, None]

    def emit_mm_item(jj, th2, wq_tok, tctx=[None]):
        par = jj % 2
        tcol = th2 * 1024
        if th2 == 0:
            l1 = p.dma("sync", "d_sl%d" % par, uslab[:, par, 0, :, :],
                       wgh_n[jj * 128:(jj + 1) * 128, :].rearrange("p (c n) -> p c n", n=128),
                       waits=[t_uslab_free[par], wq_tok])
            l2 = p.dma("sync", "d_sl%d" % par, uslab[:, par, 1, :, :],
                       wgh_n[Hs + jj * 128:Hs + (jj + 1) * 128, :]
                       .rearrange("p (c n) -> p c n", n=128))
            evs = []
            for gu_i in range(2):
                dst = wsg if gu_i == 0 else wsu
                for d0 in range(0, ND, 4):
                    tts = []
                    for si in range(4):
                        d = d0 + si
                        bank = 4 + si
                        tts.append(p.step("tensor", lambda e, gu_i=gu_i, d=d, bank=bank, par=par:
                                          e.matmul(PS[:, bank, 0:128],
                                                   uslab[:, par, gu_i, d, :].bitcast(BF16),
                                                   identb[:], start=True, stop=True),
                                          waits=[l2, qs["tbank"][si]], serial=False))
                    tev = p.step("scalar", lambda e, dst=dst, d0=d0, par=par: e.activation(
                        dst[:, par, d0:d0 + 4, :], PS[:, 4:8, 0:128], AF.Copy),
                        waits=tts + [t_wsgu_free[par]])
                    for si in range(4):
                        qs["tbank"][si] = tev
                    evs.append(tev)
            tctx[0] = evs
        evs = tctx[0]
        def gatemm(e, par=par, tcol=tcol):
            last = None
            for d in range(ND):
                lh = wsg[:, par, d, :]
                for half in range(2):
                    cs = tcol + half * 512
                    last = e.matmul(PS[:, half, :], lh, bth[:, d, cs:cs + 512],
                                    start=(d == 0), stop=False)
                    last = e.matmul(PS[:, half, :], lh, btl[:, d, cs:cs + 512],
                                    start=False, stop=(d == ND - 1))
            return last
        def upmm(e, par=par, tcol=tcol):
            last = None
            for d in range(ND):
                lh = wsu[:, par, d, :]
                for half in range(2):
                    cs = tcol + half * 512
                    last = e.matmul(PS[:, 2 + half, :], lh, bth[:, d, cs:cs + 512],
                                    start=(d == 0), stop=False)
                    last = e.matmul(PS[:, 2 + half, :], lh, btl[:, d, cs:cs + 512],
                                    start=False, stop=(d == ND - 1))
            return last
        tg_ = p.step("tensor", gatemm, waits=evs + [bfree[0], bfree[1]], serial=False)
        tu_ = p.step("tensor", upmm, waits=evs + [bfree[2], bfree[3]], serial=False)
        t_uslab_free[par] = tu_
        t_wsgu_free[par] = tu_
        return (jj, tcol, tg_, tu_)

    def emit_evac(ctx):
        jj, tcol, tg_, tu_ = ctx
        hrow = jj * 128
        e1 = p.step("scalar", lambda e: e.activation(silu_s[:], PS[:, 0:2, :], AF.Silu),
                    waits=[tg_, t_ssq[0]])
        e2 = p.step("vector", lambda e: e.tensor_tensor(h32[:], silu_s[:], PS[:, 2:4, :],
                                                        ALU.mult), waits=[e1, tu_])
        bfree[0] = e2; bfree[1] = e2; bfree[2] = e2; bfree[3] = e2
        e3 = p.step("scalar", lambda e: e.activation(hsq[:], h32[:], AF.Square), waits=[e2])
        t_ssq[0] = p.step("vector", lambda e, tcol=tcol: e.tensor_tensor(
            ssqP[:, tcol:tcol + 1024], ssqP[:, tcol:tcol + 1024], hsq[:], ALU.add),
            waits=[e3, t_ssq[0]])
        e5 = p.step("scalar", lambda e: e.activation(hhs[:], h32[:], AF.Copy),
                    waits=[e2, t_hh_free[0]])
        e6 = p.step("vector", lambda e: e.tensor_tensor(hls[:], h32[:], hhs[:], ALU.subtract),
                    waits=[e5])
        p.dma("sync", "st_h", hhi[hrow:hrow + 128, tcol:tcol + 1024], hhs[:], waits=[e6])
        t_hh_free[0] = p.dma("sync", "st_h", hlo[hrow:hrow + 128, tcol:tcol + 1024],
                             hls[:], waits=[e6])
        return tu_

    NBP = Hs // 1024
    NJT = 1024 // 128
    NTH = max(1, n_own // 1024)
    NWPC = (DM // 128) * (Hs // 1024)
    wp_i = [0]
    wq_tok_of = {}
    pend = []
    for bp in range(NBP + 1):
        if bp < NBP:
            qthunks = []
            for jr in range(8 * bp, 8 * (bp + 1)):
                for dh in range(2):
                    qthunks.append((wgun[jr * 128:(jr + 1) * 128, dh * 1024:(dh + 1) * 1024],
                                    wgh_n[jr * 128:(jr + 1) * 128, dh * 1024:(dh + 1) * 1024],
                                    False))
                    qthunks.append((wgun[Hs + jr * 128:Hs + (jr + 1) * 128, dh * 1024:(dh + 1) * 1024],
                                    wgh_n[Hs + jr * 128:Hs + (jr + 1) * 128, dh * 1024:(dh + 1) * 1024],
                                    False))
            w0 = (NWPC * bp) // NBP
            w1 = (NWPC * (bp + 1)) // NBP
            for wi in range(w0, w1):
                er, h8 = wi // (Hs // 1024), wi % (Hs // 1024)
                qthunks.append((wpn[er * 128:(er + 1) * 128, h8 * 1024:(h8 + 1) * 1024],
                                wph[h8 * 1024:(h8 + 1) * 1024, er * 128:(er + 1) * 128]
                                .rearrange("(s p) j -> p s j", p=128),
                                True))
        else:
            qthunks = []
        items = [(jj, th2) for jj in range((bp - 1) * NJT, bp * NJT) for th2 in range(NTH)]             if bp > 0 else []
        nq, ni = len(qthunks), len(items)
        qi = ii = 0
        last_st = None
        prev_item = None
        while qi < nq or ii < ni:
            if qi < nq and (ii >= ni or qi * max(ni, 1) <= ii * nq):
                last_st = quant_chain(*qthunks[qi]); qi += 1
            else:
                jj, th2 = items[ii]
                if th2 == 0:
                    cur_tctx = [None]
                if prev_item is not None:
                    emit_evac(prev_item)
                prev_item = emit_mm_item(jj, th2, wq_tok_of[bp - 1], tctx=cur_tctx)
                ii += 1
        if prev_item is not None:
            emit_evac(prev_item)
        if bp < NBP:
            wq_tok_of[bp] = ("st_wq", p.ctr["st_wq"])
    n_h_stores = p.ctr["st_h"]
    n_wp_stores = p.ctr["st_wq"]
    st3.close()

    p.barrier()

    st3b = ExitStack()
    rowt = st3b.enter_context(nc.sbuf_tensor("rowt", [1, 4, 512], F32))
    for tq in range(NTQ):
        tm = p.step("tensor", lambda e, tq=tq: e.matmul(
            PS[0:1, 7, :], ones1fs[:], ssqP[:, tq * 512:(tq + 1) * 512],
            start=True, stop=True), waits=[bfree[7]], serial=False)
        t1 = p.step("vector", lambda e: e.tensor_scalar(
            rowt[:, 0, :], PS[0:1, 7, :], 1.0 / Hs, EPS_RMS, ALU.mult, ALU.add), waits=[tm])
        bfree[7] = t1
        t2 = p.step("vector", lambda e: e.reciprocal(rowt[:, 1, :], rowt[:, 0, :]), waits=[t1])
        t3 = p.step("scalar", lambda e: e.activation(rowt[:, 1, :], rowt[:, 1, :], AF.Sqrt),
                    waits=[t2])
        tn = t3
        for _ in range(2):
            n1 = p.step("scalar", lambda e: e.activation(rowt[:, 2, :], rowt[:, 1, :], AF.Square),
                        waits=[tn])
            n2 = p.step("vector", lambda e: e.tensor_tensor(rowt[:, 2, :], rowt[:, 2, :],
                                                            rowt[:, 0, :], ALU.mult), waits=[n1])
            n3 = p.step("vector", lambda e: e.tensor_scalar(rowt[:, 2, :], rowt[:, 2, :],
                                                            -0.5, 1.5, ALU.mult, ALU.add),
                        waits=[n2])
            tn = p.step("vector", lambda e: e.tensor_tensor(rowt[:, 1, :], rowt[:, 1, :],
                                                            rowt[:, 2, :], ALU.mult), waits=[n3])
        for s in range(NS):
            tt = p.step("tensor", lambda e, s=s: e.transpose(
                PS[:, 6, 0:1], rowt[0:1, 1, s * 128:(s + 1) * 128], onesrs[0:1, 0:1]),
                waits=[tn, bfree[6]], serial=False)
            tv = p.step("vector", lambda e, tq=tq, s=s: e.tensor_copy(
                invr[:, tq, s:s + 1], PS[:, 6, 0:1]), waits=[tt])
            bfree[6] = tv
    st3b.close()

    p.barrier()

    st4 = ExitStack()
    sb4 = lambda nm, sh, dt: st4.enter_context(nc.sbuf_tensor(nm, sh, dt))
    hts_h = sb4("hts_h", [128, 4, 512], BF16)
    hts_l = sb4("hts_l", [128, 4, 512], BF16)
    wps = sb4("wps", [128, 4, 1024], U16)
    o32s = sb4("o32s", [128, 1024], F32)
    r1s = sb4("r1s", [128, 1024], F32)
    ots = sb4("ots", [128, 1024], F32)
    htok = ("st_h", n_h_stores)
    wptok = ("st_wq", n_wp_stores)
    r1tok = ("st_r1", n_r1_stores)
    t_out_dma = None
    t_hfree = [None, None, None, None]
    t_r1s_free = None
    for tp in range(NTQ):
        for eh in range(2):
            gtoks = [None] * NS
            for hc in range(NHC):
                par = hc % 4
                ta_ = p.dma("sync", "d_hh%d" % par, hts_h[:, par, :],
                      hhi[hc * 128:(hc + 1) * 128, tp * 512:(tp + 1) * 512],
                      waits=[t_hfree[par], htok])
                tb_ = p.dma("gpsimd", "d_hl%d" % par, hts_l[:, par, :],
                      hlo[hc * 128:(hc + 1) * 128, tp * 512:(tp + 1) * 512],
                      waits=[t_hfree[par], htok])
                tl3 = p.dma("sync", "d_hw%d" % par, wps[:, par, :],
                            wph[hc * 128:(hc + 1) * 128, eh * 1024:(eh + 1) * 1024],
                            waits=[t_hfree[par], wptok])
                for ts in range(NS):
                    def rmm(e, ts=ts, par=par, hc=hc):
                        last = None
                        for half in range(2):
                            o = PS[:, ts * 2 + half, :]
                            wslab = wps[:, par, half * 512:(half + 1) * 512].bitcast(BF16)
                            last = e.matmul(o, hts_h[:, par, ts * 128:(ts + 1) * 128], wslab,
                                            start=(hc == 0), stop=False)
                            last = e.matmul(o, hts_l[:, par, ts * 128:(ts + 1) * 128], wslab,
                                            start=False, stop=(hc == NHC - 1))
                        return last
                    gtoks[ts] = p.step("tensor", rmm,
                                       waits=[ta_, tb_, tl3] + ([bfree[ts * 2], bfree[ts * 2 + 1]]
                                                                if hc == 0 else []), serial=False)
                t_hfree[par] = gtoks[NS - 1]
            for ts in range(NS):
                rows = tp * 512 + ts * 128
                e1 = p.step("vector", lambda e, ts=ts, tp=tp: e.tensor_scalar_mul(
                    o32s[:], PS[:, ts * 2:ts * 2 + 2, :], invr[:, tp, ts:ts + 1]),
                    waits=[gtoks[ts]])
                bfree[ts * 2] = e1
                bfree[ts * 2 + 1] = e1
                tlr = p.dma("sync", "d_r1s", r1s[:], r1d[rows:rows + 128, eh * 1024:(eh + 1) * 1024],
                            waits=[t_r1s_free, r1tok])
                e2 = p.step("vector", lambda e: e.tensor_tensor(ots[:], o32s[:], r1s[:], ALU.add),
                            waits=[e1, tlr, t_out_dma])
                t_r1s_free = e2
                t_out_dma = p.dma("gpsimd", "st_out", out[rows:rows + 128, eh * 1024:(eh + 1) * 1024],
                                  ots[:], waits=[e2])
    p.step("gpsimd", lambda e: None, waits=[t_out_dma], serial=False)
    st4.close()
    ffn_st.close()

    p.emit_all()
    st.close()
    return nc


def _host_consts():
    import ml_dtypes
    BF = ml_dtypes.bfloat16
    sig = np.arange(C)[:, None]
    ta = np.arange(C)[None, :]
    expnt = np.where(sig > ta, (sig - ta - 1).astype(np.float32),
                     np.float32(1e30)).astype(np.float32)
    expn_in = np.ascontiguousarray(expnt.reshape(4, 128, C))
    tauv_in = np.ascontiguousarray((C - 1 - np.arange(C, dtype=np.float32)).reshape(4, 128, 1))
    sigv_in = np.ascontiguousarray(np.arange(C, dtype=np.float32).reshape(4, 128, 1))
    id32_in = np.eye(128, dtype=np.float32)
    seg_in = np.zeros((128, 2), dtype=np.float32)
    seg_in[0:64, 0] = 1.0 / 64
    seg_in[64:128, 1] = 1.0 / 64
    segt_in = np.zeros((2, 128), dtype=np.float32)
    segt_in[0, 0:64] = 1.0
    segt_in[1, 64:128] = 1.0
    return dict(expn=expn_in, tauv=tauv_in, sigv=sigv_in, id32=id32_in,
                idbf=id32_in.astype(BF), seg=seg_in.astype(BF),
                segtb=segt_in.astype(BF), segtf=segt_in,
                onesr=np.ones((1, 128), dtype=np.float32),
                ones1f=np.ones((128, 1), dtype=np.float32),
                ones1b=np.ones((128, 1), dtype=np.float32).astype(BF))


def kernel(L0_attn_x, L0_attn_Wq, L0_attn_Wk, L0_attn_Wv, L0_attn_Wo, L0_attn_Wg,
           L0_attn_decay_logits, L0_attn_out_scale, L0_ffn_w_gate_up, L0_ffn_w_proj):
    xfull = np.ascontiguousarray(np.asarray(L0_attn_x, dtype=np.float32))
    B, T, _ = xfull.shape
    half = T // 2
    wq_in = np.ascontiguousarray(np.asarray(L0_attn_Wq, np.float32))
    wk_in = np.ascontiguousarray(np.asarray(L0_attn_Wk, np.float32))
    wv_in = np.ascontiguousarray(np.asarray(L0_attn_Wv, np.float32))
    wo_in = np.ascontiguousarray(np.asarray(L0_attn_Wo, np.float32))
    wg_in = np.zeros((DM, 8), dtype=np.float32)
    wg_in[:, 0:3] = np.asarray(L0_attn_Wg, np.float32)
    wgun_in = np.ascontiguousarray(np.asarray(L0_ffn_w_gate_up, np.float32))
    wpn_in = np.ascontiguousarray(np.asarray(L0_ffn_w_proj, np.float32))
    dl = np.asarray(L0_attn_decay_logits, np.float32)
    osc = np.float32(L0_attn_out_scale)
    consts = _host_consts()

    nc = build_nc(half, half, wpn_in.shape[1])
    in_maps = []
    for c in range(8):
        b, hf = c // 2, c % 2
        t0 = hf * half
        xc = np.ascontiguousarray(xfull[b, t0:t0 + half])
        xtc = np.ascontiguousarray(xfull[b, t0:t0 + half].T)
        if hf == 0:
            xoc = np.ascontiguousarray(xfull[b, half:])
            xotc = np.ascontiguousarray(xfull[b, half:].T)
            smask = 1.0
        else:
            xoc, xotc, smask = xc, xtc, 0.0
        cstv = np.zeros((1, 16), dtype=np.float32)
        cstv[0, 0:3] = dl
        cstv[0, 3] = osc
        cstv[0, 4] = smask
        cstv[0, 5] = float(C)
        m = dict(x=xc, xt=xtc, xo=xoc, xot=xotc, wq=wq_in, wk=wk_in, wv=wv_in,
                 wo=wo_in, wg=wg_in, cst=cstv, wgun=wgun_in, wpn=wpn_in, **consts)
        in_maps.append(m)
    res = run_bass_kernel_spmd(nc, in_maps, core_ids=list(range(8)))
    outv = np.empty_like(xfull)
    for c in range(8):
        b, hf = c // 2, c % 2
        outv[b, hf * half:(hf + 1) * half] = res.results[c]["out"]
    return outv


# version 22
# speedup vs baseline: 1.8031x; 1.0486x over previous
import numpy as np
import concourse.bass as bass
import concourse.mybir as mybir
from concourse.bass_utils import run_bass_kernel_spmd
from contextlib import ExitStack

F32 = mybir.dt.float32
BF16 = mybir.dt.bfloat16
U16 = mybir.dt.uint16
I32 = mybir.dt.int32
AF = mybir.ActivationFunctionType
ALU = mybir.AluOpType
AX = mybir.AxisListType

C = 512
D = 512
DM = 2048
NS = 4
ND = 16
NE = 4
EPS_LN = 1e-5
EPS_RMS = 1.1920929e-7


class PB:
    ENGS = ("sync", "gpsimd", "scalar", "vector", "tensor")

    def __init__(self, nc, stack):
        self.nc = nc
        self.stack = stack
        self.q = {e: [] for e in self.ENGS}
        self.sems = {}
        self.ctr = {}
        self.fence = {e: [] for e in self.ENGS}
        for e in self.ENGS:
            self.sem("p_" + e)

    def sem(self, name):
        if name not in self.sems:
            self.sems[name] = self.stack.enter_context(self.nc.semaphore(name))
            self.ctr[name] = 0
        return self.sems[name]

    def step(self, eng, fn, waits=(), amt=1, semname=None, serial=None):
        sn = semname or ("p_" + eng)
        self.sem(sn)
        wl = {}
        if serial is None:
            serial = eng in ("vector", "scalar")
        if serial and self.ctr["p_" + eng] > 0:
            wl["p_" + eng] = self.ctr["p_" + eng]
        for w in list(waits) + self.fence[eng]:
            if w is None:
                continue
            s, v = w
            if v > wl.get(s, -1):
                wl[s] = v
        self.fence[eng] = []
        self.ctr[sn] += amt
        tok = (sn, self.ctr[sn])
        self.q[eng].append((sorted(wl.items()), fn, sn, amt))
        return tok

    def dma(self, eng, sem, out_ap, in_ap, waits=(), **kw):
        self.sem(sem)
        prev = self.ctr[sem]
        w = list(waits) + ([(sem, prev)] if prev > 0 else [])
        return self.step(eng, lambda e: e.dma_start(out_ap, in_ap, **kw),
                         waits=w, amt=16, semname=sem, serial=False)

    def barrier(self):
        toks = [self.step(e, lambda en: None,
                          waits=[("p_" + e, self.ctr["p_" + e])]) for e in self.ENGS]
        for e in self.ENGS:
            self.fence[e] = list(toks)

    def emit_all(self):
        nc = self.nc
        with nc.Block() as block:
            def mk(eng_name):
                def run(e):
                    for wl, fn, sn, amt in self.q[eng_name]:
                        for s, v in wl:
                            e.wait_ge(self.sems[s], v)
                        inst = fn(e)
                        if inst is not None:
                            inst.then_inc(self.sems[sn], amt)
                        else:
                            e.nop(nofuse=True).then_inc(self.sems[sn], amt)
                return run
            block.sync(mk("sync"))
            block.gpsimd(mk("gpsimd"))
            block.scalar(mk("scalar"))
            block.vector(mk("vector"))
            block.tensor(mk("tensor"))


def build_nc(n_own, n_oth, Hs):
    H2s = 2 * Hs
    nch = n_own // C
    noch = n_oth // C
    NHC = Hs // 128
    NTQ = n_own // C
    NGG = Hs // 256
    NPE = DM // 256

    nc = bass.Bass("TRN2", target_bir_lowering=False)
    din = lambda name, shape, dt=F32: nc.dram_tensor(name, shape, dt, kind="ExternalInput").ap()

    x = din("x", [n_own, DM]);        xt = din("xt", [DM, n_own])
    xo = din("xo", [n_oth, DM]);      xot = din("xot", [DM, n_oth])
    wq = din("wq", [DM, D]);          wk = din("wk", [3, DM, D])
    wv = din("wv", [3, DM, D]);       wo = din("wo", [D, DM])
    wg = din("wg", [DM, 8]);          cst = din("cst", [1, 16])
    wgun = din("wgun", [H2s, DM]);    wpn = din("wpn", [DM, Hs])
    expn = din("expn", [4, 128, C]);  tauv = din("tauv", [4, 128, 1])
    sigv = din("sigv", [4, 128, 1])
    id32 = din("id32", [128, 128]);   idbf = din("idbf", [128, 128], BF16)
    seg = din("seg", [128, 2], BF16); segtb = din("segtb", [2, 128], BF16)
    segtf = din("segtf", [2, 128]);   onesr = din("onesr", [1, 128])
    ones1f = din("ones1f", [128, 1]); ones1b = din("ones1b", [128, 1], BF16)

    out = nc.dram_tensor("out", [n_own, DM], F32, kind="ExternalOutput").ap()

    r1d = nc.dram_tensor("r1d", [n_own, DM], F32, kind="Internal").ap()
    hhi = nc.dram_tensor("hhi", [Hs, n_own], BF16, kind="Internal").ap()
    hlo = nc.dram_tensor("hlo", [Hs, n_own], BF16, kind="Internal").ap()
    wph = nc.dram_tensor("wph", [Hs, DM], U16, kind="Internal").ap()
    wgh_n = nc.dram_tensor("wgh_n", [H2s, DM], U16, kind="Internal").ap()
    s_w = {}
    for nm, shp in [("q", [DM, D]), ("k", [3, DM, D]), ("v", [3, DM, D]), ("o", [D, DM])]:
        s_w[nm + "h"] = nc.dram_tensor("s_w" + nm + "h", shp, BF16, kind="Internal").ap()
        s_w[nm + "l"] = nc.dram_tensor("s_w" + nm + "l", shp, BF16, kind="Internal").ap()

    st = ExitStack()
    p = PB(nc, st)
    sbt = lambda nm, sh, dt: st.enter_context(nc.sbuf_tensor(nm, sh, dt))

    PS = st.enter_context(nc.psum_tensor("PS", [128, 8, 512], F32))
    bfree = [None] * 8

    csts = sbt("csts", [1, 16], F32)
    srow = sbt("srow", [1, 16], F32)
    srob = sbt("srob", [1, 4], BF16)
    sbc = sbt("sbc", [128, 16], F32)
    ident = sbt("ident", [128, 128], F32)
    identb = sbt("identb", [128, 128], BF16)
    segs = sbt("segs", [128, 2], BF16)
    segtbs = sbt("segtbs", [2, 128], BF16)
    segtfs = sbt("segtfs", [2, 128], F32)
    onesrs = sbt("onesrs", [1, 128], F32)
    ones1fs = sbt("ones1fs", [128, 1], F32)
    ones1bs = sbt("ones1bs", [128, 1], BF16)
    invr = sbt("invr", [128, NTQ, 4], F32)
    ap_st = ExitStack()
    apt = lambda nm, sh, dt: ap_st.enter_context(nc.sbuf_tensor(nm, sh, dt))
    dgh = apt("dgh", [128, 3, 128], BF16)
    dgl = apt("dgl", [128, 3, 128], BF16)
    cvec = apt("cvec", [128, 4, 4], F32)
    evecs = apt("evecs", [128, 2, 4, 4], F32)
    msk = apt("msk", [128, 3, 4, 512], BF16)
    tauvs = apt("tauvs", [128, 4, 1], F32)
    sigvs = apt("sigvs", [128, 4, 1], F32)
    csr_q = apt("csr_q", [1, 512], BF16)
    csr_k = apt("csr_k", [1, 3, 512], BF16)
    csr_v = apt("csr_v", [1, 3, 512], BF16)
    csr_g = apt("csr_g", [1, 8], BF16)
    wgh = apt("wgh", [128, ND, 8], BF16)
    wgl = apt("wgl", [128, ND, 8], BF16)
    Sh = apt("Sh", [128, 3, 4, 512], BF16)
    Sl = apt("Sl", [128, 3, 4, 512], BF16)

    st0a = ExitStack()
    expns = st0a.enter_context(nc.sbuf_tensor("expns", [128, 4, 512], F32))
    t_cst = p.dma("sync", "d_cst", csts[:], cst[:, :])
    for dst_, src_ in [(ident, id32), (identb, idbf), (segs, seg), (segtbs, segtb),
                       (segtfs, segtf), (onesrs, onesr), (ones1fs, ones1f), (ones1bs, ones1b)]:
        p.dma("sync", "d_cst", dst_[:], src_[:, :])
    p.dma("sync", "d_cst", tauvs[:], tauv.rearrange("s p o -> p s o"), allow_slow_non_contiguous=True)
    p.dma("sync", "d_cst", sigvs[:], sigv.rearrange("s p o -> p s o"), allow_slow_non_contiguous=True)
    t_ldall = p.dma("sync", "d_cst", expns[:], expn.rearrange("s p t -> p s t"))

    t0m = p.step("vector", lambda e: e.memset(srow[:], 0.0))
    t1 = p.step("scalar", lambda e: e.activation(srow[:, 14:15], csts[:, 0:1], AF.Sigmoid), waits=[t_ldall, t0m])
    t1 = p.step("scalar", lambda e: e.activation(srow[:, 0:1], srow[:, 14:15], AF.Ln), waits=[t1])
    t1 = p.step("scalar", lambda e: e.activation(srow[:, 14:15], csts[:, 1:2], AF.Sigmoid), waits=[t1])
    t1 = p.step("scalar", lambda e: e.activation(srow[:, 1:2], srow[:, 14:15], AF.Ln), waits=[t1])
    t1 = p.step("scalar", lambda e: e.activation(srow[:, 14:15], csts[:, 2:3], AF.Sigmoid), waits=[t1])
    t1 = p.step("scalar", lambda e: e.activation(srow[:, 2:3], srow[:, 14:15], AF.Ln), waits=[t1])
    tds = []
    for i in range(3):
        tds.append(p.step("scalar", lambda e, i=i: e.activation(
            srow[:, 3 + i:4 + i], csts[:, 5:6], AF.Exp, scale=srow[:, i:i + 1]), waits=[t1]))
    t2 = p.step("vector", lambda e: e.tensor_copy(srob[:, 0:3], srow[:, 3:6]), waits=tds)
    t2 = p.step("vector", lambda e: e.tensor_copy(srow[:, 6:9], srob[:, 0:3]), waits=[t2])
    t2 = p.step("vector", lambda e: e.tensor_tensor(srow[:, 9:12], srow[:, 3:6], srow[:, 6:9],
                                                    ALU.subtract), waits=[t2])
    t2 = p.step("vector", lambda e: e.tensor_copy(srow[:, 12:14], csts[:, 3:5]), waits=[t2])
    t3 = p.step("tensor", lambda e: e.matmul(PS[:, 7, 0:16], onesrs[:], srow[:, 0:16],
                                             start=True, stop=True), waits=[t2], serial=False)
    t_sbc = p.step("vector", lambda e: e.tensor_copy(sbc[:], PS[:, 7, 0:16]), waits=[t3])
    bfree[7] = t_sbc
    for i in range(3):
        p.step("vector", lambda e, i=i: e.tensor_scalar_mul(dgh[:, i, :], identb[:],
                                                            sbc[:, 6 + i:7 + i]), waits=[t_sbc])
        p.step("vector", lambda e, i=i: e.tensor_scalar_mul(dgl[:, i, :], identb[:],
                                                            sbc[:, 9 + i:10 + i]), waits=[t_sbc])
    for i in range(3):
        for s in range(4):
            p.step("scalar", lambda e, i=i, s=s: e.activation(
                cvec[:, s, i:i + 1], tauvs[:, s, :], AF.Exp, scale=sbc[:, i:i + 1]), waits=[t_sbc])
            te = p.step("scalar", lambda e, i=i, s=s: e.activation(
                evecs[:, 0, s, i:i + 1], sigvs[:, s, :], AF.Exp, scale=sbc[:, i:i + 1]), waits=[t_sbc])
            p.step("vector", lambda e, i=i, s=s: e.tensor_scalar_mul(
                evecs[:, 1, s, i:i + 1], evecs[:, 0, s, i:i + 1], sbc[:, 13:14]), waits=[te])
    for i in range(3):
        for s in range(4):
            p.step("scalar", lambda e, i=i, s=s: e.activation(
                msk[:, i, s, :], expns[:, s, :], AF.Exp, scale=sbc[:, i:i + 1]), waits=[t_sbc])

    p.barrier()
    st0a.close()

    with ExitStack() as st0:
        wst = st0.enter_context(nc.sbuf_tensor("wst", [128, 2, 512], F32))
        whst = st0.enter_context(nc.sbuf_tensor("whst", [128, 2, 512], BF16))
        wlst = st0.enter_context(nc.sbuf_tensor("wlst", [128, 2, 512], BF16))
        wcons = [None, None]

        def prep_weight(src_ap, dsth, dstl, ntile, csum_to, tile_sl):
            tcs = None
            first = [True]
            for t in range(ntile):
                par = t % 2
                tl = p.dma("sync", "d_w%d" % par, wst[:, par, :], tile_sl(src_ap, t),
                           waits=wcons[par] or [])
                th = p.step("vector", lambda e, par=par: e.tensor_copy(whst[:, par, :], wst[:, par, :]),
                            waits=[tl])
                tlo = p.step("vector", lambda e, par=par: e.tensor_tensor(
                    wlst[:, par, :], wst[:, par, :], whst[:, par, :], ALU.subtract), waits=[th])
                if csum_to is not None:
                    def csmm(e, par=par, t=t, f=first[0]):
                        e.matmul(PS[0:1, 6, :], ones1bs[:], whst[:, par, :], start=f, stop=False)
                        return e.matmul(PS[0:1, 6, :], ones1bs[:], wlst[:, par, :],
                                        start=False, stop=(t == ntile - 1))
                    tcs = p.step("tensor", csmm, waits=[tlo, bfree[6] if first[0] else None],
                                 serial=False)
                    first[0] = False
                d1 = p.dma("gpsimd", "st_wei", tile_sl(dsth, t), whst[:, par, :], waits=[tlo])
                d2 = p.dma("gpsimd", "st_wei", tile_sl(dstl, t), wlst[:, par, :], waits=[tlo])
                wcons[par] = [d2, tcs] if csum_to is not None else [d2]
            if csum_to is not None:
                tv = p.step("vector", lambda e: e.tensor_copy(csum_to, PS[0:1, 6, :]), waits=[tcs])
                bfree[6] = tv

        rowtile = lambda ap_, t: ap_[t * 128:(t + 1) * 128, :]
        wo_tile = lambda ap_, t: ap_[(t // 4) * 128:(t // 4 + 1) * 128, (t % 4) * 512:(t % 4 + 1) * 512]
        prep_weight(wq, s_w["qh"], s_w["ql"], ND, csr_q[:, :], rowtile)
        for i in range(3):
            prep_weight(wk[i], s_w["kh"][i], s_w["kl"][i], ND, csr_k[:, i, :], rowtile)
            prep_weight(wv[i], s_w["vh"][i], s_w["vl"][i], ND, csr_v[:, i, :], rowtile)
        prep_weight(wo, s_w["oh"], s_w["ol"], 16, None, wo_tile)
        tcs = None
        for t in range(ND):
            par = t % 2
            tl = p.dma("sync", "d_w%d" % par, wst[:, par, 0:8], wg[t * 128:(t + 1) * 128, :],
                       waits=wcons[par] or [])
            th = p.step("vector", lambda e, par=par, t=t: e.tensor_copy(wgh[:, t, :], wst[:, par, 0:8]),
                        waits=[tl])
            tlo = p.step("vector", lambda e, par=par, t=t: e.tensor_tensor(
                wgl[:, t, :], wst[:, par, 0:8], wgh[:, t, :], ALU.subtract), waits=[th])
            def gcs(e, t=t):
                e.matmul(PS[0:1, 6, 0:8], ones1bs[:], wgh[:, t, :], start=(t == 0), stop=False)
                return e.matmul(PS[0:1, 6, 0:8], ones1bs[:], wgl[:, t, :],
                                start=False, stop=(t == ND - 1))
            tcs = p.step("tensor", gcs, waits=[tlo, bfree[6] if t == 0 else None], serial=False)
            wcons[par] = [tcs]
        tv = p.step("vector", lambda e: e.tensor_copy(csr_g[:], PS[0:1, 6, 0:8]), waits=[tcs])
        bfree[6] = tv
        n_wei_stores = p.ctr["st_wei"]

    p.barrier()

    st1 = ExitStack()
    sb1 = lambda nm, sh, dt: st1.enter_context(nc.sbuf_tensor(nm, sh, dt))
    xsb = sb1("xsb", [128, NS, DM], F32)
    xtst = sb1("xtst", [128, 2, 512], F32)
    xth = sb1("xth", [128, ND, 512], BF16)
    xtl = sb1("xtl", [128, ND, 512], BF16)
    sqs = sb1("sqs", [128, DM], F32)
    stf = sb1("stf", [128, NS, 8], F32)
    stb = sb1("stb", [128, NS, 1], BF16)
    nmrow = sb1("nmrow", [1, 512], BF16)
    einv2 = sb1("einv2", [128, NS, 4], F32)
    qh = sb1("qh", [128, NE, 512], BF16);  ql = sb1("ql", [128, NE, 512], BF16)
    kh = sb1("kh", [128, NE, 512], BF16);  kl = sb1("kl", [128, NE, 512], BF16)
    kth = sb1("kth", [128, NS, 512], BF16); ktl = sb1("ktl", [128, NS, 512], BF16)
    vth = sb1("vth", [128, NS, 512], BF16); vtl = sb1("vtl", [128, NS, 512], BF16)
    vsh = sb1("vsh", [128, NS, 512], BF16); vsl = sb1("vsl", [128, NS, 512], BF16)
    sch = sb1("sch", [128, NS, 512], BF16); scl_ = sb1("scl_", [128, NS, 512], BF16)
    scm32 = sb1("scm32", [128, 512], F32)
    rc32 = sb1("rc32", [128, 512], F32)
    ra32 = sb1("ra32", [128, 512], F32)
    vt32 = sb1("vt32", [128, 512], F32)
    blsb = sb1("blsb", [128, NS, 512], F32)
    blth = sb1("blth", [128, NE, 512], BF16); bltl = sb1("bltl", [128, NE, 512], BF16)
    gl3 = sb1("gl3", [128, NS, 8], F32)
    giv = sb1("giv", [128, NS, 4], F32)
    wsl_h = sb1("wsl_h", [128, 4, 512], BF16)
    wsl_l = sb1("wsl_l", [128, 4, 512], BF16)

    tok = {
        "xsb_free": [None] * NS,
        "xcast": [None] * ND,
        "xth_cons": None,
        "S_ready": [None] * 3,
        "S_read": [None] * 3,
        "wslab": [("p_tensor", 0)] * 4,
    }
    wei_tok = ("st_wei", n_wei_stores)

    def load_wslab(src_h, src_l, r0, c0, c1, idx):
        par = idx % 4
        p.dma("sync", "d_wsl%d" % par, wsl_h[:, par, 0:c1 - c0], src_h[r0:r0 + 128, c0:c1],
              waits=[tok["wslab"][par], wei_tok])
        t2 = p.dma("sync", "d_wsl%d" % par, wsl_l[:, par, 0:c1 - c0], src_l[r0:r0 + 128, c0:c1])
        return t2, par

    def proj_dloop(src_h, src_l, fm, t_xt, wcnt, b0=0):
        groups = [None] * 4
        for d in range(ND):
            tw, par = load_wslab(src_h, src_l, d * 128, 0, 512, wcnt[0])
            wcnt[0] += 1
            for g in range(4):
                first = (d == 0)
                def mm(e, d=d, g=g, par=par, first=first, b0=b0):
                    o = PS[:, b0 + g, :]
                    if fm:
                        w_h = wsl_h[:, par, g * 128:(g + 1) * 128]
                        w_l = wsl_l[:, par, g * 128:(g + 1) * 128]
                        e.matmul(o, w_h, xth[:, d, :], start=first, stop=False)
                        e.matmul(o, w_l, xth[:, d, :], start=False, stop=False)
                        return e.matmul(o, w_h, xtl[:, d, :], start=False, stop=False)
                    else:
                        xh_t = xth[:, d, g * 128:(g + 1) * 128]
                        xl_t = xtl[:, d, g * 128:(g + 1) * 128]
                        e.matmul(o, xh_t, wsl_h[:, par, :], start=first, stop=False)
                        e.matmul(o, xl_t, wsl_h[:, par, :], start=False, stop=False)
                        return e.matmul(o, xh_t, wsl_l[:, par, :], start=False, stop=False)
                groups[g] = p.step("tensor", mm,
                                   waits=[tw, tok["xcast"][d],
                                          bfree[b0 + g] if d == 0 else None], serial=False)
            tok["wslab"][par] = groups[3]
        return groups

    def proj_finish(groups, dst_h, dst_l, csr_ap, fm, t_nm, t_stats, extra=None, b0=0):
        for g in range(4):
            def r1mm(e, g=g, b0=b0):
                o = PS[:, b0 + g, :]
                if fm:
                    return e.matmul(o, csr_ap[:, g * 128:(g + 1) * 128], nmrow[:],
                                    start=False, stop=True)
                else:
                    return e.matmul(o, nmrow[:, g * 128:(g + 1) * 128], csr_ap,
                                    start=False, stop=True)
            groups[g] = p.step("tensor", r1mm, waits=[groups[g], t_nm], serial=False)
        ev = [None] * 4
        for g in range(4):
            pg = PS[:, b0 + g, :]
            if extra is None:
                th_ = p.step("scalar", lambda e, g=g, pg=pg: e.activation(dst_h[:, g, :], pg,
                                                                          AF.Copy), waits=[groups[g]])
                tl_ = p.step("vector", lambda e, g=g, pg=pg: e.tensor_tensor(
                    dst_l[:, g, :], pg, dst_h[:, g, :], ALU.subtract), waits=[th_])
            else:
                sc1, (edh, edl), sc2 = extra
                t0_ = p.step("vector", lambda e, g=g, pg=pg: e.tensor_scalar_mul(
                    vt32[:], pg, sc1(g)), waits=[groups[g], t_stats])
                th_ = p.step("scalar", lambda e, g=g: e.activation(dst_h[:, g, :], vt32[:],
                                                                   AF.Copy), waits=[t0_])
                tl_ = p.step("vector", lambda e, g=g: e.tensor_tensor(
                    dst_l[:, g, :], vt32[:], dst_h[:, g, :], ALU.subtract), waits=[th_])
                t0b = p.step("vector", lambda e, g=g, pg=pg: e.tensor_scalar_mul(
                    rc32[:], pg, sc2(g)), waits=[groups[g], t_stats])
                th2 = p.step("scalar", lambda e, g=g: e.activation(edh[:, g, :], rc32[:],
                                                                   AF.Copy), waits=[t0b])
                tl_ = p.step("vector", lambda e, g=g: e.tensor_tensor(
                    edl[:, g, :], rc32[:], edh[:, g, :], ALU.subtract), waits=[th2, tl_])
            ev[g] = tl_
            bfree[b0 + g] = tl_
        return ev

    def proj_pass(dst_h, dst_l, src_h, src_l, csr_ap, fm, t_nm, t_xt, t_stats,
                  wcnt, extra=None, b0=0):
        groups = proj_dloop(src_h, src_l, fm, t_xt, wcnt, b0=b0)
        return proj_finish(groups, dst_h, dst_l, csr_ap, fm, t_nm, t_stats, extra=extra, b0=b0)

    wcnt = [0]

    def do_chunk(xsrc, xtsrc, base, is_comp, cx):
        t_xs = [None] * NS
        for ts in range(NS):
            t_xs[ts] = p.dma("sync", "d_xsb", xsb[:, ts, :],
                             xsrc[base + ts * 128:base + (ts + 1) * 128, :],
                             waits=[tok["xsb_free"][ts]])
        t_x = t_xs[NS - 1]
        for d in range(ND):
            par = d % 2
            tl = p.dma("sync", "d_xt%d" % par, xtst[:, par, :], xtsrc[d * 128:(d + 1) * 128, base:base + C],
                       waits=[tok["xcast"][d - 2] if d >= 2 else tok["xth_cons"]])
            th = p.step("vector", lambda e, d=d, par=par: e.tensor_copy(xth[:, d, :], xtst[:, par, :]),
                        waits=[tl, tok["xth_cons"] if d < 2 else None])
            tok["xcast"][d] = p.step("vector", lambda e, d=d, par=par: e.tensor_tensor(
                xtl[:, d, :], xtst[:, par, :], xth[:, d, :], ALU.subtract), waits=[th])
        t_xt = tok["xcast"][ND - 1]
        if not is_comp:
            q_groups = proj_dloop(s_w["qh"], s_w["ql"], True, t_xt, wcnt, b0=0)
            k0_groups = proj_dloop(s_w["kh"][0], s_w["kl"][0], True, t_xt, wcnt, b0=4)
        else:
            k0_groups = proj_dloop(s_w["kh"][0], s_w["kl"][0], True, t_xt, wcnt, b0=4)
            v0_groups = proj_dloop(s_w["vh"][0], s_w["vl"][0], False, t_xt, wcnt, b0=0)
        t_stats = None
        t_nm = None
        for s in range(NS):
            tq_ = p.step("scalar", lambda e, s=s: e.activation(sqs[:], xsb[:, s, :], AF.Square),
                         waits=[t_xs[s], t_stats])
            ta = p.step("vector", lambda e, s=s: e.reduce_sum(
                stf[:, s, 0:1], xsb[:, s, :], AX.X, negate=True), waits=[t_xs[s]])
            tb = p.step("vector", lambda e, s=s: e.reduce_sum(stf[:, s, 6:7], sqs[:], AX.X),
                        waits=[tq_])
            tc = p.step("vector", lambda e, s=s: e.tensor_scalar_mul(
                stf[:, s, 0:1], stf[:, s, 0:1], 1.0 / DM), waits=[ta])
            td = p.step("vector", lambda e, s=s: e.tensor_scalar(
                stf[:, s, 1:2], stf[:, s, 6:7], 1.0 / DM, EPS_LN, ALU.mult, ALU.add), waits=[tb])
            te = p.step("vector", lambda e, s=s: e.tensor_scalar_mul(
                stf[:, s, 2:3], stf[:, s, 0:1], stf[:, s, 0:1]), waits=[tc])
            tf = p.step("vector", lambda e, s=s: e.scalar_tensor_tensor(
                stf[:, s, 3:4], stf[:, s, 2:3], -1.0, stf[:, s, 1:2], ALU.mult, ALU.add),
                waits=[td, te])
            tg = p.step("vector", lambda e, s=s: e.reciprocal(stf[:, s, 4:5], stf[:, s, 3:4]),
                        waits=[tf])
            tn = p.step("scalar", lambda e, s=s: e.activation(
                stf[:, s, 4:5], stf[:, s, 4:5], AF.Sqrt), waits=[tg])
            for _ in range(2):
                n1 = p.step("scalar", lambda e, s=s: e.activation(
                    stf[:, s, 6:7], stf[:, s, 4:5], AF.Square), waits=[tn])
                n2 = p.step("vector", lambda e, s=s: e.tensor_tensor(
                    stf[:, s, 6:7], stf[:, s, 6:7], stf[:, s, 3:4], ALU.mult), waits=[n1])
                n3 = p.step("vector", lambda e, s=s: e.tensor_scalar(
                    stf[:, s, 6:7], stf[:, s, 6:7], -0.5, 1.5, ALU.mult, ALU.add), waits=[n2])
                tn = p.step("vector", lambda e, s=s: e.tensor_tensor(
                    stf[:, s, 4:5], stf[:, s, 4:5], stf[:, s, 6:7], ALU.mult), waits=[n3])
            ti = p.step("vector", lambda e, s=s: e.tensor_scalar_mul(
                stf[:, s, 5:6], stf[:, s, 4:5], stf[:, s, 4:5]), waits=[tn])
            for i in range(3):
                ti = p.step("vector", lambda e, s=s, i=i, ic=1 if is_comp else 0: e.tensor_tensor(
                    einv2[:, s, i:i + 1], evecs[:, ic, s, i:i + 1], stf[:, s, 5:6], ALU.mult),
                    waits=[ti])
            tj = p.step("vector", lambda e, s=s: e.tensor_copy(
                stb[:, s, 0:1], stf[:, s, 0:1]), waits=[tc])
            t_nm = p.dma("sync", "d_nm", nmrow[0:1, s * 128:(s + 1) * 128],
                         stb[:, s, 0:1], waits=[tj], allow_slow_non_contiguous=True)
            t_stats = ti
        t_qev = None
        if not is_comp:
            t_qev = proj_finish(q_groups, qh, ql, csr_q, True, t_nm, t_stats, b0=0)
            t_k0ev = proj_finish(k0_groups, kh, kl, csr_k[:, 0, :], True, t_nm, t_stats, b0=4)
        else:
            t_k0ev = proj_finish(k0_groups, kh, kl, csr_k[:, 0, :], True, t_nm, t_stats, b0=4)
        t_gate = None
        if not is_comp:
            for s in range(NS):
                def gmm(e, s=s):
                    o = PS[:, 7, 0:8]
                    for d in range(ND):
                        e.matmul(o, xth[:, d, s * 128:(s + 1) * 128], wgh[:, d, :],
                                 start=(d == 0), stop=False)
                        e.matmul(o, xtl[:, d, s * 128:(s + 1) * 128], wgh[:, d, :],
                                 start=False, stop=False)
                        e.matmul(o, xth[:, d, s * 128:(s + 1) * 128], wgl[:, d, :],
                                 start=False, stop=False)
                    return e.matmul(o, nmrow[:, s * 128:(s + 1) * 128], csr_g[:],
                                    start=False, stop=True)
                tg1 = p.step("tensor", gmm, waits=[t_xt, t_nm, bfree[7]], serial=False)
                tg2 = p.step("vector", lambda e, s=s: e.tensor_scalar_mul(
                    gl3[:, s, 0:3], PS[:, 7, 0:3], stf[:, s, 4:5]), waits=[tg1, t_stats])
                bfree[7] = tg2
                tg3 = p.step("vector", lambda e, s=s: e.reduce_max(
                    gl3[:, s, 4:5], gl3[:, s, 0:3], AX.X, negate=True), waits=[tg2])
                tg4 = p.step("scalar", lambda e, s=s: e.activation(
                    gl3[:, s, 0:3], gl3[:, s, 0:3], AF.Exp, bias=gl3[:, s, 4:5]), waits=[tg3])
                tg5 = p.step("vector", lambda e, s=s: e.reduce_sum(
                    gl3[:, s, 4:5], gl3[:, s, 0:3], AX.X), waits=[tg4])
                tg6 = p.step("vector", lambda e, s=s: e.reciprocal(
                    gl3[:, s, 5:6], gl3[:, s, 4:5]), waits=[tg5])
                tg7 = p.step("vector", lambda e, s=s: e.tensor_scalar_mul(
                    gl3[:, s, 0:3], gl3[:, s, 0:3], gl3[:, s, 5:6]), waits=[tg6])
                t_gate = p.step("vector", lambda e, s=s: e.tensor_scalar_mul(
                    giv[:, s, 0:3], gl3[:, s, 0:3], stf[:, s, 4:5]), waits=[tg7])
        for i in range(3):
            if i == 0:
                t_kev = t_k0ev
            else:
                t_kev = proj_pass(kh, kl, s_w["kh"][i], s_w["kl"][i], csr_k[:, i, :], True,
                                  t_nm, t_xt, t_stats, wcnt)
            t_ktev = [None] * NS
            tb4 = [bfree[4], bfree[5], bfree[6], bfree[7]]
            for hl, (src, dst) in enumerate(((kh, kth), (kl, ktl))):
                for ec in range(NE):
                    tts = []
                    for ss in range(4):
                        bank = 4 + ss
                        tts.append(p.step("tensor", lambda e, ec=ec, ss=ss, bank=bank, src=src:
                                          e.matmul(PS[:, bank, 0:128],
                                                   src[:, ec, ss * 128:(ss + 1) * 128], identb[:],
                                                   start=True, stop=True),
                                          waits=[t_kev[ec], tb4[ss]], serial=False))
                    tev = p.step("scalar", lambda e, ec=ec, dst=dst: e.activation(
                        dst[:, 0:4, ec * 128:(ec + 1) * 128].rearrange("p s j -> p (s j)")
                        if False else dst[:, 0:4, ec * 128:(ec + 1) * 128],
                        PS[:, 4:8, 0:128], AF.Copy), waits=tts)
                    for ss in range(4):
                        tb4[ss] = tev
                        t_ktev[ss] = tev
            bfree[4], bfree[5], bfree[6], bfree[7] = tb4
            t_scev = [None] * NS
            if not is_comp:
                for ss in range(NS):
                    bank = 6 + (ss % 2)
                    def smm(e, ss=ss, bank=bank):
                        o = PS[:, bank, :]
                        last = None
                        for ec in range(NE):
                            last = e.matmul(o, kh[:, ec, ss * 128:(ss + 1) * 128], qh[:, ec, :],
                                            start=(ec == 0), stop=False)
                            last = e.matmul(o, kl[:, ec, ss * 128:(ss + 1) * 128], qh[:, ec, :],
                                            start=False, stop=False)
                            last = e.matmul(o, kh[:, ec, ss * 128:(ss + 1) * 128], ql[:, ec, :],
                                            start=False, stop=(ec == NE - 1))
                        return last
                    tsc = p.step("tensor", smm, waits=t_kev + t_qev + [bfree[bank]], serial=False)
                    e1 = p.step("vector", lambda e, ss=ss, bank=bank, i=i: e.scalar_tensor_tensor(
                        scm32[:], PS[:, bank, :], stf[:, ss, 4:5], msk[:, i, ss, :],
                        ALU.mult, ALU.mult), waits=[tsc, t_stats])
                    bfree[bank] = e1
                    e2 = p.step("scalar", lambda e, ss=ss: e.activation(sch[:, ss, :], scm32[:],
                                                                        AF.Copy), waits=[e1])
                    t_scev[ss] = p.step("vector", lambda e, ss=ss: e.tensor_tensor(
                        scl_[:, ss, :], scm32[:], sch[:, ss, :], ALU.subtract), waits=[e2])
            if is_comp and i == 0:
                t_vev = proj_finish(v0_groups, vth, vtl, csr_v[:, 0, :], False, t_nm, t_stats,
                                    extra=(lambda g: stf[:, g, 4:5], (vsh, vsl),
                                           lambda g, i=i: einv2[:, g, i:i + 1]), b0=0)
            else:
                t_vev = proj_pass(vth, vtl, s_w["vh"][i], s_w["vl"][i], csr_v[:, i, :], False,
                                  t_nm, t_xt, t_stats, wcnt,
                                  extra=(lambda g: stf[:, g, 4:5], (vsh, vsl),
                                         lambda g, i=i: einv2[:, g, i:i + 1]))
            t_wc = []
            if not is_comp:
                for ts in range(NS):
                    bw = 6 if ts % 2 == 0 else 7
                    bc = 4 if ts % 2 == 0 else 5
                    def wmm(e, ts=ts, bw=bw):
                        o = PS[:, bw, :]
                        last = None
                        for ss in range(NS):
                            last = e.matmul(o, sch[:, ss, ts * 128:(ts + 1) * 128], vth[:, ss, :],
                                            start=(ss == 0), stop=False)
                            last = e.matmul(o, scl_[:, ss, ts * 128:(ts + 1) * 128], vth[:, ss, :],
                                            start=False, stop=False)
                            last = e.matmul(o, sch[:, ss, ts * 128:(ts + 1) * 128], vtl[:, ss, :],
                                            start=False, stop=(ss == NS - 1))
                        return last
                    tw_ = p.step("tensor", wmm, waits=t_scev + t_vev + [bfree[bw]], serial=False)
                    def cmm(e, ts=ts, bc=bc, i=i):
                        o = PS[:, bc, :]
                        last = None
                        for ec in range(NE):
                            last = e.matmul(o, qh[:, ec, ts * 128:(ts + 1) * 128], Sh[:, i, ec, :],
                                            start=(ec == 0), stop=False)
                            last = e.matmul(o, ql[:, ec, ts * 128:(ts + 1) * 128], Sh[:, i, ec, :],
                                            start=False, stop=False)
                            last = e.matmul(o, qh[:, ec, ts * 128:(ts + 1) * 128], Sl[:, i, ec, :],
                                            start=False, stop=(ec == NE - 1))
                        return last
                    tc_ = p.step("tensor", cmm,
                                 waits=t_qev + [tok["S_ready"][i], bfree[bc]], serial=False)
                    e1 = p.step("vector", lambda e, ts=ts, bc=bc, i=i: e.tensor_scalar_mul(
                        rc32[:], PS[:, bc, :], cvec[:, ts, i:i + 1]), waits=[tc_])
                    bfree[bc] = e1
                    e2 = p.step("vector", lambda e, bw=bw: e.tensor_tensor(
                        ra32[:], PS[:, bw, :], rc32[:], ALU.add), waits=[e1, tw_])
                    bfree[bw] = e2
                    if i == 0:
                        e3 = p.step("vector", lambda e, ts=ts, i=i: e.tensor_scalar_mul(
                            blsb[:, ts, :], ra32[:], giv[:, ts, i:i + 1]), waits=[e2, t_gate])
                    else:
                        e2b = p.step("vector", lambda e, ts=ts, i=i: e.tensor_scalar_mul(
                            ra32[:], ra32[:], giv[:, ts, i:i + 1]), waits=[e2, t_gate])
                        e3 = p.step("vector", lambda e, ts=ts: e.tensor_tensor(
                            blsb[:, ts, :], blsb[:, ts, :], ra32[:], ALU.add), waits=[e2b])
                    t_wc.append(e3)
                    tok["S_read"][i] = tc_
            t_sup = []
            for ek in range(NE):
                def supmm(e, ek=ek, i=i, cx=cx):
                    o = PS[:, 4 + ek, :]
                    last = None
                    fs = True
                    if cx != 0:
                        last = e.matmul(o, dgh[:, i, :], Sh[:, i, ek, :], start=True, stop=False)
                        last = e.matmul(o, dgh[:, i, :], Sl[:, i, ek, :], start=False, stop=False)
                        last = e.matmul(o, dgl[:, i, :], Sh[:, i, ek, :], start=False, stop=False)
                        fs = False
                    for ss in range(NS):
                        last = e.matmul(o, kth[:, ss, ek * 128:(ek + 1) * 128], vsh[:, ss, :],
                                        start=fs, stop=False)
                        fs = False
                        last = e.matmul(o, ktl[:, ss, ek * 128:(ek + 1) * 128], vsh[:, ss, :],
                                        start=False, stop=False)
                        last = e.matmul(o, kth[:, ss, ek * 128:(ek + 1) * 128], vsl[:, ss, :],
                                        start=False, stop=(ss == NS - 1))
                    return last
                t_sup.append(p.step("tensor", supmm,
                                    waits=t_ktev + t_vev + [tok["S_ready"][i], bfree[4 + ek]],
                                    serial=False))
            t_sev = None
            for ek in range(NE):
                e1 = p.step("scalar", lambda e, ek=ek, i=i: e.activation(
                    Sh[:, i, ek, :], PS[:, 4 + ek, :], AF.Copy), waits=[t_sup[ek], tok["S_read"][i]])
                t_sev = p.step("vector", lambda e, ek=ek, i=i: e.tensor_tensor(
                    Sl[:, i, ek, :], PS[:, 4 + ek, :], Sh[:, i, ek, :], ALU.subtract), waits=[e1])
                bfree[4 + ek] = t_sev
            tok["S_ready"][i] = t_sev
        if not is_comp:
            t_blev = [None] * NE
            tpar = [bfree[6], bfree[7]]
            for idx in range(16):
                ts, ec = idx // 4, idx % 4
                bank = 6 + (idx % 2)
                tt = p.step("tensor", lambda e, ts=ts, ec=ec, bank=bank: e.transpose(
                    PS[:, bank, 0:128], blsb[:, ts, ec * 128:(ec + 1) * 128], ident[:]),
                    waits=[t_wc[ts], tpar[idx % 2]], serial=False)
                th_ = p.step("vector", lambda e, ts=ts, ec=ec, bank=bank: e.tensor_copy(
                    blth[:, ec, ts * 128:(ts + 1) * 128], PS[:, bank, 0:128]), waits=[tt])
                tv = p.step("vector", lambda e, ts=ts, ec=ec, bank=bank: e.tensor_tensor(
                    bltl[:, ec, ts * 128:(ts + 1) * 128], PS[:, bank, 0:128],
                    blth[:, ec, ts * 128:(ts + 1) * 128], ALU.subtract), waits=[th_])
                tpar[idx % 2] = tv
                t_blev[ec] = tv
            bfree[6], bfree[7] = tpar
            t_r1 = None
            for dq in range(4):
                gmm = [None] * NS
                for ec in range(NE):
                    tw, par = load_wslab(s_w["oh"], s_w["ol"], ec * 128, dq * 512, (dq + 1) * 512,
                                         wcnt[0])
                    wcnt[0] += 1
                    for ts in range(NS):
                        def omm(e, ts=ts, ec=ec, par=par):
                            o = PS[:, ts, :]
                            e.matmul(o, blth[:, ec, ts * 128:(ts + 1) * 128], wsl_h[:, par, :],
                                     start=(ec == 0), stop=False)
                            e.matmul(o, bltl[:, ec, ts * 128:(ts + 1) * 128], wsl_h[:, par, :],
                                     start=False, stop=False)
                            return e.matmul(o, blth[:, ec, ts * 128:(ts + 1) * 128],
                                            wsl_l[:, par, :], start=False, stop=(ec == NE - 1))
                        gmm[ts] = p.step("tensor", omm,
                                         waits=[tw, t_blev[ec]] +
                                               ([bfree[ts]] if ec == 0 else []), serial=False)
                    tok["wslab"][par] = gmm[NS - 1]
                for ts in range(NS):
                    e2 = p.step("vector", lambda e, ts=ts, dq=dq: e.scalar_tensor_tensor(
                        xsb[:, ts, dq * 512:(dq + 1) * 512], PS[:, ts, :], sbc[:, 12:13],
                        xsb[:, ts, dq * 512:(dq + 1) * 512], ALU.mult, ALU.add),
                        waits=[gmm[ts], t_xs[ts]])
                    bfree[ts] = e2
                    if dq == 3:
                        tok["xsb_free"][ts] = p.dma(
                            "gpsimd", "st_r1", r1d[base + ts * 128:base + (ts + 1) * 128, :],
                            xsb[:, ts, :], waits=[e2])
        else:
            for ts in range(NS):
                tok["xsb_free"][ts] = t_stats
        tok["xcast"] = [None] * ND
        tok["xth_cons"] = ("p_tensor", p.ctr["p_tensor"])

    cx = 0
    for j in range(noch - 1, -1, -1):
        do_chunk(xo, xot, j * C, True, cx)
        cx += 1
    for j in range(nch - 1, -1, -1):
        do_chunk(x, xt, j * C, False, cx)
        cx += 1
    n_r1_stores = p.ctr["st_r1"]
    st1.close()
    ap_st.close()

    p.barrier()

    ffn_st = ExitStack()
    sbf = lambda nm, sh, dt: ffn_st.enter_context(nc.sbuf_tensor(nm, sh, dt))
    bth = sbf("bth", [128, ND, n_own], BF16)
    btl = sbf("btl", [128, ND, n_own], BF16)
    ssqP = sbf("ssqP", [128, n_own], F32)
    t_ssq0 = p.step("vector", lambda e: e.memset(ssqP[:], 0.0))

    st2 = ExitStack()
    sb2 = lambda nm, sh, dt: st2.enter_context(nc.sbuf_tensor(nm, sh, dt))
    r1sb = sb2("r1sb", [128, 2, DM], F32)
    sq2 = sb2("sq2", [128, DM], F32)
    b32 = sq2
    st2f = sb2("st2f", [128, NS, 8], F32)
    t_r1free = [None, None]
    t_blast = None
    r1tok = ("st_r1", n_r1_stores)
    for gsub in range(NTQ * NS):
        tc2, s = gsub // NS, gsub % NS
        pr2 = gsub % 2
        t_x = p.dma("sync", "d_r1%d" % pr2, r1sb[:, pr2, :],
                    r1d[gsub * 128:(gsub + 1) * 128, :],
                    waits=[t_r1free[pr2], r1tok])
        if True:
            tq_ = p.step("scalar", lambda e, pr2=pr2: e.activation(sq2[:], r1sb[:, pr2, :], AF.Square),
                         waits=[t_x, t_blast])
            ta = p.step("vector", lambda e, s=s, pr2=pr2: e.reduce_sum(st2f[:, s, 0:1], r1sb[:, pr2, :], AX.X),
                        waits=[t_x])
            tb = p.step("vector", lambda e, s=s: e.reduce_sum(st2f[:, s, 6:7], sq2[:], AX.X),
                        waits=[tq_])
            tc1 = p.step("vector", lambda e, s=s: e.tensor_scalar_mul(
                st2f[:, s, 0:1], st2f[:, s, 0:1], 1.0 / DM), waits=[ta])
            td = p.step("vector", lambda e, s=s: e.tensor_scalar(
                st2f[:, s, 1:2], st2f[:, s, 6:7], 1.0 / DM, EPS_LN, ALU.mult, ALU.add), waits=[tb])
            te = p.step("vector", lambda e, s=s: e.tensor_scalar_mul(
                st2f[:, s, 2:3], st2f[:, s, 0:1], st2f[:, s, 0:1]), waits=[tc1])
            tf = p.step("vector", lambda e, s=s: e.scalar_tensor_tensor(
                st2f[:, s, 3:4], st2f[:, s, 2:3], -1.0, st2f[:, s, 1:2], ALU.mult, ALU.add),
                waits=[td, te])
            tg = p.step("vector", lambda e, s=s: e.reciprocal(st2f[:, s, 4:5], st2f[:, s, 3:4]),
                        waits=[tf])
            tn = p.step("scalar", lambda e, s=s: e.activation(
                st2f[:, s, 4:5], st2f[:, s, 4:5], AF.Sqrt), waits=[tg])
            for _ in range(2):
                n1 = p.step("scalar", lambda e, s=s: e.activation(
                    st2f[:, s, 6:7], st2f[:, s, 4:5], AF.Square), waits=[tn])
                n2 = p.step("vector", lambda e, s=s: e.tensor_tensor(
                    st2f[:, s, 6:7], st2f[:, s, 6:7], st2f[:, s, 3:4], ALU.mult), waits=[n1])
                n3 = p.step("vector", lambda e, s=s: e.tensor_scalar(
                    st2f[:, s, 6:7], st2f[:, s, 6:7], -0.5, 1.5, ALU.mult, ALU.add), waits=[n2])
                tn = p.step("vector", lambda e, s=s: e.tensor_tensor(
                    st2f[:, s, 4:5], st2f[:, s, 4:5], st2f[:, s, 6:7], ALU.mult), waits=[n3])
            tbb = p.step("vector", lambda e, s=s, pr2=pr2: e.tensor_scalar(
                b32[:], r1sb[:, pr2, :], st2f[:, s, 0:1], st2f[:, s, 4:5],
                ALU.subtract, ALU.mult), waits=[tn, t_x, tb])
            tlast = None
            col = gsub * 128
            for d0 in range(0, ND, 4):
                tts = []
                for si in range(4):
                    dch = d0 + si
                    bank = 4 + si
                    tts.append(p.step("tensor", lambda e, dch=dch, bank=bank: e.transpose(
                        PS[:, bank, 0:128], b32[:, dch * 128:(dch + 1) * 128], ident[:]),
                        waits=[tbb, bfree[bank]], serial=False))
                th_ = p.step("scalar", lambda e, d0=d0, col=col: e.activation(
                    bth[:, d0:d0 + 4, col:col + 128], PS[:, 4:8, 0:128], AF.Copy),
                    waits=tts)
                tv = p.step("vector", lambda e, d0=d0, col=col: e.tensor_tensor(
                    btl[:, d0:d0 + 4, col:col + 128], PS[:, 4:8, 0:128],
                    bth[:, d0:d0 + 4, col:col + 128], ALU.subtract), waits=[th_])
                for si in range(4):
                    bfree[4 + si] = tv
                tlast = tv
            t_r1free[pr2] = tlast
            t_blast = tlast
    st2.close()

    p.barrier()

    st3 = ExitStack()
    sb3 = lambda nm, sh, dt: st3.enter_context(nc.sbuf_tensor(nm, sh, dt))
    QF = 1024
    NG = QF // 64
    wstq = sb3("wstq", [128, 2, QF], F32)
    wbb = sb3("wbb", [128, 2, QF], BF16)
    redq = sb3("redq", [128, NG], F32)
    sbq = sb3("sbq", [128, NG], BF16)
    lnq = sb3("lnq", [128, NG], F32)
    rcq = sb3("rcq", [128, 2, NG], F32)
    rtq = sb3("rtq", [128, QF], BF16)
    g1q = sb3("g1q", [128, QF], BF16)
    nqq = sb3("nqq", [128, QF], BF16)
    whq = rtq
    d1q = g1q
    wqv = sb3("wqv", [128, QF], BF16)
    ustg = sb3("ustg", [128, 2, 8, 128], U16)
    wsg = sb3("wsg", [128, 2, ND, 128], BF16)
    wsu = sb3("wsu", [128, 2, ND, 128], BF16)
    uslab = sb3("uslab", [128, 2, 2, ND, 128], U16)
    silu_s = sb3("silu_s", [128, 1024], F32)
    h32 = sb3("h32", [128, 1024], F32)
    hsq = silu_s
    hhs = sb3("hhs", [128, 1024], BF16)
    hls = sb3("hls", [128, 1024], BF16)
    qs = {"par": 0, "wst_free": [[], []], "wbb_free": [None, None],
          "rcq_free": [None, None], "ustg_free": [None, None],
          "tbank": [None, None, None, None], "last_store": None, "g1_free": None,
          "rtq_free": None}

    def quant_chain(load_src, dst_hbm_ap, transpose):
        par = qs["par"]
        qs["par"] ^= 1
        tl = p.dma("sync", "d_wq%d" % par, wstq[:, par, :], load_src,
                   waits=list(qs["wst_free"][par]))
        tcp = p.step("scalar", lambda e, par=par: e.activation(wbb[:, par, :], wstq[:, par, :],
                                                               AF.Copy),
                     waits=[tl, qs["wbb_free"][par]])
        qs["wst_free"][par] = [tcp]
        trd = p.step("vector", lambda e, par=par: e.reduce_sum(
            redq[:], wbb[:, par, :].rearrange("p (g k) -> p g k", k=64), AX.X,
            apply_absolute_value=True), waits=[tcp])
        tcl = p.step("vector", lambda e: e.tensor_scalar(
            sbq[:], redq[:], 1.0 / 64, 1e-8, ALU.mult, ALU.max), waits=[trd])
        tln = p.step("scalar", lambda e: e.activation(lnq[:], sbq[:], AF.Ln), waits=[tcl])
        tex = p.step("scalar", lambda e, par=par: e.activation(rcq[:, par, :], lnq[:], AF.Exp,
                                                               scale=-1.0),
                     waits=[tln, qs["rcq_free"][par]])
        rc_rep = bass.AP(rcq.ap().tensor, par * NG, [[2 * NG, 128], [1, NG], [0, 64]])
        sb_rep = bass.AP(sbq.ap().tensor, 0, [[NG, 128], [1, NG], [0, 64]])
        g3 = lambda t: t.rearrange("p (g k) -> p g k", k=64)
        trt = p.step("vector", lambda e, par=par: e.tensor_tensor(
            g3(rtq[:]), g3(wbb[:, par, :]), rc_rep, ALU.mult), waits=[tex, qs["rtq_free"]])
        qs["rcq_free"][par] = trt
        tgt = p.step("vector", lambda e: e.tensor_scalar(g1q[:], rtq[:], 0.5, None, ALU.is_gt),
                     waits=[trt])
        tnq = p.step("vector", lambda e: e.scalar_tensor_tensor(
            nqq[:], rtq[:], -0.5, g1q[:], ALU.is_lt, ALU.subtract), waits=[tgt])
        twh = p.step("vector", lambda e: e.scalar_tensor_tensor(
            g3(whq[:]), g3(nqq[:]), -1.0, sb_rep, ALU.mult, ALU.mult), waits=[tnq])
        td1 = p.step("vector", lambda e, par=par: e.tensor_tensor(
            d1q[:], whq[:], wbb[:, par, :], ALU.subtract), waits=[twh])
        tqv = p.step("vector", lambda e, par=par: e.tensor_tensor(
            wqv[:], wbb[:, par, :], d1q[:], ALU.add), waits=[td1, qs["last_store"]])
        qs["wbb_free"][par] = tqv
        if not transpose:
            tst = p.dma("gpsimd", "st_wq", dst_hbm_ap, wqv[:].bitcast(U16), waits=[tqv])
        else:
            ev = []
            for half in range(2):
                tts = []
                for si in range(4):
                    sub = half * 4 + si
                    bank = 4 + si
                    tts.append(p.step("tensor", lambda e, sub=sub, bank=bank: e.matmul(
                        PS[:, bank, 0:128], wqv[:, sub * 128:(sub + 1) * 128], identb[:],
                        start=True, stop=True), waits=[tqv, qs["tbank"][si]], serial=False))
                tev = p.step("scalar", lambda e, half=half, par=par: e.activation(
                    ustg[:, par, half * 4:half * 4 + 4, :].bitcast(BF16), PS[:, 4:8, 0:128],
                    AF.Copy), waits=tts + [qs["ustg_free"][par]])
                for si in range(4):
                    qs["tbank"][si] = tev
                ev.append(tev)
            tst = p.dma("gpsimd", "st_wq", dst_hbm_ap, ustg[:, par, :, :], waits=ev)
            qs["ustg_free"][par] = tst
        qs["last_store"] = tst
        return tst

    t_hh_free = [None]
    t_ssq = [t_ssq0]
    t_uslab_free = [None, None]
    t_wsgu_free = [None, None]

    def emit_mm_item(jj, th2, wq_tok, tctx=[None]):
        par = jj % 2
        tcol = th2 * 1024
        if th2 == 0:
            l1 = p.dma("sync", "d_sl%d" % par, uslab[:, par, 0, :, :],
                       wgh_n[jj * 128:(jj + 1) * 128, :].rearrange("p (c n) -> p c n", n=128),
                       waits=[t_uslab_free[par], wq_tok])
            l2 = p.dma("sync", "d_sl%d" % par, uslab[:, par, 1, :, :],
                       wgh_n[Hs + jj * 128:Hs + (jj + 1) * 128, :]
                       .rearrange("p (c n) -> p c n", n=128))
            evs = []
            for gu_i in range(2):
                dst = wsg if gu_i == 0 else wsu
                for d0 in range(0, ND, 4):
                    tts = []
                    for si in range(4):
                        d = d0 + si
                        bank = 4 + si
                        tts.append(p.step("tensor", lambda e, gu_i=gu_i, d=d, bank=bank, par=par:
                                          e.matmul(PS[:, bank, 0:128],
                                                   uslab[:, par, gu_i, d, :].bitcast(BF16),
                                                   identb[:], start=True, stop=True),
                                          waits=[l2, qs["tbank"][si]], serial=False))
                    tev = p.step("scalar", lambda e, dst=dst, d0=d0, par=par: e.activation(
                        dst[:, par, d0:d0 + 4, :], PS[:, 4:8, 0:128], AF.Copy),
                        waits=tts + [t_wsgu_free[par]])
                    for si in range(4):
                        qs["tbank"][si] = tev
                    evs.append(tev)
            tctx[0] = evs
        evs = tctx[0]
        def gatemm(e, par=par, tcol=tcol):
            last = None
            for d in range(ND):
                lh = wsg[:, par, d, :]
                for half in range(2):
                    cs = tcol + half * 512
                    last = e.matmul(PS[:, half, :], lh, bth[:, d, cs:cs + 512],
                                    start=(d == 0), stop=False)
                    last = e.matmul(PS[:, half, :], lh, btl[:, d, cs:cs + 512],
                                    start=False, stop=(d == ND - 1))
            return last
        def upmm(e, par=par, tcol=tcol):
            last = None
            for d in range(ND):
                lh = wsu[:, par, d, :]
                for half in range(2):
                    cs = tcol + half * 512
                    last = e.matmul(PS[:, 2 + half, :], lh, bth[:, d, cs:cs + 512],
                                    start=(d == 0), stop=False)
                    last = e.matmul(PS[:, 2 + half, :], lh, btl[:, d, cs:cs + 512],
                                    start=False, stop=(d == ND - 1))
            return last
        tg_ = p.step("tensor", gatemm, waits=evs + [bfree[0], bfree[1]], serial=False)
        tu_ = p.step("tensor", upmm, waits=evs + [bfree[2], bfree[3]], serial=False)
        t_uslab_free[par] = tu_
        t_wsgu_free[par] = tu_
        return (jj, tcol, tg_, tu_)

    def emit_evac(ctx):
        jj, tcol, tg_, tu_ = ctx
        hrow = jj * 128
        e1 = p.step("scalar", lambda e: e.activation(silu_s[:], PS[:, 0:2, :], AF.Silu),
                    waits=[tg_, t_ssq[0]])
        e2 = p.step("vector", lambda e: e.tensor_tensor(h32[:], silu_s[:], PS[:, 2:4, :],
                                                        ALU.mult), waits=[e1, tu_])
        bfree[0] = e2; bfree[1] = e2; bfree[2] = e2; bfree[3] = e2
        e3 = p.step("scalar", lambda e: e.activation(hsq[:], h32[:], AF.Square), waits=[e2])
        t_ssq[0] = p.step("vector", lambda e, tcol=tcol: e.tensor_tensor(
            ssqP[:, tcol:tcol + 1024], ssqP[:, tcol:tcol + 1024], hsq[:], ALU.add),
            waits=[e3, t_ssq[0]])
        e5 = p.step("scalar", lambda e: e.activation(hhs[:], h32[:], AF.Copy),
                    waits=[e2, t_hh_free[0]])
        e6 = p.step("vector", lambda e: e.tensor_tensor(hls[:], h32[:], hhs[:], ALU.subtract),
                    waits=[e5])
        p.dma("gpsimd", "st_h", hhi[hrow:hrow + 128, tcol:tcol + 1024], hhs[:], waits=[e6])
        t_hh_free[0] = p.dma("gpsimd", "st_h", hlo[hrow:hrow + 128, tcol:tcol + 1024],
                             hls[:], waits=[e6])
        return tu_

    NBP = Hs // 1024
    NJT = 1024 // 128
    NTH = max(1, n_own // 1024)
    NWPC = (DM // 128) * (Hs // 1024)
    wp_i = [0]
    wq_tok_of = {}
    pend = []
    for bp in range(NBP + 1):
        if bp < NBP:
            qthunks = []
            for jr in range(8 * bp, 8 * (bp + 1)):
                for dh in range(2):
                    qthunks.append((wgun[jr * 128:(jr + 1) * 128, dh * 1024:(dh + 1) * 1024],
                                    wgh_n[jr * 128:(jr + 1) * 128, dh * 1024:(dh + 1) * 1024],
                                    False))
                    qthunks.append((wgun[Hs + jr * 128:Hs + (jr + 1) * 128, dh * 1024:(dh + 1) * 1024],
                                    wgh_n[Hs + jr * 128:Hs + (jr + 1) * 128, dh * 1024:(dh + 1) * 1024],
                                    False))
            w0 = (NWPC * bp) // NBP
            w1 = (NWPC * (bp + 1)) // NBP
            for wi in range(w0, w1):
                er, h8 = wi // (Hs // 1024), wi % (Hs // 1024)
                qthunks.append((wpn[er * 128:(er + 1) * 128, h8 * 1024:(h8 + 1) * 1024],
                                wph[h8 * 1024:(h8 + 1) * 1024, er * 128:(er + 1) * 128]
                                .rearrange("(s p) j -> p s j", p=128),
                                True))
        else:
            qthunks = []
        items = [(jj, th2) for jj in range((bp - 1) * NJT, bp * NJT) for th2 in range(NTH)]             if bp > 0 else []
        nq, ni = len(qthunks), len(items)
        qi = ii = 0
        last_st = None
        prev_item = None
        while qi < nq or ii < ni:
            if qi < nq and (ii >= ni or qi * max(ni, 1) <= ii * nq):
                last_st = quant_chain(*qthunks[qi]); qi += 1
            else:
                jj, th2 = items[ii]
                if th2 == 0:
                    cur_tctx = [None]
                if prev_item is not None:
                    emit_evac(prev_item)
                prev_item = emit_mm_item(jj, th2, wq_tok_of[bp - 1], tctx=cur_tctx)
                ii += 1
        if prev_item is not None:
            emit_evac(prev_item)
        if bp < NBP:
            wq_tok_of[bp] = ("st_wq", p.ctr["st_wq"])
    n_h_stores = p.ctr["st_h"]
    n_wp_stores = p.ctr["st_wq"]
    st3.close()

    p.barrier()

    st3b = ExitStack()
    rowt = st3b.enter_context(nc.sbuf_tensor("rowt", [1, 4, 512], F32))
    for tq in range(NTQ):
        tm = p.step("tensor", lambda e, tq=tq: e.matmul(
            PS[0:1, 7, :], ones1fs[:], ssqP[:, tq * 512:(tq + 1) * 512],
            start=True, stop=True), waits=[bfree[7]], serial=False)
        t1 = p.step("vector", lambda e: e.tensor_scalar(
            rowt[:, 0, :], PS[0:1, 7, :], 1.0 / Hs, EPS_RMS, ALU.mult, ALU.add), waits=[tm])
        bfree[7] = t1
        t2 = p.step("vector", lambda e: e.reciprocal(rowt[:, 1, :], rowt[:, 0, :]), waits=[t1])
        t3 = p.step("scalar", lambda e: e.activation(rowt[:, 1, :], rowt[:, 1, :], AF.Sqrt),
                    waits=[t2])
        tn = t3
        for _ in range(2):
            n1 = p.step("scalar", lambda e: e.activation(rowt[:, 2, :], rowt[:, 1, :], AF.Square),
                        waits=[tn])
            n2 = p.step("vector", lambda e: e.tensor_tensor(rowt[:, 2, :], rowt[:, 2, :],
                                                            rowt[:, 0, :], ALU.mult), waits=[n1])
            n3 = p.step("vector", lambda e: e.tensor_scalar(rowt[:, 2, :], rowt[:, 2, :],
                                                            -0.5, 1.5, ALU.mult, ALU.add),
                        waits=[n2])
            tn = p.step("vector", lambda e: e.tensor_tensor(rowt[:, 1, :], rowt[:, 1, :],
                                                            rowt[:, 2, :], ALU.mult), waits=[n3])
        for s in range(NS):
            tt = p.step("tensor", lambda e, s=s: e.transpose(
                PS[:, 6, 0:1], rowt[0:1, 1, s * 128:(s + 1) * 128], onesrs[0:1, 0:1]),
                waits=[tn, bfree[6]], serial=False)
            tv = p.step("vector", lambda e, tq=tq, s=s: e.tensor_copy(
                invr[:, tq, s:s + 1], PS[:, 6, 0:1]), waits=[tt])
            bfree[6] = tv
    st3b.close()

    p.barrier()

    st4 = ExitStack()
    sb4 = lambda nm, sh, dt: st4.enter_context(nc.sbuf_tensor(nm, sh, dt))
    hts_h = sb4("hts_h", [128, 4, 512], BF16)
    hts_l = sb4("hts_l", [128, 4, 512], BF16)
    wps = sb4("wps", [128, 4, 1024], U16)
    o32s = sb4("o32s", [128, 1024], F32)
    r1s = sb4("r1s", [128, 1024], F32)
    ots = sb4("ots", [128, 1024], F32)
    htok = ("st_h", n_h_stores)
    wptok = ("st_wq", n_wp_stores)
    r1tok = ("st_r1", n_r1_stores)
    t_out_dma = None
    t_hfree = [None, None, None, None]
    t_r1s_free = None
    for tp in range(NTQ):
        for eh in range(2):
            gtoks = [None] * NS
            for hc in range(NHC):
                par = hc % 4
                ta_ = p.dma("sync", "d_hh%d" % par, hts_h[:, par, :],
                      hhi[hc * 128:(hc + 1) * 128, tp * 512:(tp + 1) * 512],
                      waits=[t_hfree[par], htok])
                tb_ = p.dma("gpsimd", "d_hl%d" % par, hts_l[:, par, :],
                      hlo[hc * 128:(hc + 1) * 128, tp * 512:(tp + 1) * 512],
                      waits=[t_hfree[par], htok])
                tl3 = p.dma("sync", "d_hw%d" % par, wps[:, par, :],
                            wph[hc * 128:(hc + 1) * 128, eh * 1024:(eh + 1) * 1024],
                            waits=[t_hfree[par], wptok])
                for ts in range(NS):
                    def rmm(e, ts=ts, par=par, hc=hc):
                        last = None
                        for half in range(2):
                            o = PS[:, ts * 2 + half, :]
                            wslab = wps[:, par, half * 512:(half + 1) * 512].bitcast(BF16)
                            last = e.matmul(o, hts_h[:, par, ts * 128:(ts + 1) * 128], wslab,
                                            start=(hc == 0), stop=False)
                            last = e.matmul(o, hts_l[:, par, ts * 128:(ts + 1) * 128], wslab,
                                            start=False, stop=(hc == NHC - 1))
                        return last
                    gtoks[ts] = p.step("tensor", rmm,
                                       waits=[ta_, tb_, tl3] + ([bfree[ts * 2], bfree[ts * 2 + 1]]
                                                                if hc == 0 else []), serial=False)
                t_hfree[par] = gtoks[NS - 1]
            for ts in range(NS):
                rows = tp * 512 + ts * 128
                e1 = p.step("vector", lambda e, ts=ts, tp=tp: e.tensor_scalar_mul(
                    o32s[:], PS[:, ts * 2:ts * 2 + 2, :], invr[:, tp, ts:ts + 1]),
                    waits=[gtoks[ts]])
                bfree[ts * 2] = e1
                bfree[ts * 2 + 1] = e1
                tlr = p.dma("sync", "d_r1s", r1s[:], r1d[rows:rows + 128, eh * 1024:(eh + 1) * 1024],
                            waits=[t_r1s_free, r1tok])
                e2 = p.step("vector", lambda e: e.tensor_tensor(ots[:], o32s[:], r1s[:], ALU.add),
                            waits=[e1, tlr, t_out_dma])
                t_r1s_free = e2
                t_out_dma = p.dma("gpsimd", "st_out", out[rows:rows + 128, eh * 1024:(eh + 1) * 1024],
                                  ots[:], waits=[e2])
    p.step("gpsimd", lambda e: None, waits=[t_out_dma], serial=False)
    st4.close()
    ffn_st.close()

    p.emit_all()
    st.close()
    return nc


def _host_consts():
    import ml_dtypes
    BF = ml_dtypes.bfloat16
    sig = np.arange(C)[:, None]
    ta = np.arange(C)[None, :]
    expnt = np.where(sig > ta, (sig - ta - 1).astype(np.float32),
                     np.float32(1e30)).astype(np.float32)
    expn_in = np.ascontiguousarray(expnt.reshape(4, 128, C))
    tauv_in = np.ascontiguousarray((C - 1 - np.arange(C, dtype=np.float32)).reshape(4, 128, 1))
    sigv_in = np.ascontiguousarray(np.arange(C, dtype=np.float32).reshape(4, 128, 1))
    id32_in = np.eye(128, dtype=np.float32)
    seg_in = np.zeros((128, 2), dtype=np.float32)
    seg_in[0:64, 0] = 1.0 / 64
    seg_in[64:128, 1] = 1.0 / 64
    segt_in = np.zeros((2, 128), dtype=np.float32)
    segt_in[0, 0:64] = 1.0
    segt_in[1, 64:128] = 1.0
    return dict(expn=expn_in, tauv=tauv_in, sigv=sigv_in, id32=id32_in,
                idbf=id32_in.astype(BF), seg=seg_in.astype(BF),
                segtb=segt_in.astype(BF), segtf=segt_in,
                onesr=np.ones((1, 128), dtype=np.float32),
                ones1f=np.ones((128, 1), dtype=np.float32),
                ones1b=np.ones((128, 1), dtype=np.float32).astype(BF))


def kernel(L0_attn_x, L0_attn_Wq, L0_attn_Wk, L0_attn_Wv, L0_attn_Wo, L0_attn_Wg,
           L0_attn_decay_logits, L0_attn_out_scale, L0_ffn_w_gate_up, L0_ffn_w_proj):
    xfull = np.ascontiguousarray(np.asarray(L0_attn_x, dtype=np.float32))
    B, T, _ = xfull.shape
    half = T // 2
    wq_in = np.ascontiguousarray(np.asarray(L0_attn_Wq, np.float32))
    wk_in = np.ascontiguousarray(np.asarray(L0_attn_Wk, np.float32))
    wv_in = np.ascontiguousarray(np.asarray(L0_attn_Wv, np.float32))
    wo_in = np.ascontiguousarray(np.asarray(L0_attn_Wo, np.float32))
    wg_in = np.zeros((DM, 8), dtype=np.float32)
    wg_in[:, 0:3] = np.asarray(L0_attn_Wg, np.float32)
    wgun_in = np.ascontiguousarray(np.asarray(L0_ffn_w_gate_up, np.float32))
    wpn_in = np.ascontiguousarray(np.asarray(L0_ffn_w_proj, np.float32))
    dl = np.asarray(L0_attn_decay_logits, np.float32)
    osc = np.float32(L0_attn_out_scale)
    consts = _host_consts()

    nc = build_nc(half, half, wpn_in.shape[1])
    in_maps = []
    for c in range(8):
        b, hf = c // 2, c % 2
        t0 = hf * half
        xc = np.ascontiguousarray(xfull[b, t0:t0 + half])
        xtc = np.ascontiguousarray(xfull[b, t0:t0 + half].T)
        if hf == 0:
            xoc = np.ascontiguousarray(xfull[b, half:])
            xotc = np.ascontiguousarray(xfull[b, half:].T)
            smask = 1.0
        else:
            xoc, xotc, smask = xc, xtc, 0.0
        cstv = np.zeros((1, 16), dtype=np.float32)
        cstv[0, 0:3] = dl
        cstv[0, 3] = osc
        cstv[0, 4] = smask
        cstv[0, 5] = float(C)
        m = dict(x=xc, xt=xtc, xo=xoc, xot=xotc, wq=wq_in, wk=wk_in, wv=wv_in,
                 wo=wo_in, wg=wg_in, cst=cstv, wgun=wgun_in, wpn=wpn_in, **consts)
        in_maps.append(m)
    res = run_bass_kernel_spmd(nc, in_maps, core_ids=list(range(8)))
    outv = np.empty_like(xfull)
    for c in range(8):
        b, hf = c // 2, c % 2
        outv[b, hf * half:(hf + 1) * half] = res.results[c]["out"]
    return outv


# version 24
# speedup vs baseline: 1.8357x; 1.0180x over previous
import numpy as np
import concourse.bass as bass
import concourse.mybir as mybir
from concourse.bass_utils import run_bass_kernel_spmd
from contextlib import ExitStack

F32 = mybir.dt.float32
BF16 = mybir.dt.bfloat16
U16 = mybir.dt.uint16
I32 = mybir.dt.int32
AF = mybir.ActivationFunctionType
ALU = mybir.AluOpType
AX = mybir.AxisListType

C = 512
D = 512
DM = 2048
NS = 4
ND = 16
NE = 4
EPS_LN = 1e-5
EPS_RMS = 1.1920929e-7


class PB:
    ENGS = ("sync", "gpsimd", "scalar", "vector", "tensor")

    def __init__(self, nc, stack):
        self.nc = nc
        self.stack = stack
        self.q = {e: [] for e in self.ENGS}
        self.sems = {}
        self.ctr = {}
        self.fence = {e: [] for e in self.ENGS}
        for e in self.ENGS:
            self.sem("p_" + e)

    def sem(self, name):
        if name not in self.sems:
            self.sems[name] = self.stack.enter_context(self.nc.semaphore(name))
            self.ctr[name] = 0
        return self.sems[name]

    def step(self, eng, fn, waits=(), amt=1, semname=None, serial=None):
        sn = semname or ("p_" + eng)
        self.sem(sn)
        wl = {}
        if serial is None:
            serial = eng in ("vector", "scalar")
        if serial and self.ctr["p_" + eng] > 0:
            wl["p_" + eng] = self.ctr["p_" + eng]
        for w in list(waits) + self.fence[eng]:
            if w is None:
                continue
            s, v = w
            if v > wl.get(s, -1):
                wl[s] = v
        self.fence[eng] = []
        self.ctr[sn] += amt
        tok = (sn, self.ctr[sn])
        self.q[eng].append((sorted(wl.items()), fn, sn, amt))
        return tok

    def dma(self, eng, sem, out_ap, in_ap, waits=(), **kw):
        self.sem(sem)
        prev = self.ctr[sem]
        w = list(waits) + ([(sem, prev)] if prev > 0 else [])
        return self.step(eng, lambda e: e.dma_start(out_ap, in_ap, **kw),
                         waits=w, amt=16, semname=sem, serial=False)

    def barrier(self):
        toks = [self.step(e, lambda en: None,
                          waits=[("p_" + e, self.ctr["p_" + e])]) for e in self.ENGS]
        for e in self.ENGS:
            self.fence[e] = list(toks)

    def emit_all(self):
        nc = self.nc
        with nc.Block() as block:
            def mk(eng_name):
                def run(e):
                    for wl, fn, sn, amt in self.q[eng_name]:
                        for s, v in wl:
                            e.wait_ge(self.sems[s], v)
                        inst = fn(e)
                        if inst is not None:
                            inst.then_inc(self.sems[sn], amt)
                        else:
                            e.nop(nofuse=True).then_inc(self.sems[sn], amt)
                return run
            block.sync(mk("sync"))
            block.gpsimd(mk("gpsimd"))
            block.scalar(mk("scalar"))
            block.vector(mk("vector"))
            block.tensor(mk("tensor"))


def build_nc(n_own, n_oth, Hs):
    H2s = 2 * Hs
    nch = n_own // C
    noch = n_oth // C
    NHC = Hs // 128
    NTQ = n_own // C
    NGG = Hs // 256
    NPE = DM // 256

    nc = bass.Bass("TRN2", target_bir_lowering=False)
    din = lambda name, shape, dt=F32: nc.dram_tensor(name, shape, dt, kind="ExternalInput").ap()

    x = din("x", [n_own, DM]);        xt = din("xt", [DM, n_own])
    xo = din("xo", [n_oth, DM]);      xot = din("xot", [DM, n_oth])
    wq = din("wq", [DM, D]);          wk = din("wk", [3, DM, D])
    wv = din("wv", [3, DM, D]);       wo = din("wo", [D, DM])
    wg = din("wg", [DM, 8]);          cst = din("cst", [1, 16])
    wgun = din("wgun", [H2s, DM]);    wpn = din("wpn", [DM, Hs])
    expn = din("expn", [4, 128, C]);  tauv = din("tauv", [4, 128, 1])
    sigv = din("sigv", [4, 128, 1])
    id32 = din("id32", [128, 128]);   idbf = din("idbf", [128, 128], BF16)
    seg = din("seg", [128, 2], BF16); segtb = din("segtb", [2, 128], BF16)
    segtf = din("segtf", [2, 128]);   onesr = din("onesr", [1, 128])
    ones1f = din("ones1f", [128, 1]); ones1b = din("ones1b", [128, 1], BF16)

    out = nc.dram_tensor("out", [n_own, DM], F32, kind="ExternalOutput").ap()

    r1d = nc.dram_tensor("r1d", [n_own, DM], F32, kind="Internal").ap()
    hhi = nc.dram_tensor("hhi", [Hs, n_own], BF16, kind="Internal").ap()
    hlo = nc.dram_tensor("hlo", [Hs, n_own], BF16, kind="Internal").ap()
    wph = nc.dram_tensor("wph", [Hs, DM], U16, kind="Internal").ap()
    wgh_n = nc.dram_tensor("wgh_n", [H2s, DM], U16, kind="Internal").ap()
    s_w = {}
    for nm, shp in [("q", [DM, D]), ("k", [3, DM, D]), ("v", [3, DM, D]), ("o", [D, DM])]:
        s_w[nm + "h"] = nc.dram_tensor("s_w" + nm + "h", shp, BF16, kind="Internal").ap()
        s_w[nm + "l"] = nc.dram_tensor("s_w" + nm + "l", shp, BF16, kind="Internal").ap()

    st = ExitStack()
    p = PB(nc, st)
    sbt = lambda nm, sh, dt: st.enter_context(nc.sbuf_tensor(nm, sh, dt))

    PS = st.enter_context(nc.psum_tensor("PS", [128, 8, 512], F32))
    bfree = [None] * 8

    csts = sbt("csts", [1, 16], F32)
    srow = sbt("srow", [1, 16], F32)
    srob = sbt("srob", [1, 4], BF16)
    sbc = sbt("sbc", [128, 16], F32)
    ident = sbt("ident", [128, 128], F32)
    identb = sbt("identb", [128, 128], BF16)
    segs = sbt("segs", [128, 2], BF16)
    segtbs = sbt("segtbs", [2, 128], BF16)
    segtfs = sbt("segtfs", [2, 128], F32)
    onesrs = sbt("onesrs", [1, 128], F32)
    ones1fs = sbt("ones1fs", [128, 1], F32)
    ones1bs = sbt("ones1bs", [128, 1], BF16)
    invr = sbt("invr", [128, NTQ, 4], F32)
    ap_st = ExitStack()
    apt = lambda nm, sh, dt: ap_st.enter_context(nc.sbuf_tensor(nm, sh, dt))
    dgh = apt("dgh", [128, 3, 128], BF16)
    dgl = apt("dgl", [128, 3, 128], BF16)
    cvec = apt("cvec", [128, 4, 4], F32)
    evecs = apt("evecs", [128, 2, 4, 4], F32)
    msk = apt("msk", [128, 3, 4, 512], BF16)
    tauvs = apt("tauvs", [128, 4, 1], F32)
    sigvs = apt("sigvs", [128, 4, 1], F32)
    csr_q = apt("csr_q", [1, 512], BF16)
    csr_k = apt("csr_k", [1, 3, 512], BF16)
    csr_v = apt("csr_v", [1, 3, 512], BF16)
    csr_g = apt("csr_g", [1, 8], BF16)
    wgh = apt("wgh", [128, ND, 8], BF16)
    wgl = apt("wgl", [128, ND, 8], BF16)
    Sh = apt("Sh", [128, 3, 4, 512], BF16)
    Sl = apt("Sl", [128, 3, 4, 512], BF16)

    st0a = ExitStack()
    expns = st0a.enter_context(nc.sbuf_tensor("expns", [128, 4, 512], F32))
    t_cst = p.dma("sync", "d_cst", csts[:], cst[:, :])
    for dst_, src_ in [(ident, id32), (identb, idbf), (segs, seg), (segtbs, segtb),
                       (segtfs, segtf), (onesrs, onesr), (ones1fs, ones1f), (ones1bs, ones1b)]:
        p.dma("sync", "d_cst", dst_[:], src_[:, :])
    p.dma("sync", "d_cst", tauvs[:], tauv.rearrange("s p o -> p s o"), allow_slow_non_contiguous=True)
    p.dma("sync", "d_cst", sigvs[:], sigv.rearrange("s p o -> p s o"), allow_slow_non_contiguous=True)
    t_ldall = p.dma("sync", "d_cst", expns[:], expn.rearrange("s p t -> p s t"))

    t0m = p.step("vector", lambda e: e.memset(srow[:], 0.0))
    t1 = p.step("scalar", lambda e: e.activation(srow[:, 14:15], csts[:, 0:1], AF.Sigmoid), waits=[t_ldall, t0m])
    t1 = p.step("scalar", lambda e: e.activation(srow[:, 0:1], srow[:, 14:15], AF.Ln), waits=[t1])
    t1 = p.step("scalar", lambda e: e.activation(srow[:, 14:15], csts[:, 1:2], AF.Sigmoid), waits=[t1])
    t1 = p.step("scalar", lambda e: e.activation(srow[:, 1:2], srow[:, 14:15], AF.Ln), waits=[t1])
    t1 = p.step("scalar", lambda e: e.activation(srow[:, 14:15], csts[:, 2:3], AF.Sigmoid), waits=[t1])
    t1 = p.step("scalar", lambda e: e.activation(srow[:, 2:3], srow[:, 14:15], AF.Ln), waits=[t1])
    tds = []
    for i in range(3):
        tds.append(p.step("scalar", lambda e, i=i: e.activation(
            srow[:, 3 + i:4 + i], csts[:, 5:6], AF.Exp, scale=srow[:, i:i + 1]), waits=[t1]))
    t2 = p.step("vector", lambda e: e.tensor_copy(srob[:, 0:3], srow[:, 3:6]), waits=tds)
    t2 = p.step("vector", lambda e: e.tensor_copy(srow[:, 6:9], srob[:, 0:3]), waits=[t2])
    t2 = p.step("vector", lambda e: e.tensor_tensor(srow[:, 9:12], srow[:, 3:6], srow[:, 6:9],
                                                    ALU.subtract), waits=[t2])
    t2 = p.step("vector", lambda e: e.tensor_copy(srow[:, 12:14], csts[:, 3:5]), waits=[t2])
    t3 = p.step("tensor", lambda e: e.matmul(PS[:, 7, 0:16], onesrs[:], srow[:, 0:16],
                                             start=True, stop=True), waits=[t2], serial=False)
    t_sbc = p.step("vector", lambda e: e.tensor_copy(sbc[:], PS[:, 7, 0:16]), waits=[t3])
    bfree[7] = t_sbc
    for i in range(3):
        p.step("vector", lambda e, i=i: e.tensor_scalar_mul(dgh[:, i, :], identb[:],
                                                            sbc[:, 6 + i:7 + i]), waits=[t_sbc])
        p.step("vector", lambda e, i=i: e.tensor_scalar_mul(dgl[:, i, :], identb[:],
                                                            sbc[:, 9 + i:10 + i]), waits=[t_sbc])
    for i in range(3):
        for s in range(4):
            p.step("scalar", lambda e, i=i, s=s: e.activation(
                cvec[:, s, i:i + 1], tauvs[:, s, :], AF.Exp, scale=sbc[:, i:i + 1]), waits=[t_sbc])
            te = p.step("scalar", lambda e, i=i, s=s: e.activation(
                evecs[:, 0, s, i:i + 1], sigvs[:, s, :], AF.Exp, scale=sbc[:, i:i + 1]), waits=[t_sbc])
            p.step("vector", lambda e, i=i, s=s: e.tensor_scalar_mul(
                evecs[:, 1, s, i:i + 1], evecs[:, 0, s, i:i + 1], sbc[:, 13:14]), waits=[te])
    for i in range(3):
        for s in range(4):
            p.step("scalar", lambda e, i=i, s=s: e.activation(
                msk[:, i, s, :], expns[:, s, :], AF.Exp, scale=sbc[:, i:i + 1]), waits=[t_sbc])

    p.barrier()
    st0a.close()

    with ExitStack() as st0:
        wst = st0.enter_context(nc.sbuf_tensor("wst", [128, 2, 512], F32))
        whst = st0.enter_context(nc.sbuf_tensor("whst", [128, 2, 512], BF16))
        wlst = st0.enter_context(nc.sbuf_tensor("wlst", [128, 2, 512], BF16))
        wcons = [None, None]

        def prep_weight(src_ap, dsth, dstl, ntile, csum_to, tile_sl):
            tcs = None
            first = [True]
            for t in range(ntile):
                par = t % 2
                tl = p.dma("sync", "d_w%d" % par, wst[:, par, :], tile_sl(src_ap, t),
                           waits=wcons[par] or [])
                th = p.step("vector", lambda e, par=par: e.tensor_copy(whst[:, par, :], wst[:, par, :]),
                            waits=[tl])
                tlo = p.step("vector", lambda e, par=par: e.tensor_tensor(
                    wlst[:, par, :], wst[:, par, :], whst[:, par, :], ALU.subtract), waits=[th])
                if csum_to is not None:
                    def csmm(e, par=par, t=t, f=first[0]):
                        e.matmul(PS[0:1, 6, :], ones1bs[:], whst[:, par, :], start=f, stop=False)
                        return e.matmul(PS[0:1, 6, :], ones1bs[:], wlst[:, par, :],
                                        start=False, stop=(t == ntile - 1))
                    tcs = p.step("tensor", csmm, waits=[tlo, bfree[6] if first[0] else None],
                                 serial=False)
                    first[0] = False
                d1 = p.dma("gpsimd", "st_wei", tile_sl(dsth, t), whst[:, par, :], waits=[tlo])
                d2 = p.dma("gpsimd", "st_wei", tile_sl(dstl, t), wlst[:, par, :], waits=[tlo])
                wcons[par] = [d2, tcs] if csum_to is not None else [d2]
            if csum_to is not None:
                tv = p.step("vector", lambda e: e.tensor_copy(csum_to, PS[0:1, 6, :]), waits=[tcs])
                bfree[6] = tv

        rowtile = lambda ap_, t: ap_[t * 128:(t + 1) * 128, :]
        wo_tile = lambda ap_, t: ap_[(t // 4) * 128:(t // 4 + 1) * 128, (t % 4) * 512:(t % 4 + 1) * 512]
        prep_weight(wq, s_w["qh"], s_w["ql"], ND, csr_q[:, :], rowtile)
        for i in range(3):
            prep_weight(wk[i], s_w["kh"][i], s_w["kl"][i], ND, csr_k[:, i, :], rowtile)
            prep_weight(wv[i], s_w["vh"][i], s_w["vl"][i], ND, csr_v[:, i, :], rowtile)
        prep_weight(wo, s_w["oh"], s_w["ol"], 16, None, wo_tile)
        tcs = None
        for t in range(ND):
            par = t % 2
            tl = p.dma("sync", "d_w%d" % par, wst[:, par, 0:8], wg[t * 128:(t + 1) * 128, :],
                       waits=wcons[par] or [])
            th = p.step("vector", lambda e, par=par, t=t: e.tensor_copy(wgh[:, t, :], wst[:, par, 0:8]),
                        waits=[tl])
            tlo = p.step("vector", lambda e, par=par, t=t: e.tensor_tensor(
                wgl[:, t, :], wst[:, par, 0:8], wgh[:, t, :], ALU.subtract), waits=[th])
            def gcs(e, t=t):
                e.matmul(PS[0:1, 6, 0:8], ones1bs[:], wgh[:, t, :], start=(t == 0), stop=False)
                return e.matmul(PS[0:1, 6, 0:8], ones1bs[:], wgl[:, t, :],
                                start=False, stop=(t == ND - 1))
            tcs = p.step("tensor", gcs, waits=[tlo, bfree[6] if t == 0 else None], serial=False)
            wcons[par] = [tcs]
        tv = p.step("vector", lambda e: e.tensor_copy(csr_g[:], PS[0:1, 6, 0:8]), waits=[tcs])
        bfree[6] = tv
        n_wei_stores = p.ctr["st_wei"]

    p.barrier()

    st1 = ExitStack()
    sb1 = lambda nm, sh, dt: st1.enter_context(nc.sbuf_tensor(nm, sh, dt))
    xsb = sb1("xsb", [128, NS, DM], F32)
    xtst = sb1("xtst", [128, 2, 512], F32)
    xth = sb1("xth", [128, ND, 512], BF16)
    xtl = sb1("xtl", [128, ND, 512], BF16)
    sqs = sb1("sqs", [128, DM], F32)
    stf = sb1("stf", [128, NS, 8], F32)
    stb = sb1("stb", [128, NS, 1], BF16)
    nmrow = sb1("nmrow", [1, 512], BF16)
    einv2 = sb1("einv2", [128, NS, 4], F32)
    qh = sb1("qh", [128, NE, 512], BF16);  ql = sb1("ql", [128, NE, 512], BF16)
    kh = sb1("kh", [128, NE, 512], BF16);  kl = sb1("kl", [128, NE, 512], BF16)
    kth = sb1("kth", [128, NS, 512], BF16); ktl = sb1("ktl", [128, NS, 512], BF16)
    vth = sb1("vth", [128, NS, 512], BF16); vtl = sb1("vtl", [128, NS, 512], BF16)
    vsh = sb1("vsh", [128, NS, 512], BF16); vsl = sb1("vsl", [128, NS, 512], BF16)
    sch = sb1("sch", [128, NS, 512], BF16); scl_ = sb1("scl_", [128, NS, 512], BF16)
    scm32 = sb1("scm32", [128, 512], F32)
    rc32 = sb1("rc32", [128, 512], F32)
    ra32 = sb1("ra32", [128, 512], F32)
    vt32 = sb1("vt32", [128, 512], F32)
    blsb = sb1("blsb", [128, NS, 512], F32)
    blth = sb1("blth", [128, NE, 512], BF16); bltl = sb1("bltl", [128, NE, 512], BF16)
    gl3 = sb1("gl3", [128, NS, 8], F32)
    giv = sb1("giv", [128, NS, 4], F32)
    wsl_h = sb1("wsl_h", [128, 4, 512], BF16)
    wsl_l = sb1("wsl_l", [128, 4, 512], BF16)

    tok = {
        "xsb_free": [None] * NS,
        "xcast": [None] * ND,
        "xth_cons": None,
        "S_ready": [None] * 3,
        "S_read": [None] * 3,
        "wslab": [("p_tensor", 0)] * 4,
    }
    wei_tok = ("st_wei", n_wei_stores)

    def load_wslab(src_h, src_l, r0, c0, c1, idx):
        par = idx % 4
        p.dma("sync", "d_wsl%d" % par, wsl_h[:, par, 0:c1 - c0], src_h[r0:r0 + 128, c0:c1],
              waits=[tok["wslab"][par], wei_tok])
        t2 = p.dma("sync", "d_wsl%d" % par, wsl_l[:, par, 0:c1 - c0], src_l[r0:r0 + 128, c0:c1])
        return t2, par

    def proj_dloop(src_h, src_l, fm, t_xt, wcnt, b0=0):
        groups = [None] * 4
        for d in range(ND):
            tw, par = load_wslab(src_h, src_l, d * 128, 0, 512, wcnt[0])
            wcnt[0] += 1
            for g in range(4):
                first = (d == 0)
                def mm(e, d=d, g=g, par=par, first=first, b0=b0):
                    o = PS[:, b0 + g, :]
                    if fm:
                        w_h = wsl_h[:, par, g * 128:(g + 1) * 128]
                        w_l = wsl_l[:, par, g * 128:(g + 1) * 128]
                        e.matmul(o, w_h, xth[:, d, :], start=first, stop=False)
                        e.matmul(o, w_l, xth[:, d, :], start=False, stop=False)
                        return e.matmul(o, w_h, xtl[:, d, :], start=False, stop=False)
                    else:
                        xh_t = xth[:, d, g * 128:(g + 1) * 128]
                        xl_t = xtl[:, d, g * 128:(g + 1) * 128]
                        e.matmul(o, xh_t, wsl_h[:, par, :], start=first, stop=False)
                        e.matmul(o, xl_t, wsl_h[:, par, :], start=False, stop=False)
                        return e.matmul(o, xh_t, wsl_l[:, par, :], start=False, stop=False)
                groups[g] = p.step("tensor", mm,
                                   waits=[tw, tok["xcast"][d],
                                          bfree[b0 + g] if d == 0 else None], serial=False)
            tok["wslab"][par] = groups[3]
        return groups

    def proj_finish(groups, dst_h, dst_l, csr_ap, fm, t_nm, t_stats, extra=None, b0=0):
        for g in range(4):
            def r1mm(e, g=g, b0=b0):
                o = PS[:, b0 + g, :]
                if fm:
                    return e.matmul(o, csr_ap[:, g * 128:(g + 1) * 128], nmrow[:],
                                    start=False, stop=True)
                else:
                    return e.matmul(o, nmrow[:, g * 128:(g + 1) * 128], csr_ap,
                                    start=False, stop=True)
            groups[g] = p.step("tensor", r1mm, waits=[groups[g], t_nm], serial=False)
        ev = [None] * 4
        for g in range(4):
            pg = PS[:, b0 + g, :]
            if extra is None:
                th_ = p.step("scalar", lambda e, g=g, pg=pg: e.activation(dst_h[:, g, :], pg,
                                                                          AF.Copy), waits=[groups[g]])
                tl_ = p.step("vector", lambda e, g=g, pg=pg: e.tensor_tensor(
                    dst_l[:, g, :], pg, dst_h[:, g, :], ALU.subtract), waits=[th_])
            else:
                sc1, (edh, edl), sc2 = extra
                t0_ = p.step("vector", lambda e, g=g, pg=pg: e.tensor_scalar_mul(
                    vt32[:], pg, sc1(g)), waits=[groups[g], t_stats])
                th_ = p.step("scalar", lambda e, g=g: e.activation(dst_h[:, g, :], vt32[:],
                                                                   AF.Copy), waits=[t0_])
                tl_ = p.step("vector", lambda e, g=g: e.tensor_tensor(
                    dst_l[:, g, :], vt32[:], dst_h[:, g, :], ALU.subtract), waits=[th_])
                t0b = p.step("vector", lambda e, g=g, pg=pg: e.tensor_scalar_mul(
                    rc32[:], pg, sc2(g)), waits=[groups[g], t_stats])
                th2 = p.step("scalar", lambda e, g=g: e.activation(edh[:, g, :], rc32[:],
                                                                   AF.Copy), waits=[t0b])
                tl_ = p.step("vector", lambda e, g=g: e.tensor_tensor(
                    edl[:, g, :], rc32[:], edh[:, g, :], ALU.subtract), waits=[th2, tl_])
            ev[g] = tl_
            bfree[b0 + g] = tl_
        return ev

    def proj_pass(dst_h, dst_l, src_h, src_l, csr_ap, fm, t_nm, t_xt, t_stats,
                  wcnt, extra=None, b0=0):
        groups = proj_dloop(src_h, src_l, fm, t_xt, wcnt, b0=b0)
        return proj_finish(groups, dst_h, dst_l, csr_ap, fm, t_nm, t_stats, extra=extra, b0=b0)

    wcnt = [0]

    def do_chunk(xsrc, xtsrc, base, is_comp, cx):
        t_xs = [None] * NS
        for ts in range(NS):
            t_xs[ts] = p.dma("sync", "d_xsb", xsb[:, ts, :],
                             xsrc[base + ts * 128:base + (ts + 1) * 128, :],
                             waits=[tok["xsb_free"][ts]])
        t_x = t_xs[NS - 1]
        for d in range(ND):
            par = d % 2
            tl = p.dma("sync", "d_xt%d" % par, xtst[:, par, :], xtsrc[d * 128:(d + 1) * 128, base:base + C],
                       waits=[tok["xcast"][d - 2] if d >= 2 else tok["xth_cons"]])
            th = p.step("vector", lambda e, d=d, par=par: e.tensor_copy(xth[:, d, :], xtst[:, par, :]),
                        waits=[tl, tok["xth_cons"] if d < 2 else None])
            tok["xcast"][d] = p.step("vector", lambda e, d=d, par=par: e.tensor_tensor(
                xtl[:, d, :], xtst[:, par, :], xth[:, d, :], ALU.subtract), waits=[th])
        t_xt = tok["xcast"][ND - 1]
        if not is_comp:
            q_groups = proj_dloop(s_w["qh"], s_w["ql"], True, t_xt, wcnt, b0=0)
            k0_groups = proj_dloop(s_w["kh"][0], s_w["kl"][0], True, t_xt, wcnt, b0=4)
        else:
            k0_groups = proj_dloop(s_w["kh"][0], s_w["kl"][0], True, t_xt, wcnt, b0=4)
            v0_groups = proj_dloop(s_w["vh"][0], s_w["vl"][0], False, t_xt, wcnt, b0=0)
        t_stats = None
        t_nm = None
        for s in range(NS):
            tq_ = p.step("scalar", lambda e, s=s: e.activation(sqs[:], xsb[:, s, :], AF.Square),
                         waits=[t_xs[s], t_stats])
            ta = p.step("vector", lambda e, s=s: e.reduce_sum(
                stf[:, s, 0:1], xsb[:, s, :], AX.X, negate=True), waits=[t_xs[s]])
            tb = p.step("vector", lambda e, s=s: e.reduce_sum(stf[:, s, 6:7], sqs[:], AX.X),
                        waits=[tq_])
            tc = p.step("vector", lambda e, s=s: e.tensor_scalar_mul(
                stf[:, s, 0:1], stf[:, s, 0:1], 1.0 / DM), waits=[ta])
            td = p.step("vector", lambda e, s=s: e.tensor_scalar(
                stf[:, s, 1:2], stf[:, s, 6:7], 1.0 / DM, EPS_LN, ALU.mult, ALU.add), waits=[tb])
            te = p.step("vector", lambda e, s=s: e.tensor_scalar_mul(
                stf[:, s, 2:3], stf[:, s, 0:1], stf[:, s, 0:1]), waits=[tc])
            tf = p.step("vector", lambda e, s=s: e.scalar_tensor_tensor(
                stf[:, s, 3:4], stf[:, s, 2:3], -1.0, stf[:, s, 1:2], ALU.mult, ALU.add),
                waits=[td, te])
            tg = p.step("vector", lambda e, s=s: e.reciprocal(stf[:, s, 4:5], stf[:, s, 3:4]),
                        waits=[tf])
            tn = p.step("scalar", lambda e, s=s: e.activation(
                stf[:, s, 4:5], stf[:, s, 4:5], AF.Sqrt), waits=[tg])
            for _ in range(2):
                n1 = p.step("scalar", lambda e, s=s: e.activation(
                    stf[:, s, 6:7], stf[:, s, 4:5], AF.Square), waits=[tn])
                n2 = p.step("vector", lambda e, s=s: e.tensor_tensor(
                    stf[:, s, 6:7], stf[:, s, 6:7], stf[:, s, 3:4], ALU.mult), waits=[n1])
                n3 = p.step("vector", lambda e, s=s: e.tensor_scalar(
                    stf[:, s, 6:7], stf[:, s, 6:7], -0.5, 1.5, ALU.mult, ALU.add), waits=[n2])
                tn = p.step("vector", lambda e, s=s: e.tensor_tensor(
                    stf[:, s, 4:5], stf[:, s, 4:5], stf[:, s, 6:7], ALU.mult), waits=[n3])
            ti = p.step("vector", lambda e, s=s: e.tensor_scalar_mul(
                stf[:, s, 5:6], stf[:, s, 4:5], stf[:, s, 4:5]), waits=[tn])
            for i in range(3):
                ti = p.step("vector", lambda e, s=s, i=i, ic=1 if is_comp else 0: e.tensor_tensor(
                    einv2[:, s, i:i + 1], evecs[:, ic, s, i:i + 1], stf[:, s, 5:6], ALU.mult),
                    waits=[ti])
            tj = p.step("vector", lambda e, s=s: e.tensor_copy(
                stb[:, s, 0:1], stf[:, s, 0:1]), waits=[tc])
            t_nm = p.dma("sync", "d_nm", nmrow[0:1, s * 128:(s + 1) * 128],
                         stb[:, s, 0:1], waits=[tj], allow_slow_non_contiguous=True)
            t_stats = ti
        t_qev = None
        if not is_comp:
            t_qev = proj_finish(q_groups, qh, ql, csr_q, True, t_nm, t_stats, b0=0)
            t_k0ev = proj_finish(k0_groups, kh, kl, csr_k[:, 0, :], True, t_nm, t_stats, b0=4)
        else:
            t_k0ev = proj_finish(k0_groups, kh, kl, csr_k[:, 0, :], True, t_nm, t_stats, b0=4)
        t_gate = None
        if not is_comp:
            for s in range(NS):
                def gmm(e, s=s):
                    o = PS[:, 7, 0:8]
                    for d in range(ND):
                        e.matmul(o, xth[:, d, s * 128:(s + 1) * 128], wgh[:, d, :],
                                 start=(d == 0), stop=False)
                        e.matmul(o, xtl[:, d, s * 128:(s + 1) * 128], wgh[:, d, :],
                                 start=False, stop=False)
                        e.matmul(o, xth[:, d, s * 128:(s + 1) * 128], wgl[:, d, :],
                                 start=False, stop=False)
                    return e.matmul(o, nmrow[:, s * 128:(s + 1) * 128], csr_g[:],
                                    start=False, stop=True)
                tg1 = p.step("tensor", gmm, waits=[t_xt, t_nm, bfree[7]], serial=False)
                tg2 = p.step("vector", lambda e, s=s: e.tensor_scalar_mul(
                    gl3[:, s, 0:3], PS[:, 7, 0:3], stf[:, s, 4:5]), waits=[tg1, t_stats])
                bfree[7] = tg2
                tg3 = p.step("vector", lambda e, s=s: e.reduce_max(
                    gl3[:, s, 4:5], gl3[:, s, 0:3], AX.X, negate=True), waits=[tg2])
                tg4 = p.step("scalar", lambda e, s=s: e.activation(
                    gl3[:, s, 0:3], gl3[:, s, 0:3], AF.Exp, bias=gl3[:, s, 4:5]), waits=[tg3])
                tg5 = p.step("vector", lambda e, s=s: e.reduce_sum(
                    gl3[:, s, 4:5], gl3[:, s, 0:3], AX.X), waits=[tg4])
                tg6 = p.step("vector", lambda e, s=s: e.reciprocal(
                    gl3[:, s, 5:6], gl3[:, s, 4:5]), waits=[tg5])
                tg7 = p.step("vector", lambda e, s=s: e.tensor_scalar_mul(
                    gl3[:, s, 0:3], gl3[:, s, 0:3], gl3[:, s, 5:6]), waits=[tg6])
                t_gate = p.step("vector", lambda e, s=s: e.tensor_scalar_mul(
                    giv[:, s, 0:3], gl3[:, s, 0:3], stf[:, s, 4:5]), waits=[tg7])
        for i in range(3):
            if i == 0:
                t_kev = t_k0ev
            else:
                t_kev = proj_pass(kh, kl, s_w["kh"][i], s_w["kl"][i], csr_k[:, i, :], True,
                                  t_nm, t_xt, t_stats, wcnt)
            t_ktev = [None] * NS
            tb4 = [bfree[4], bfree[5], bfree[6], bfree[7]]
            for hl, (src, dst) in enumerate(((kh, kth), (kl, ktl))):
                for ec in range(NE):
                    tts = []
                    for ss in range(4):
                        bank = 4 + ss
                        tts.append(p.step("tensor", lambda e, ec=ec, ss=ss, bank=bank, src=src:
                                          e.matmul(PS[:, bank, 0:128],
                                                   src[:, ec, ss * 128:(ss + 1) * 128], identb[:],
                                                   start=True, stop=True),
                                          waits=[t_kev[ec], tb4[ss]], serial=False))
                    tev = p.step("scalar", lambda e, ec=ec, dst=dst: e.activation(
                        dst[:, 0:4, ec * 128:(ec + 1) * 128].rearrange("p s j -> p (s j)")
                        if False else dst[:, 0:4, ec * 128:(ec + 1) * 128],
                        PS[:, 4:8, 0:128], AF.Copy), waits=tts)
                    for ss in range(4):
                        tb4[ss] = tev
                        t_ktev[ss] = tev
            bfree[4], bfree[5], bfree[6], bfree[7] = tb4
            t_scev = [None] * NS
            if not is_comp:
                for ss in range(NS):
                    bank = 6 + (ss % 2)
                    def smm(e, ss=ss, bank=bank):
                        o = PS[:, bank, :]
                        last = None
                        for ec in range(NE):
                            last = e.matmul(o, kh[:, ec, ss * 128:(ss + 1) * 128], qh[:, ec, :],
                                            start=(ec == 0), stop=False)
                            last = e.matmul(o, kl[:, ec, ss * 128:(ss + 1) * 128], qh[:, ec, :],
                                            start=False, stop=False)
                            last = e.matmul(o, kh[:, ec, ss * 128:(ss + 1) * 128], ql[:, ec, :],
                                            start=False, stop=(ec == NE - 1))
                        return last
                    tsc = p.step("tensor", smm, waits=t_kev + t_qev + [bfree[bank]], serial=False)
                    e1 = p.step("vector", lambda e, ss=ss, bank=bank, i=i: e.scalar_tensor_tensor(
                        scm32[:], PS[:, bank, :], stf[:, ss, 4:5], msk[:, i, ss, :],
                        ALU.mult, ALU.mult), waits=[tsc, t_stats])
                    bfree[bank] = e1
                    e2 = p.step("scalar", lambda e, ss=ss: e.activation(sch[:, ss, :], scm32[:],
                                                                        AF.Copy), waits=[e1])
                    t_scev[ss] = p.step("vector", lambda e, ss=ss: e.tensor_tensor(
                        scl_[:, ss, :], scm32[:], sch[:, ss, :], ALU.subtract), waits=[e2])
            if is_comp and i == 0:
                t_vev = proj_finish(v0_groups, vth, vtl, csr_v[:, 0, :], False, t_nm, t_stats,
                                    extra=(lambda g: stf[:, g, 4:5], (vsh, vsl),
                                           lambda g, i=i: einv2[:, g, i:i + 1]), b0=0)
            else:
                t_vev = proj_pass(vth, vtl, s_w["vh"][i], s_w["vl"][i], csr_v[:, i, :], False,
                                  t_nm, t_xt, t_stats, wcnt,
                                  extra=(lambda g: stf[:, g, 4:5], (vsh, vsl),
                                         lambda g, i=i: einv2[:, g, i:i + 1]))
            t_wc = []
            if not is_comp:
                for ts in range(NS):
                    bw = 6 if ts % 2 == 0 else 7
                    bc = 4 if ts % 2 == 0 else 5
                    def wmm(e, ts=ts, bw=bw):
                        o = PS[:, bw, :]
                        last = None
                        for ss in range(NS):
                            last = e.matmul(o, sch[:, ss, ts * 128:(ts + 1) * 128], vth[:, ss, :],
                                            start=(ss == 0), stop=False)
                            last = e.matmul(o, scl_[:, ss, ts * 128:(ts + 1) * 128], vth[:, ss, :],
                                            start=False, stop=False)
                            last = e.matmul(o, sch[:, ss, ts * 128:(ts + 1) * 128], vtl[:, ss, :],
                                            start=False, stop=(ss == NS - 1))
                        return last
                    tw_ = p.step("tensor", wmm, waits=t_scev + t_vev + [bfree[bw]], serial=False)
                    def cmm(e, ts=ts, bc=bc, i=i):
                        o = PS[:, bc, :]
                        last = None
                        for ec in range(NE):
                            last = e.matmul(o, qh[:, ec, ts * 128:(ts + 1) * 128], Sh[:, i, ec, :],
                                            start=(ec == 0), stop=False)
                            last = e.matmul(o, ql[:, ec, ts * 128:(ts + 1) * 128], Sh[:, i, ec, :],
                                            start=False, stop=False)
                            last = e.matmul(o, qh[:, ec, ts * 128:(ts + 1) * 128], Sl[:, i, ec, :],
                                            start=False, stop=(ec == NE - 1))
                        return last
                    tc_ = p.step("tensor", cmm,
                                 waits=t_qev + [tok["S_ready"][i], bfree[bc]], serial=False)
                    e1 = p.step("vector", lambda e, ts=ts, bc=bc, i=i: e.tensor_scalar_mul(
                        rc32[:], PS[:, bc, :], cvec[:, ts, i:i + 1]), waits=[tc_])
                    bfree[bc] = e1
                    e2 = p.step("vector", lambda e, bw=bw: e.tensor_tensor(
                        ra32[:], PS[:, bw, :], rc32[:], ALU.add), waits=[e1, tw_])
                    bfree[bw] = e2
                    if i == 0:
                        e3 = p.step("vector", lambda e, ts=ts, i=i: e.tensor_scalar_mul(
                            blsb[:, ts, :], ra32[:], giv[:, ts, i:i + 1]), waits=[e2, t_gate])
                    else:
                        e2b = p.step("vector", lambda e, ts=ts, i=i: e.tensor_scalar_mul(
                            ra32[:], ra32[:], giv[:, ts, i:i + 1]), waits=[e2, t_gate])
                        e3 = p.step("vector", lambda e, ts=ts: e.tensor_tensor(
                            blsb[:, ts, :], blsb[:, ts, :], ra32[:], ALU.add), waits=[e2b])
                    t_wc.append(e3)
                    tok["S_read"][i] = tc_
            t_sup = []
            for ek in range(NE):
                def supmm(e, ek=ek, i=i, cx=cx):
                    o = PS[:, 4 + ek, :]
                    last = None
                    fs = True
                    if cx != 0:
                        last = e.matmul(o, dgh[:, i, :], Sh[:, i, ek, :], start=True, stop=False)
                        last = e.matmul(o, dgh[:, i, :], Sl[:, i, ek, :], start=False, stop=False)
                        last = e.matmul(o, dgl[:, i, :], Sh[:, i, ek, :], start=False, stop=False)
                        fs = False
                    for ss in range(NS):
                        last = e.matmul(o, kth[:, ss, ek * 128:(ek + 1) * 128], vsh[:, ss, :],
                                        start=fs, stop=False)
                        fs = False
                        last = e.matmul(o, ktl[:, ss, ek * 128:(ek + 1) * 128], vsh[:, ss, :],
                                        start=False, stop=False)
                        last = e.matmul(o, kth[:, ss, ek * 128:(ek + 1) * 128], vsl[:, ss, :],
                                        start=False, stop=(ss == NS - 1))
                    return last
                t_sup.append(p.step("tensor", supmm,
                                    waits=t_ktev + t_vev + [tok["S_ready"][i], bfree[4 + ek]],
                                    serial=False))
            t_sev = None
            for ek in range(NE):
                e1 = p.step("scalar", lambda e, ek=ek, i=i: e.activation(
                    Sh[:, i, ek, :], PS[:, 4 + ek, :], AF.Copy), waits=[t_sup[ek], tok["S_read"][i]])
                t_sev = p.step("vector", lambda e, ek=ek, i=i: e.tensor_tensor(
                    Sl[:, i, ek, :], PS[:, 4 + ek, :], Sh[:, i, ek, :], ALU.subtract), waits=[e1])
                bfree[4 + ek] = t_sev
            tok["S_ready"][i] = t_sev
        if not is_comp:
            t_blev = [None] * NE
            tpar = [bfree[6], bfree[7]]
            for idx in range(16):
                ts, ec = idx // 4, idx % 4
                bank = 6 + (idx % 2)
                tt = p.step("tensor", lambda e, ts=ts, ec=ec, bank=bank: e.transpose(
                    PS[:, bank, 0:128], blsb[:, ts, ec * 128:(ec + 1) * 128], ident[:]),
                    waits=[t_wc[ts], tpar[idx % 2]], serial=False)
                th_ = p.step("vector", lambda e, ts=ts, ec=ec, bank=bank: e.tensor_copy(
                    blth[:, ec, ts * 128:(ts + 1) * 128], PS[:, bank, 0:128]), waits=[tt])
                tv = p.step("vector", lambda e, ts=ts, ec=ec, bank=bank: e.tensor_tensor(
                    bltl[:, ec, ts * 128:(ts + 1) * 128], PS[:, bank, 0:128],
                    blth[:, ec, ts * 128:(ts + 1) * 128], ALU.subtract), waits=[th_])
                tpar[idx % 2] = tv
                t_blev[ec] = tv
            bfree[6], bfree[7] = tpar
            t_r1 = None
            for dq in range(4):
                gmm = [None] * NS
                for ec in range(NE):
                    tw, par = load_wslab(s_w["oh"], s_w["ol"], ec * 128, dq * 512, (dq + 1) * 512,
                                         wcnt[0])
                    wcnt[0] += 1
                    for ts in range(NS):
                        def omm(e, ts=ts, ec=ec, par=par):
                            o = PS[:, ts, :]
                            e.matmul(o, blth[:, ec, ts * 128:(ts + 1) * 128], wsl_h[:, par, :],
                                     start=(ec == 0), stop=False)
                            e.matmul(o, bltl[:, ec, ts * 128:(ts + 1) * 128], wsl_h[:, par, :],
                                     start=False, stop=False)
                            return e.matmul(o, blth[:, ec, ts * 128:(ts + 1) * 128],
                                            wsl_l[:, par, :], start=False, stop=(ec == NE - 1))
                        gmm[ts] = p.step("tensor", omm,
                                         waits=[tw, t_blev[ec]] +
                                               ([bfree[ts]] if ec == 0 else []), serial=False)
                    tok["wslab"][par] = gmm[NS - 1]
                for ts in range(NS):
                    e2 = p.step("vector", lambda e, ts=ts, dq=dq: e.scalar_tensor_tensor(
                        xsb[:, ts, dq * 512:(dq + 1) * 512], PS[:, ts, :], sbc[:, 12:13],
                        xsb[:, ts, dq * 512:(dq + 1) * 512], ALU.mult, ALU.add),
                        waits=[gmm[ts], t_xs[ts]])
                    bfree[ts] = e2
                    if dq == 3:
                        tok["xsb_free"][ts] = p.dma(
                            "gpsimd", "st_r1", r1d[base + ts * 128:base + (ts + 1) * 128, :],
                            xsb[:, ts, :], waits=[e2])
        else:
            for ts in range(NS):
                tok["xsb_free"][ts] = t_stats
        tok["xcast"] = [None] * ND
        tok["xth_cons"] = ("p_tensor", p.ctr["p_tensor"])

    cx = 0
    for j in range(noch - 1, -1, -1):
        do_chunk(xo, xot, j * C, True, cx)
        cx += 1
    for j in range(nch - 1, -1, -1):
        do_chunk(x, xt, j * C, False, cx)
        cx += 1
    n_r1_stores = p.ctr["st_r1"]
    st1.close()
    ap_st.close()

    p.barrier()

    ffn_st = ExitStack()
    sbf = lambda nm, sh, dt: ffn_st.enter_context(nc.sbuf_tensor(nm, sh, dt))
    bth = sbf("bth", [128, ND, n_own], BF16)
    btl = sbf("btl", [128, ND, n_own], BF16)
    ssqP = sbf("ssqP", [128, n_own], F32)
    t_ssq0 = p.step("vector", lambda e: e.memset(ssqP[:], 0.0))

    st2 = ExitStack()
    sb2 = lambda nm, sh, dt: st2.enter_context(nc.sbuf_tensor(nm, sh, dt))
    r1sb = sb2("r1sb", [128, 2, DM], F32)
    sq2 = sb2("sq2", [128, DM], F32)
    b32 = sq2
    st2f = sb2("st2f", [128, NS, 8], F32)
    t_r1free = [None, None]
    t_blast = None
    r1tok = ("st_r1", n_r1_stores)
    for gsub in range(NTQ * NS):
        tc2, s = gsub // NS, gsub % NS
        pr2 = gsub % 2
        t_x = p.dma("sync", "d_r1%d" % pr2, r1sb[:, pr2, :],
                    r1d[gsub * 128:(gsub + 1) * 128, :],
                    waits=[t_r1free[pr2], r1tok])
        if True:
            tq_ = p.step("scalar", lambda e, pr2=pr2: e.activation(sq2[:], r1sb[:, pr2, :], AF.Square),
                         waits=[t_x, t_blast])
            ta = p.step("vector", lambda e, s=s, pr2=pr2: e.reduce_sum(st2f[:, s, 0:1], r1sb[:, pr2, :], AX.X),
                        waits=[t_x])
            tb = p.step("vector", lambda e, s=s: e.reduce_sum(st2f[:, s, 6:7], sq2[:], AX.X),
                        waits=[tq_])
            tc1 = p.step("vector", lambda e, s=s: e.tensor_scalar_mul(
                st2f[:, s, 0:1], st2f[:, s, 0:1], 1.0 / DM), waits=[ta])
            td = p.step("vector", lambda e, s=s: e.tensor_scalar(
                st2f[:, s, 1:2], st2f[:, s, 6:7], 1.0 / DM, EPS_LN, ALU.mult, ALU.add), waits=[tb])
            te = p.step("vector", lambda e, s=s: e.tensor_scalar_mul(
                st2f[:, s, 2:3], st2f[:, s, 0:1], st2f[:, s, 0:1]), waits=[tc1])
            tf = p.step("vector", lambda e, s=s: e.scalar_tensor_tensor(
                st2f[:, s, 3:4], st2f[:, s, 2:3], -1.0, st2f[:, s, 1:2], ALU.mult, ALU.add),
                waits=[td, te])
            tg = p.step("vector", lambda e, s=s: e.reciprocal(st2f[:, s, 4:5], st2f[:, s, 3:4]),
                        waits=[tf])
            tn = p.step("scalar", lambda e, s=s: e.activation(
                st2f[:, s, 4:5], st2f[:, s, 4:5], AF.Sqrt), waits=[tg])
            for _ in range(2):
                n1 = p.step("scalar", lambda e, s=s: e.activation(
                    st2f[:, s, 6:7], st2f[:, s, 4:5], AF.Square), waits=[tn])
                n2 = p.step("vector", lambda e, s=s: e.tensor_tensor(
                    st2f[:, s, 6:7], st2f[:, s, 6:7], st2f[:, s, 3:4], ALU.mult), waits=[n1])
                n3 = p.step("vector", lambda e, s=s: e.tensor_scalar(
                    st2f[:, s, 6:7], st2f[:, s, 6:7], -0.5, 1.5, ALU.mult, ALU.add), waits=[n2])
                tn = p.step("vector", lambda e, s=s: e.tensor_tensor(
                    st2f[:, s, 4:5], st2f[:, s, 4:5], st2f[:, s, 6:7], ALU.mult), waits=[n3])
            tbb = p.step("vector", lambda e, s=s, pr2=pr2: e.tensor_scalar(
                b32[:], r1sb[:, pr2, :], st2f[:, s, 0:1], st2f[:, s, 4:5],
                ALU.subtract, ALU.mult), waits=[tn, t_x, tb])
            tlast = None
            col = gsub * 128
            for d0 in range(0, ND, 4):
                b0 = 4 if (d0 // 4) % 2 == 0 else 0
                tts = []
                for si in range(4):
                    dch = d0 + si
                    bank = b0 + si
                    tts.append(p.step("tensor", lambda e, dch=dch, bank=bank: e.transpose(
                        PS[:, bank, 0:128], b32[:, dch * 128:(dch + 1) * 128], ident[:]),
                        waits=[tbb, bfree[bank]], serial=False))
                th_ = p.step("scalar", lambda e, d0=d0, col=col, b0=b0: e.activation(
                    bth[:, d0:d0 + 4, col:col + 128], PS[:, b0:b0 + 4, 0:128], AF.Copy),
                    waits=tts)
                tv = p.step("vector", lambda e, d0=d0, col=col, b0=b0: e.tensor_tensor(
                    btl[:, d0:d0 + 4, col:col + 128], PS[:, b0:b0 + 4, 0:128],
                    bth[:, d0:d0 + 4, col:col + 128], ALU.subtract), waits=[th_])
                for si in range(4):
                    bfree[b0 + si] = tv
                tlast = tv
            t_r1free[pr2] = tlast
            t_blast = tlast
    st2.close()

    p.barrier()

    st3 = ExitStack()
    sb3 = lambda nm, sh, dt: st3.enter_context(nc.sbuf_tensor(nm, sh, dt))
    QF = 1024
    NG = QF // 64
    wstq = sb3("wstq", [128, 2, QF], F32)
    wbb = sb3("wbb", [128, 2, QF], BF16)
    redq = sb3("redq", [128, NG], F32)
    sbq = sb3("sbq", [128, NG], BF16)
    lnq = sb3("lnq", [128, NG], F32)
    rcq = sb3("rcq", [128, 2, NG], F32)
    rtq = sb3("rtq", [128, QF], BF16)
    g1q = sb3("g1q", [128, QF], BF16)
    nqq = sb3("nqq", [128, QF], BF16)
    whq = rtq
    d1q = g1q
    wqv = sb3("wqv", [128, QF], BF16)
    ustg = sb3("ustg", [128, 2, 8, 128], U16)
    wsg = sb3("wsg", [128, 2, ND, 128], BF16)
    wsu = sb3("wsu", [128, 2, ND, 128], BF16)
    uslab = sb3("uslab", [128, 2, 2, ND, 128], U16)
    silu_s = sb3("silu_s", [128, 1024], F32)
    h32 = sb3("h32", [128, 1024], F32)
    hsq = silu_s
    hhs = sb3("hhs", [128, 1024], BF16)
    hls = sb3("hls", [128, 1024], BF16)
    qs = {"par": 0, "wst_free": [[], []], "wbb_free": [None, None],
          "rcq_free": [None, None], "ustg_free": [None, None],
          "tbank": [None, None, None, None], "last_store": None}

    def quant_chain(load_src, dst_hbm_ap, transpose):
        par = qs["par"]
        qs["par"] ^= 1
        tl = p.dma("sync", "d_wq%d" % par, wstq[:, par, :], load_src,
                   waits=list(qs["wst_free"][par]))
        tcp = p.step("scalar", lambda e, par=par: e.activation(wbb[:, par, :], wstq[:, par, :],
                                                               AF.Copy),
                     waits=[tl, qs["wbb_free"][par]])
        qs["wst_free"][par] = [tcp]
        trd = p.step("vector", lambda e, par=par: e.reduce_sum(
            redq[:], wbb[:, par, :].rearrange("p (g k) -> p g k", k=64), AX.X,
            apply_absolute_value=True), waits=[tcp])
        tcl = p.step("vector", lambda e: e.tensor_scalar(
            sbq[:], redq[:], 1.0 / 64, 1e-8, ALU.mult, ALU.max), waits=[trd])
        tln = p.step("scalar", lambda e: e.activation(lnq[:], sbq[:], AF.Ln), waits=[tcl])
        tex = p.step("scalar", lambda e, par=par: e.activation(rcq[:, par, :], lnq[:], AF.Exp,
                                                               scale=-1.0),
                     waits=[tln, qs["rcq_free"][par]])
        rc_rep = bass.AP(rcq.ap().tensor, par * NG, [[2 * NG, 128], [1, NG], [0, 64]])
        sb_rep = bass.AP(sbq.ap().tensor, 0, [[NG, 128], [1, NG], [0, 64]])
        g3 = lambda t: t.rearrange("p (g k) -> p g k", k=64)
        trt = p.step("vector", lambda e, par=par: e.tensor_tensor(
            g3(rtq[:]), g3(wbb[:, par, :]), rc_rep, ALU.mult), waits=[tex])
        qs["rcq_free"][par] = trt
        tgt = p.step("vector", lambda e: e.tensor_scalar(g1q[:], rtq[:], 0.5, None, ALU.is_gt),
                     waits=[trt])
        tnq = p.step("vector", lambda e: e.scalar_tensor_tensor(
            nqq[:], rtq[:], -0.5, g1q[:], ALU.is_lt, ALU.subtract), waits=[tgt])
        twh = p.step("vector", lambda e: e.scalar_tensor_tensor(
            g3(whq[:]), g3(nqq[:]), -1.0, sb_rep, ALU.mult, ALU.mult), waits=[tnq])
        td1 = p.step("vector", lambda e, par=par: e.tensor_tensor(
            d1q[:], whq[:], wbb[:, par, :], ALU.subtract), waits=[twh])
        tqv = p.step("vector", lambda e, par=par: e.tensor_tensor(
            wqv[:], wbb[:, par, :], d1q[:], ALU.add), waits=[td1, qs["last_store"]])
        qs["wbb_free"][par] = tqv
        if not transpose:
            tst = p.dma("gpsimd", "st_wq", dst_hbm_ap, wqv[:].bitcast(U16), waits=[tqv])
        else:
            ev = []
            for half in range(2):
                tts = []
                for si in range(4):
                    sub = half * 4 + si
                    bank = 4 + si
                    tts.append(p.step("tensor", lambda e, sub=sub, bank=bank: e.matmul(
                        PS[:, bank, 0:128], wqv[:, sub * 128:(sub + 1) * 128], identb[:],
                        start=True, stop=True), waits=[tqv, qs["tbank"][si]], serial=False))
                tev = p.step("scalar", lambda e, half=half, par=par: e.activation(
                    ustg[:, par, half * 4:half * 4 + 4, :].bitcast(BF16), PS[:, 4:8, 0:128],
                    AF.Copy), waits=tts + [qs["ustg_free"][par]])
                for si in range(4):
                    qs["tbank"][si] = tev
                ev.append(tev)
            tst = p.dma("gpsimd", "st_wq", dst_hbm_ap, ustg[:, par, :, :], waits=ev)
            qs["ustg_free"][par] = tst
        qs["last_store"] = tst
        return tst

    t_hh_free = [None]
    t_ssq = [t_ssq0]
    t_uslab_free = [None, None]
    t_wsgu_free = [None, None]

    def emit_mm_item(jj, th2, wq_tok, tctx=[None]):
        par = jj % 2
        tcol = th2 * 1024
        if th2 == 0:
            l1 = p.dma("sync", "d_sl%d" % par, uslab[:, par, 0, :, :],
                       wgh_n[jj * 128:(jj + 1) * 128, :].rearrange("p (c n) -> p c n", n=128),
                       waits=[t_uslab_free[par], wq_tok])
            l2 = p.dma("sync", "d_sl%d" % par, uslab[:, par, 1, :, :],
                       wgh_n[Hs + jj * 128:Hs + (jj + 1) * 128, :]
                       .rearrange("p (c n) -> p c n", n=128))
            evs = []
            bi = 0
            for gu_i in range(2):
                dst = wsg if gu_i == 0 else wsu
                for d0 in range(0, ND, 4):
                    grp47 = (bi % 2 == 0)
                    b0 = 4 if grp47 else 0
                    tts = []
                    for si in range(4):
                        d = d0 + si
                        bank = b0 + si
                        tts.append(p.step("tensor", lambda e, gu_i=gu_i, d=d, bank=bank, par=par:
                                          e.matmul(PS[:, bank, 0:128],
                                                   uslab[:, par, gu_i, d, :].bitcast(BF16),
                                                   identb[:], start=True, stop=True),
                                          waits=[l2, qs["tbank"][si] if grp47 else bfree[si]],
                                          serial=False))
                    tev = p.step("scalar", lambda e, dst=dst, d0=d0, par=par, b0=b0: e.activation(
                        dst[:, par, d0:d0 + 4, :], PS[:, b0:b0 + 4, 0:128], AF.Copy),
                        waits=tts + [t_wsgu_free[par]])
                    for si in range(4):
                        if grp47:
                            qs["tbank"][si] = tev
                        else:
                            bfree[si] = tev
                    evs.append(tev)
                    bi += 1
            tctx[0] = evs
        evs = tctx[0]
        def gatemm(e, par=par, tcol=tcol):
            last = None
            for d in range(ND):
                lh = wsg[:, par, d, :]
                for half in range(2):
                    cs = tcol + half * 512
                    last = e.matmul(PS[:, half, :], lh, bth[:, d, cs:cs + 512],
                                    start=(d == 0), stop=False)
                    last = e.matmul(PS[:, half, :], lh, btl[:, d, cs:cs + 512],
                                    start=False, stop=(d == ND - 1))
            return last
        def upmm(e, par=par, tcol=tcol):
            last = None
            for d in range(ND):
                lh = wsu[:, par, d, :]
                for half in range(2):
                    cs = tcol + half * 512
                    last = e.matmul(PS[:, 2 + half, :], lh, bth[:, d, cs:cs + 512],
                                    start=(d == 0), stop=False)
                    last = e.matmul(PS[:, 2 + half, :], lh, btl[:, d, cs:cs + 512],
                                    start=False, stop=(d == ND - 1))
            return last
        tg_ = p.step("tensor", gatemm, waits=evs + [bfree[0], bfree[1]], serial=False)
        tu_ = p.step("tensor", upmm, waits=evs + [bfree[2], bfree[3]], serial=False)
        t_uslab_free[par] = tu_
        t_wsgu_free[par] = tu_
        return (jj, tcol, tg_, tu_)

    def emit_evac(ctx):
        jj, tcol, tg_, tu_ = ctx
        hrow = jj * 128
        e1 = p.step("scalar", lambda e: e.activation(silu_s[:], PS[:, 0:2, :], AF.Silu),
                    waits=[tg_, t_ssq[0]])
        e2 = p.step("vector", lambda e: e.tensor_tensor(h32[:], silu_s[:], PS[:, 2:4, :],
                                                        ALU.mult), waits=[e1, tu_])
        bfree[0] = e2; bfree[1] = e2; bfree[2] = e2; bfree[3] = e2
        e3 = p.step("scalar", lambda e: e.activation(hsq[:], h32[:], AF.Square), waits=[e2])
        t_ssq[0] = p.step("vector", lambda e, tcol=tcol: e.tensor_tensor(
            ssqP[:, tcol:tcol + 1024], ssqP[:, tcol:tcol + 1024], hsq[:], ALU.add),
            waits=[e3, t_ssq[0]])
        e5 = p.step("vector", lambda e: e.tensor_copy(hhs[:], h32[:]), waits=[t_hh_free[0]])
        e6 = p.step("vector", lambda e: e.tensor_tensor(hls[:], h32[:], hhs[:], ALU.subtract),
                    waits=[e5])
        p.dma("gpsimd", "st_h", hhi[hrow:hrow + 128, tcol:tcol + 1024], hhs[:], waits=[e6])
        t_hh_free[0] = p.dma("gpsimd", "st_h", hlo[hrow:hrow + 128, tcol:tcol + 1024],
                             hls[:], waits=[e6])
        return tu_

    NBP = Hs // 1024
    NJT = 1024 // 128
    NTH = max(1, n_own // 1024)
    NWPC = (DM // 128) * (Hs // 1024)
    wp_i = [0]
    wq_tok_of = {}
    pend = []
    for bp in range(NBP + 1):
        if bp < NBP:
            qthunks = []
            for jr in range(8 * bp, 8 * (bp + 1)):
                for dh in range(2):
                    qthunks.append((wgun[jr * 128:(jr + 1) * 128, dh * 1024:(dh + 1) * 1024],
                                    wgh_n[jr * 128:(jr + 1) * 128, dh * 1024:(dh + 1) * 1024],
                                    False))
                    qthunks.append((wgun[Hs + jr * 128:Hs + (jr + 1) * 128, dh * 1024:(dh + 1) * 1024],
                                    wgh_n[Hs + jr * 128:Hs + (jr + 1) * 128, dh * 1024:(dh + 1) * 1024],
                                    False))
            w0 = (NWPC * bp) // NBP
            w1 = (NWPC * (bp + 1)) // NBP
            for wi in range(w0, w1):
                er, h8 = wi // (Hs // 1024), wi % (Hs // 1024)
                qthunks.append((wpn[er * 128:(er + 1) * 128, h8 * 1024:(h8 + 1) * 1024],
                                wph[h8 * 1024:(h8 + 1) * 1024, er * 128:(er + 1) * 128]
                                .rearrange("(s p) j -> p s j", p=128),
                                True))
        else:
            qthunks = []
        items = [(jj, th2) for jj in range((bp - 1) * NJT, bp * NJT) for th2 in range(NTH)]             if bp > 0 else []
        nq, ni = len(qthunks), len(items)
        qi = ii = 0
        last_st = None
        prev_item = None
        while qi < nq or ii < ni:
            if qi < nq and (ii >= ni or qi * max(ni, 1) <= ii * nq):
                last_st = quant_chain(*qthunks[qi]); qi += 1
            else:
                jj, th2 = items[ii]
                if th2 == 0:
                    cur_tctx = [None]
                if prev_item is not None:
                    emit_evac(prev_item)
                prev_item = emit_mm_item(jj, th2, wq_tok_of[bp - 1], tctx=cur_tctx)
                ii += 1
        if prev_item is not None:
            emit_evac(prev_item)
        if bp < NBP:
            wq_tok_of[bp] = ("st_wq", p.ctr["st_wq"])
    n_h_stores = p.ctr["st_h"]
    n_wp_stores = p.ctr["st_wq"]
    st3.close()

    p.barrier()

    st3b = ExitStack()
    rowt = st3b.enter_context(nc.sbuf_tensor("rowt", [1, 4, 512], F32))
    for tq in range(NTQ):
        tm = p.step("tensor", lambda e, tq=tq: e.matmul(
            PS[0:1, 7, :], ones1fs[:], ssqP[:, tq * 512:(tq + 1) * 512],
            start=True, stop=True), waits=[bfree[7]], serial=False)
        t1 = p.step("vector", lambda e: e.tensor_scalar(
            rowt[:, 0, :], PS[0:1, 7, :], 1.0 / Hs, EPS_RMS, ALU.mult, ALU.add), waits=[tm])
        bfree[7] = t1
        t2 = p.step("vector", lambda e: e.reciprocal(rowt[:, 1, :], rowt[:, 0, :]), waits=[t1])
        t3 = p.step("scalar", lambda e: e.activation(rowt[:, 1, :], rowt[:, 1, :], AF.Sqrt),
                    waits=[t2])
        tn = t3
        for _ in range(2):
            n1 = p.step("scalar", lambda e: e.activation(rowt[:, 2, :], rowt[:, 1, :], AF.Square),
                        waits=[tn])
            n2 = p.step("vector", lambda e: e.tensor_tensor(rowt[:, 2, :], rowt[:, 2, :],
                                                            rowt[:, 0, :], ALU.mult), waits=[n1])
            n3 = p.step("vector", lambda e: e.tensor_scalar(rowt[:, 2, :], rowt[:, 2, :],
                                                            -0.5, 1.5, ALU.mult, ALU.add),
                        waits=[n2])
            tn = p.step("vector", lambda e: e.tensor_tensor(rowt[:, 1, :], rowt[:, 1, :],
                                                            rowt[:, 2, :], ALU.mult), waits=[n3])
        for s in range(NS):
            tt = p.step("tensor", lambda e, s=s: e.transpose(
                PS[:, 6, 0:1], rowt[0:1, 1, s * 128:(s + 1) * 128], onesrs[0:1, 0:1]),
                waits=[tn, bfree[6]], serial=False)
            tv = p.step("vector", lambda e, tq=tq, s=s: e.tensor_copy(
                invr[:, tq, s:s + 1], PS[:, 6, 0:1]), waits=[tt])
            bfree[6] = tv
    st3b.close()

    p.barrier()

    st4 = ExitStack()
    sb4 = lambda nm, sh, dt: st4.enter_context(nc.sbuf_tensor(nm, sh, dt))
    hts_h = sb4("hts_h", [128, 4, 512], BF16)
    hts_l = sb4("hts_l", [128, 4, 512], BF16)
    wps = sb4("wps", [128, 4, 1024], U16)
    o32s = sb4("o32s", [128, 1024], F32)
    r1s = sb4("r1s", [128, 1024], F32)
    ots = sb4("ots", [128, 1024], F32)
    htok = ("st_h", n_h_stores)
    wptok = ("st_wq", n_wp_stores)
    r1tok = ("st_r1", n_r1_stores)
    t_out_dma = None
    t_hfree = [None, None, None, None]
    t_r1s_free = None
    for tp in range(NTQ):
        for eh in range(2):
            gtoks = [None] * NS
            for hc in range(NHC):
                par = hc % 4
                ta_ = p.dma("sync", "d_hh%d" % par, hts_h[:, par, :],
                      hhi[hc * 128:(hc + 1) * 128, tp * 512:(tp + 1) * 512],
                      waits=[t_hfree[par], htok])
                tb_ = p.dma("gpsimd", "d_hl%d" % par, hts_l[:, par, :],
                      hlo[hc * 128:(hc + 1) * 128, tp * 512:(tp + 1) * 512],
                      waits=[t_hfree[par], htok])
                tl3 = p.dma("sync", "d_hw%d" % par, wps[:, par, :],
                            wph[hc * 128:(hc + 1) * 128, eh * 1024:(eh + 1) * 1024],
                            waits=[t_hfree[par], wptok])
                for ts in range(NS):
                    def rmm(e, ts=ts, par=par, hc=hc):
                        last = None
                        for half in range(2):
                            o = PS[:, ts * 2 + half, :]
                            wslab = wps[:, par, half * 512:(half + 1) * 512].bitcast(BF16)
                            last = e.matmul(o, hts_h[:, par, ts * 128:(ts + 1) * 128], wslab,
                                            start=(hc == 0), stop=False)
                            last = e.matmul(o, hts_l[:, par, ts * 128:(ts + 1) * 128], wslab,
                                            start=False, stop=(hc == NHC - 1))
                        return last
                    gtoks[ts] = p.step("tensor", rmm,
                                       waits=[ta_, tb_, tl3] + ([bfree[ts * 2], bfree[ts * 2 + 1]]
                                                                if hc == 0 else []), serial=False)
                t_hfree[par] = gtoks[NS - 1]
            for ts in range(NS):
                rows = tp * 512 + ts * 128
                e1 = p.step("vector", lambda e, ts=ts, tp=tp: e.tensor_scalar_mul(
                    o32s[:], PS[:, ts * 2:ts * 2 + 2, :], invr[:, tp, ts:ts + 1]),
                    waits=[gtoks[ts]])
                bfree[ts * 2] = e1
                bfree[ts * 2 + 1] = e1
                tlr = p.dma("sync", "d_r1s", r1s[:], r1d[rows:rows + 128, eh * 1024:(eh + 1) * 1024],
                            waits=[t_r1s_free, r1tok])
                e2 = p.step("vector", lambda e: e.tensor_tensor(ots[:], o32s[:], r1s[:], ALU.add),
                            waits=[e1, tlr, t_out_dma])
                t_r1s_free = e2
                t_out_dma = p.dma("gpsimd", "st_out", out[rows:rows + 128, eh * 1024:(eh + 1) * 1024],
                                  ots[:], waits=[e2])
    p.step("gpsimd", lambda e: None, waits=[t_out_dma], serial=False)
    st4.close()
    ffn_st.close()

    p.emit_all()
    st.close()
    return nc


def _host_consts():
    import ml_dtypes
    BF = ml_dtypes.bfloat16
    sig = np.arange(C)[:, None]
    ta = np.arange(C)[None, :]
    expnt = np.where(sig > ta, (sig - ta - 1).astype(np.float32),
                     np.float32(1e30)).astype(np.float32)
    expn_in = np.ascontiguousarray(expnt.reshape(4, 128, C))
    tauv_in = np.ascontiguousarray((C - 1 - np.arange(C, dtype=np.float32)).reshape(4, 128, 1))
    sigv_in = np.ascontiguousarray(np.arange(C, dtype=np.float32).reshape(4, 128, 1))
    id32_in = np.eye(128, dtype=np.float32)
    seg_in = np.zeros((128, 2), dtype=np.float32)
    seg_in[0:64, 0] = 1.0 / 64
    seg_in[64:128, 1] = 1.0 / 64
    segt_in = np.zeros((2, 128), dtype=np.float32)
    segt_in[0, 0:64] = 1.0
    segt_in[1, 64:128] = 1.0
    return dict(expn=expn_in, tauv=tauv_in, sigv=sigv_in, id32=id32_in,
                idbf=id32_in.astype(BF), seg=seg_in.astype(BF),
                segtb=segt_in.astype(BF), segtf=segt_in,
                onesr=np.ones((1, 128), dtype=np.float32),
                ones1f=np.ones((128, 1), dtype=np.float32),
                ones1b=np.ones((128, 1), dtype=np.float32).astype(BF))


def kernel(L0_attn_x, L0_attn_Wq, L0_attn_Wk, L0_attn_Wv, L0_attn_Wo, L0_attn_Wg,
           L0_attn_decay_logits, L0_attn_out_scale, L0_ffn_w_gate_up, L0_ffn_w_proj):
    xfull = np.ascontiguousarray(np.asarray(L0_attn_x, dtype=np.float32))
    B, T, _ = xfull.shape
    half = T // 2
    wq_in = np.ascontiguousarray(np.asarray(L0_attn_Wq, np.float32))
    wk_in = np.ascontiguousarray(np.asarray(L0_attn_Wk, np.float32))
    wv_in = np.ascontiguousarray(np.asarray(L0_attn_Wv, np.float32))
    wo_in = np.ascontiguousarray(np.asarray(L0_attn_Wo, np.float32))
    wg_in = np.zeros((DM, 8), dtype=np.float32)
    wg_in[:, 0:3] = np.asarray(L0_attn_Wg, np.float32)
    wgun_in = np.ascontiguousarray(np.asarray(L0_ffn_w_gate_up, np.float32))
    wpn_in = np.ascontiguousarray(np.asarray(L0_ffn_w_proj, np.float32))
    dl = np.asarray(L0_attn_decay_logits, np.float32)
    osc = np.float32(L0_attn_out_scale)
    consts = _host_consts()

    nc = build_nc(half, half, wpn_in.shape[1])
    in_maps = []
    for c in range(8):
        b, hf = c // 2, c % 2
        t0 = hf * half
        xc = np.ascontiguousarray(xfull[b, t0:t0 + half])
        xtc = np.ascontiguousarray(xfull[b, t0:t0 + half].T)
        if hf == 0:
            xoc = np.ascontiguousarray(xfull[b, half:])
            xotc = np.ascontiguousarray(xfull[b, half:].T)
            smask = 1.0
        else:
            xoc, xotc, smask = xc, xtc, 0.0
        cstv = np.zeros((1, 16), dtype=np.float32)
        cstv[0, 0:3] = dl
        cstv[0, 3] = osc
        cstv[0, 4] = smask
        cstv[0, 5] = float(C)
        m = dict(x=xc, xt=xtc, xo=xoc, xot=xotc, wq=wq_in, wk=wk_in, wv=wv_in,
                 wo=wo_in, wg=wg_in, cst=cstv, wgun=wgun_in, wpn=wpn_in, **consts)
        in_maps.append(m)
    res = run_bass_kernel_spmd(nc, in_maps, core_ids=list(range(8)))
    outv = np.empty_like(xfull)
    for c in range(8):
        b, hf = c // 2, c % 2
        outv[b, hf * half:(hf + 1) * half] = res.results[c]["out"]
    return outv


# version 25
# speedup vs baseline: 1.8652x; 1.0161x over previous
import numpy as np
import concourse.bass as bass
import concourse.mybir as mybir
from concourse.bass_utils import run_bass_kernel_spmd
from contextlib import ExitStack

F32 = mybir.dt.float32
BF16 = mybir.dt.bfloat16
U16 = mybir.dt.uint16
I32 = mybir.dt.int32
AF = mybir.ActivationFunctionType
ALU = mybir.AluOpType
AX = mybir.AxisListType

C = 512
D = 512
DM = 2048
NS = 4
ND = 16
NE = 4
EPS_LN = 1e-5
EPS_RMS = 1.1920929e-7


class PB:
    ENGS = ("sync", "gpsimd", "scalar", "vector", "tensor")

    def __init__(self, nc, stack):
        self.nc = nc
        self.stack = stack
        self.q = {e: [] for e in self.ENGS}
        self.sems = {}
        self.ctr = {}
        self.fence = {e: [] for e in self.ENGS}
        for e in self.ENGS:
            self.sem("p_" + e)

    def sem(self, name):
        if name not in self.sems:
            self.sems[name] = self.stack.enter_context(self.nc.semaphore(name))
            self.ctr[name] = 0
        return self.sems[name]

    def step(self, eng, fn, waits=(), amt=1, semname=None, serial=None):
        sn = semname or ("p_" + eng)
        self.sem(sn)
        wl = {}
        if serial is None:
            serial = eng in ("vector", "scalar")
        if serial and self.ctr["p_" + eng] > 0:
            wl["p_" + eng] = self.ctr["p_" + eng]
        for w in list(waits) + self.fence[eng]:
            if w is None:
                continue
            s, v = w
            if v > wl.get(s, -1):
                wl[s] = v
        self.fence[eng] = []
        self.ctr[sn] += amt
        tok = (sn, self.ctr[sn])
        self.q[eng].append((sorted(wl.items()), fn, sn, amt))
        return tok

    def dma(self, eng, sem, out_ap, in_ap, waits=(), **kw):
        self.sem(sem)
        prev = self.ctr[sem]
        w = list(waits) + ([(sem, prev)] if prev > 0 else [])
        return self.step(eng, lambda e: e.dma_start(out_ap, in_ap, **kw),
                         waits=w, amt=16, semname=sem, serial=False)

    def barrier(self):
        toks = [self.step(e, lambda en: None,
                          waits=[("p_" + e, self.ctr["p_" + e])]) for e in self.ENGS]
        for e in self.ENGS:
            self.fence[e] = list(toks)

    def emit_all(self):
        nc = self.nc
        with nc.Block() as block:
            def mk(eng_name):
                def run(e):
                    for wl, fn, sn, amt in self.q[eng_name]:
                        for s, v in wl:
                            e.wait_ge(self.sems[s], v)
                        inst = fn(e)
                        if inst is not None:
                            inst.then_inc(self.sems[sn], amt)
                        else:
                            e.nop(nofuse=True).then_inc(self.sems[sn], amt)
                return run
            block.sync(mk("sync"))
            block.gpsimd(mk("gpsimd"))
            block.scalar(mk("scalar"))
            block.vector(mk("vector"))
            block.tensor(mk("tensor"))


def build_nc(n_own, n_oth, Hs):
    H2s = 2 * Hs
    nch = n_own // C
    noch = n_oth // C
    NHC = Hs // 128
    NTQ = n_own // C
    NGG = Hs // 256
    NPE = DM // 256

    nc = bass.Bass("TRN2", target_bir_lowering=False)
    din = lambda name, shape, dt=F32: nc.dram_tensor(name, shape, dt, kind="ExternalInput").ap()

    x = din("x", [n_own, DM]);        xt = din("xt", [DM, n_own])
    xo = din("xo", [n_oth, DM]);      xot = din("xot", [DM, n_oth])
    wq = din("wq", [DM, D]);          wk = din("wk", [3, DM, D])
    wv = din("wv", [3, DM, D]);       wo = din("wo", [D, DM])
    wg = din("wg", [DM, 8]);          cst = din("cst", [1, 16])
    wgun = din("wgun", [H2s, DM]);    wpn = din("wpn", [DM, Hs])
    expn = din("expn", [4, 128, C]);  tauv = din("tauv", [4, 128, 1])
    sigv = din("sigv", [4, 128, 1])
    id32 = din("id32", [128, 128]);   idbf = din("idbf", [128, 128], BF16)
    seg = din("seg", [128, 2], BF16); segtb = din("segtb", [2, 128], BF16)
    segtf = din("segtf", [2, 128]);   onesr = din("onesr", [1, 128])
    ones1f = din("ones1f", [128, 1]); ones1b = din("ones1b", [128, 1], BF16)

    out = nc.dram_tensor("out", [n_own, DM], F32, kind="ExternalOutput").ap()

    r1d = nc.dram_tensor("r1d", [n_own, DM], F32, kind="Internal").ap()
    hhi = nc.dram_tensor("hhi", [Hs, n_own], BF16, kind="Internal").ap()
    hlo = nc.dram_tensor("hlo", [Hs, n_own], BF16, kind="Internal").ap()
    wph = nc.dram_tensor("wph", [Hs, DM], U16, kind="Internal").ap()
    wgh_n = nc.dram_tensor("wgh_n", [H2s, DM], U16, kind="Internal").ap()
    s_w = {}
    for nm, shp in [("q", [DM, D]), ("k", [3, DM, D]), ("v", [3, DM, D]), ("o", [D, DM])]:
        s_w[nm + "h"] = nc.dram_tensor("s_w" + nm + "h", shp, BF16, kind="Internal").ap()
        s_w[nm + "l"] = nc.dram_tensor("s_w" + nm + "l", shp, BF16, kind="Internal").ap()

    st = ExitStack()
    p = PB(nc, st)
    sbt = lambda nm, sh, dt: st.enter_context(nc.sbuf_tensor(nm, sh, dt))

    PS = st.enter_context(nc.psum_tensor("PS", [128, 8, 512], F32))
    bfree = [None] * 8

    csts = sbt("csts", [1, 16], F32)
    srow = sbt("srow", [1, 16], F32)
    srob = sbt("srob", [1, 4], BF16)
    sbc = sbt("sbc", [128, 16], F32)
    ident = sbt("ident", [128, 128], F32)
    identb = sbt("identb", [128, 128], BF16)
    segs = sbt("segs", [128, 2], BF16)
    segtbs = sbt("segtbs", [2, 128], BF16)
    segtfs = sbt("segtfs", [2, 128], F32)
    onesrs = sbt("onesrs", [1, 128], F32)
    ones1fs = sbt("ones1fs", [128, 1], F32)
    ones1bs = sbt("ones1bs", [128, 1], BF16)
    invr = sbt("invr", [128, NTQ, 4], F32)
    ap_st = ExitStack()
    apt = lambda nm, sh, dt: ap_st.enter_context(nc.sbuf_tensor(nm, sh, dt))
    dgh = apt("dgh", [128, 3, 128], BF16)
    dgl = apt("dgl", [128, 3, 128], BF16)
    cvec = apt("cvec", [128, 4, 4], F32)
    evecs = apt("evecs", [128, 2, 4, 4], F32)
    msk = apt("msk", [128, 3, 4, 512], BF16)
    tauvs = apt("tauvs", [128, 4, 1], F32)
    sigvs = apt("sigvs", [128, 4, 1], F32)
    csr_q = apt("csr_q", [1, 512], BF16)
    csr_k = apt("csr_k", [1, 3, 512], BF16)
    csr_v = apt("csr_v", [1, 3, 512], BF16)
    csr_g = apt("csr_g", [1, 8], BF16)
    wgh = apt("wgh", [128, ND, 8], BF16)
    wgl = apt("wgl", [128, ND, 8], BF16)
    Sh = apt("Sh", [128, 3, 4, 512], BF16)
    Sl = apt("Sl", [128, 3, 4, 512], BF16)

    st0a = ExitStack()
    expns = st0a.enter_context(nc.sbuf_tensor("expns", [128, 4, 512], F32))
    t_cst = p.dma("sync", "d_cst", csts[:], cst[:, :])
    for dst_, src_ in [(ident, id32), (identb, idbf), (segs, seg), (segtbs, segtb),
                       (segtfs, segtf), (onesrs, onesr), (ones1fs, ones1f), (ones1bs, ones1b)]:
        p.dma("sync", "d_cst", dst_[:], src_[:, :])
    p.dma("sync", "d_cst", tauvs[:], tauv.rearrange("s p o -> p s o"), allow_slow_non_contiguous=True)
    p.dma("sync", "d_cst", sigvs[:], sigv.rearrange("s p o -> p s o"), allow_slow_non_contiguous=True)
    t_ldall = p.dma("sync", "d_cst", expns[:], expn.rearrange("s p t -> p s t"))

    t0m = p.step("vector", lambda e: e.memset(srow[:], 0.0))
    t1 = p.step("scalar", lambda e: e.activation(srow[:, 14:15], csts[:, 0:1], AF.Sigmoid), waits=[t_ldall, t0m])
    t1 = p.step("scalar", lambda e: e.activation(srow[:, 0:1], srow[:, 14:15], AF.Ln), waits=[t1])
    t1 = p.step("scalar", lambda e: e.activation(srow[:, 14:15], csts[:, 1:2], AF.Sigmoid), waits=[t1])
    t1 = p.step("scalar", lambda e: e.activation(srow[:, 1:2], srow[:, 14:15], AF.Ln), waits=[t1])
    t1 = p.step("scalar", lambda e: e.activation(srow[:, 14:15], csts[:, 2:3], AF.Sigmoid), waits=[t1])
    t1 = p.step("scalar", lambda e: e.activation(srow[:, 2:3], srow[:, 14:15], AF.Ln), waits=[t1])
    tds = []
    for i in range(3):
        tds.append(p.step("scalar", lambda e, i=i: e.activation(
            srow[:, 3 + i:4 + i], csts[:, 5:6], AF.Exp, scale=srow[:, i:i + 1]), waits=[t1]))
    t2 = p.step("vector", lambda e: e.tensor_copy(srob[:, 0:3], srow[:, 3:6]), waits=tds)
    t2 = p.step("vector", lambda e: e.tensor_copy(srow[:, 6:9], srob[:, 0:3]), waits=[t2])
    t2 = p.step("vector", lambda e: e.tensor_tensor(srow[:, 9:12], srow[:, 3:6], srow[:, 6:9],
                                                    ALU.subtract), waits=[t2])
    t2 = p.step("vector", lambda e: e.tensor_copy(srow[:, 12:14], csts[:, 3:5]), waits=[t2])
    t3 = p.step("tensor", lambda e: e.matmul(PS[:, 7, 0:16], onesrs[:], srow[:, 0:16],
                                             start=True, stop=True), waits=[t2], serial=False)
    t_sbc = p.step("vector", lambda e: e.tensor_copy(sbc[:], PS[:, 7, 0:16]), waits=[t3])
    bfree[7] = t_sbc
    for i in range(3):
        p.step("vector", lambda e, i=i: e.tensor_scalar_mul(dgh[:, i, :], identb[:],
                                                            sbc[:, 6 + i:7 + i]), waits=[t_sbc])
        p.step("vector", lambda e, i=i: e.tensor_scalar_mul(dgl[:, i, :], identb[:],
                                                            sbc[:, 9 + i:10 + i]), waits=[t_sbc])
    for i in range(3):
        for s in range(4):
            p.step("scalar", lambda e, i=i, s=s: e.activation(
                cvec[:, s, i:i + 1], tauvs[:, s, :], AF.Exp, scale=sbc[:, i:i + 1]), waits=[t_sbc])
            te = p.step("scalar", lambda e, i=i, s=s: e.activation(
                evecs[:, 0, s, i:i + 1], sigvs[:, s, :], AF.Exp, scale=sbc[:, i:i + 1]), waits=[t_sbc])
            p.step("vector", lambda e, i=i, s=s: e.tensor_scalar_mul(
                evecs[:, 1, s, i:i + 1], evecs[:, 0, s, i:i + 1], sbc[:, 13:14]), waits=[te])
    for i in range(3):
        for s in range(4):
            p.step("scalar", lambda e, i=i, s=s: e.activation(
                msk[:, i, s, :], expns[:, s, :], AF.Exp, scale=sbc[:, i:i + 1]), waits=[t_sbc])

    p.barrier()
    st0a.close()

    with ExitStack() as st0:
        wst = st0.enter_context(nc.sbuf_tensor("wst", [128, 4, 512], F32))
        whst = st0.enter_context(nc.sbuf_tensor("whst", [128, 4, 512], BF16))
        wlst = st0.enter_context(nc.sbuf_tensor("wlst", [128, 4, 512], BF16))
        wcons = [None, None, None, None]

        def prep_weight(src_ap, dsth, dstl, ntile, csum_to, tile_sl):
            tcs = None
            first = [True]
            for t in range(ntile):
                par = t % 4
                tl = p.dma("sync", "d_w%d" % par, wst[:, par, :], tile_sl(src_ap, t),
                           waits=wcons[par] or [])
                th = p.step("vector", lambda e, par=par: e.tensor_copy(whst[:, par, :], wst[:, par, :]),
                            waits=[tl])
                tlo = p.step("vector", lambda e, par=par: e.tensor_tensor(
                    wlst[:, par, :], wst[:, par, :], whst[:, par, :], ALU.subtract), waits=[th])
                if csum_to is not None:
                    def csmm(e, par=par, t=t, f=first[0]):
                        e.matmul(PS[0:1, 6, :], ones1bs[:], whst[:, par, :], start=f, stop=False)
                        return e.matmul(PS[0:1, 6, :], ones1bs[:], wlst[:, par, :],
                                        start=False, stop=(t == ntile - 1))
                    tcs = p.step("tensor", csmm, waits=[tlo, bfree[6] if first[0] else None],
                                 serial=False)
                    first[0] = False
                d1 = p.dma("gpsimd", "st_wei", tile_sl(dsth, t), whst[:, par, :], waits=[tlo])
                d2 = p.dma("gpsimd", "st_wei", tile_sl(dstl, t), wlst[:, par, :], waits=[tlo])
                wcons[par] = [d2, tcs] if csum_to is not None else [d2]
            if csum_to is not None:
                tv = p.step("vector", lambda e: e.tensor_copy(csum_to, PS[0:1, 6, :]), waits=[tcs])
                bfree[6] = tv

        rowtile = lambda ap_, t: ap_[t * 128:(t + 1) * 128, :]
        wo_tile = lambda ap_, t: ap_[(t // 4) * 128:(t // 4 + 1) * 128, (t % 4) * 512:(t % 4 + 1) * 512]
        prep_weight(wq, s_w["qh"], s_w["ql"], ND, csr_q[:, :], rowtile)
        for i in range(3):
            prep_weight(wk[i], s_w["kh"][i], s_w["kl"][i], ND, csr_k[:, i, :], rowtile)
            prep_weight(wv[i], s_w["vh"][i], s_w["vl"][i], ND, csr_v[:, i, :], rowtile)
        prep_weight(wo, s_w["oh"], s_w["ol"], 16, None, wo_tile)
        tcs = None
        for t in range(ND):
            par = t % 4
            tl = p.dma("sync", "d_w%d" % par, wst[:, par, 0:8], wg[t * 128:(t + 1) * 128, :],
                       waits=wcons[par] or [])
            th = p.step("vector", lambda e, par=par, t=t: e.tensor_copy(wgh[:, t, :], wst[:, par, 0:8]),
                        waits=[tl])
            tlo = p.step("vector", lambda e, par=par, t=t: e.tensor_tensor(
                wgl[:, t, :], wst[:, par, 0:8], wgh[:, t, :], ALU.subtract), waits=[th])
            def gcs(e, t=t):
                e.matmul(PS[0:1, 6, 0:8], ones1bs[:], wgh[:, t, :], start=(t == 0), stop=False)
                return e.matmul(PS[0:1, 6, 0:8], ones1bs[:], wgl[:, t, :],
                                start=False, stop=(t == ND - 1))
            tcs = p.step("tensor", gcs, waits=[tlo, bfree[6] if t == 0 else None], serial=False)
            wcons[par] = [tcs]
        tv = p.step("vector", lambda e: e.tensor_copy(csr_g[:], PS[0:1, 6, 0:8]), waits=[tcs])
        bfree[6] = tv
        n_wei_stores = p.ctr["st_wei"]

    p.barrier()

    st1 = ExitStack()
    sb1 = lambda nm, sh, dt: st1.enter_context(nc.sbuf_tensor(nm, sh, dt))
    xsb = sb1("xsb", [128, NS, DM], F32)
    xtst = sb1("xtst", [128, 2, 512], F32)
    xth = sb1("xth", [128, ND, 512], BF16)
    xtl = sb1("xtl", [128, ND, 512], BF16)
    sqs = sb1("sqs", [128, DM], F32)
    stf = sb1("stf", [128, NS, 8], F32)
    stb = sb1("stb", [128, NS, 1], BF16)
    nmrow = sb1("nmrow", [1, 512], BF16)
    einv2 = sb1("einv2", [128, NS, 4], F32)
    qh = sb1("qh", [128, NE, 512], BF16);  ql = sb1("ql", [128, NE, 512], BF16)
    kh = sb1("kh", [128, NE, 512], BF16);  kl = sb1("kl", [128, NE, 512], BF16)
    kth = sb1("kth", [128, NS, 512], BF16); ktl = sb1("ktl", [128, NS, 512], BF16)
    vth = sb1("vth", [128, NS, 512], BF16); vtl = sb1("vtl", [128, NS, 512], BF16)
    vsh = sb1("vsh", [128, NS, 512], BF16); vsl = sb1("vsl", [128, NS, 512], BF16)
    sch = sb1("sch", [128, NS, 512], BF16); scl_ = sb1("scl_", [128, NS, 512], BF16)
    scm32 = sb1("scm32", [128, 512], F32)
    rc32 = sb1("rc32", [128, 512], F32)
    ra32 = sb1("ra32", [128, 512], F32)
    vt32 = sb1("vt32", [128, 512], F32)
    blsb = sb1("blsb", [128, NS, 512], F32)
    blth = sb1("blth", [128, NE, 512], BF16); bltl = sb1("bltl", [128, NE, 512], BF16)
    gl3 = sb1("gl3", [128, NS, 8], F32)
    giv = sb1("giv", [128, NS, 4], F32)
    wsl_h = sb1("wsl_h", [128, 4, 512], BF16)
    wsl_l = sb1("wsl_l", [128, 4, 512], BF16)

    tok = {
        "xsb_free": [None] * NS,
        "xcast": [None] * ND,
        "xth_cons": None,
        "S_ready": [None] * 3,
        "S_read": [None] * 3,
        "wslab": [("p_tensor", 0)] * 4,
    }
    wei_tok = ("st_wei", n_wei_stores)

    def load_wslab(src_h, src_l, r0, c0, c1, idx):
        par = idx % 4
        p.dma("sync", "d_wsl%d" % par, wsl_h[:, par, 0:c1 - c0], src_h[r0:r0 + 128, c0:c1],
              waits=[tok["wslab"][par], wei_tok])
        t2 = p.dma("sync", "d_wsl%d" % par, wsl_l[:, par, 0:c1 - c0], src_l[r0:r0 + 128, c0:c1])
        return t2, par

    def proj_dloop(src_h, src_l, fm, t_xt, wcnt, b0=0):
        groups = [None] * 4
        for d in range(ND):
            tw, par = load_wslab(src_h, src_l, d * 128, 0, 512, wcnt[0])
            wcnt[0] += 1
            for g in range(4):
                first = (d == 0)
                def mm(e, d=d, g=g, par=par, first=first, b0=b0):
                    o = PS[:, b0 + g, :]
                    if fm:
                        w_h = wsl_h[:, par, g * 128:(g + 1) * 128]
                        w_l = wsl_l[:, par, g * 128:(g + 1) * 128]
                        e.matmul(o, w_h, xth[:, d, :], start=first, stop=False)
                        e.matmul(o, w_l, xth[:, d, :], start=False, stop=False)
                        return e.matmul(o, w_h, xtl[:, d, :], start=False, stop=False)
                    else:
                        xh_t = xth[:, d, g * 128:(g + 1) * 128]
                        xl_t = xtl[:, d, g * 128:(g + 1) * 128]
                        e.matmul(o, xh_t, wsl_h[:, par, :], start=first, stop=False)
                        e.matmul(o, xl_t, wsl_h[:, par, :], start=False, stop=False)
                        return e.matmul(o, xh_t, wsl_l[:, par, :], start=False, stop=False)
                groups[g] = p.step("tensor", mm,
                                   waits=[tw, tok["xcast"][d],
                                          bfree[b0 + g] if d == 0 else None], serial=False)
            tok["wslab"][par] = groups[3]
        return groups

    def proj_finish(groups, dst_h, dst_l, csr_ap, fm, t_nm, t_stats, extra=None, b0=0):
        for g in range(4):
            def r1mm(e, g=g, b0=b0):
                o = PS[:, b0 + g, :]
                if fm:
                    return e.matmul(o, csr_ap[:, g * 128:(g + 1) * 128], nmrow[:],
                                    start=False, stop=True)
                else:
                    return e.matmul(o, nmrow[:, g * 128:(g + 1) * 128], csr_ap,
                                    start=False, stop=True)
            groups[g] = p.step("tensor", r1mm, waits=[groups[g], t_nm], serial=False)
        ev = [None] * 4
        for g in range(4):
            pg = PS[:, b0 + g, :]
            if extra is None:
                th_ = p.step("scalar", lambda e, g=g, pg=pg: e.activation(dst_h[:, g, :], pg,
                                                                          AF.Copy), waits=[groups[g]])
                tl_ = p.step("vector", lambda e, g=g, pg=pg: e.tensor_tensor(
                    dst_l[:, g, :], pg, dst_h[:, g, :], ALU.subtract), waits=[th_])
            else:
                sc1, (edh, edl), sc2 = extra
                t0_ = p.step("vector", lambda e, g=g, pg=pg: e.tensor_scalar_mul(
                    vt32[:], pg, sc1(g)), waits=[groups[g], t_stats])
                th_ = p.step("scalar", lambda e, g=g: e.activation(dst_h[:, g, :], vt32[:],
                                                                   AF.Copy), waits=[t0_])
                tl_ = p.step("vector", lambda e, g=g: e.tensor_tensor(
                    dst_l[:, g, :], vt32[:], dst_h[:, g, :], ALU.subtract), waits=[th_])
                t0b = p.step("vector", lambda e, g=g, pg=pg: e.tensor_scalar_mul(
                    rc32[:], pg, sc2(g)), waits=[groups[g], t_stats])
                th2 = p.step("scalar", lambda e, g=g: e.activation(edh[:, g, :], rc32[:],
                                                                   AF.Copy), waits=[t0b])
                tl_ = p.step("vector", lambda e, g=g: e.tensor_tensor(
                    edl[:, g, :], rc32[:], edh[:, g, :], ALU.subtract), waits=[th2, tl_])
            ev[g] = tl_
            bfree[b0 + g] = tl_
        return ev

    def proj_pass(dst_h, dst_l, src_h, src_l, csr_ap, fm, t_nm, t_xt, t_stats,
                  wcnt, extra=None, b0=0):
        groups = proj_dloop(src_h, src_l, fm, t_xt, wcnt, b0=b0)
        return proj_finish(groups, dst_h, dst_l, csr_ap, fm, t_nm, t_stats, extra=extra, b0=b0)

    wcnt = [0]

    def do_chunk(xsrc, xtsrc, base, is_comp, cx):
        t_xs = [None] * NS
        for ts in range(NS):
            t_xs[ts] = p.dma("sync", "d_xsb", xsb[:, ts, :],
                             xsrc[base + ts * 128:base + (ts + 1) * 128, :],
                             waits=[tok["xsb_free"][ts]])
        t_x = t_xs[NS - 1]
        for d in range(ND):
            par = d % 2
            tl = p.dma("sync", "d_xt%d" % par, xtst[:, par, :], xtsrc[d * 128:(d + 1) * 128, base:base + C],
                       waits=[tok["xcast"][d - 2] if d >= 2 else tok["xth_cons"]])
            th = p.step("vector", lambda e, d=d, par=par: e.tensor_copy(xth[:, d, :], xtst[:, par, :]),
                        waits=[tl, tok["xth_cons"] if d < 2 else None])
            tok["xcast"][d] = p.step("vector", lambda e, d=d, par=par: e.tensor_tensor(
                xtl[:, d, :], xtst[:, par, :], xth[:, d, :], ALU.subtract), waits=[th])
        t_xt = tok["xcast"][ND - 1]
        if not is_comp:
            q_groups = proj_dloop(s_w["qh"], s_w["ql"], True, t_xt, wcnt, b0=0)
            k0_groups = proj_dloop(s_w["kh"][0], s_w["kl"][0], True, t_xt, wcnt, b0=4)
        else:
            k0_groups = proj_dloop(s_w["kh"][0], s_w["kl"][0], True, t_xt, wcnt, b0=4)
            v0_groups = proj_dloop(s_w["vh"][0], s_w["vl"][0], False, t_xt, wcnt, b0=0)
        t_stats = None
        t_nm = None
        for s in range(NS):
            tq_ = p.step("scalar", lambda e, s=s: e.activation(sqs[:], xsb[:, s, :], AF.Square),
                         waits=[t_xs[s], t_stats])
            ta = p.step("vector", lambda e, s=s: e.reduce_sum(
                stf[:, s, 0:1], xsb[:, s, :], AX.X, negate=True), waits=[t_xs[s]])
            tb = p.step("vector", lambda e, s=s: e.reduce_sum(stf[:, s, 6:7], sqs[:], AX.X),
                        waits=[tq_])
            tc = p.step("vector", lambda e, s=s: e.tensor_scalar_mul(
                stf[:, s, 0:1], stf[:, s, 0:1], 1.0 / DM), waits=[ta])
            td = p.step("vector", lambda e, s=s: e.tensor_scalar(
                stf[:, s, 1:2], stf[:, s, 6:7], 1.0 / DM, EPS_LN, ALU.mult, ALU.add), waits=[tb])
            te = p.step("vector", lambda e, s=s: e.tensor_scalar_mul(
                stf[:, s, 2:3], stf[:, s, 0:1], stf[:, s, 0:1]), waits=[tc])
            tf = p.step("vector", lambda e, s=s: e.scalar_tensor_tensor(
                stf[:, s, 3:4], stf[:, s, 2:3], -1.0, stf[:, s, 1:2], ALU.mult, ALU.add),
                waits=[td, te])
            tg = p.step("vector", lambda e, s=s: e.reciprocal(stf[:, s, 4:5], stf[:, s, 3:4]),
                        waits=[tf])
            tn = p.step("scalar", lambda e, s=s: e.activation(
                stf[:, s, 4:5], stf[:, s, 4:5], AF.Sqrt), waits=[tg])
            for _ in range(2):
                n1 = p.step("scalar", lambda e, s=s: e.activation(
                    stf[:, s, 6:7], stf[:, s, 4:5], AF.Square), waits=[tn])
                n2 = p.step("vector", lambda e, s=s: e.tensor_tensor(
                    stf[:, s, 6:7], stf[:, s, 6:7], stf[:, s, 3:4], ALU.mult), waits=[n1])
                n3 = p.step("vector", lambda e, s=s: e.tensor_scalar(
                    stf[:, s, 6:7], stf[:, s, 6:7], -0.5, 1.5, ALU.mult, ALU.add), waits=[n2])
                tn = p.step("vector", lambda e, s=s: e.tensor_tensor(
                    stf[:, s, 4:5], stf[:, s, 4:5], stf[:, s, 6:7], ALU.mult), waits=[n3])
            ti = p.step("vector", lambda e, s=s: e.tensor_scalar_mul(
                stf[:, s, 5:6], stf[:, s, 4:5], stf[:, s, 4:5]), waits=[tn])
            for i in range(3):
                ti = p.step("vector", lambda e, s=s, i=i, ic=1 if is_comp else 0: e.tensor_tensor(
                    einv2[:, s, i:i + 1], evecs[:, ic, s, i:i + 1], stf[:, s, 5:6], ALU.mult),
                    waits=[ti])
            tj = p.step("vector", lambda e, s=s: e.tensor_copy(
                stb[:, s, 0:1], stf[:, s, 0:1]), waits=[tc])
            t_nm = p.dma("sync", "d_nm", nmrow[0:1, s * 128:(s + 1) * 128],
                         stb[:, s, 0:1], waits=[tj], allow_slow_non_contiguous=True)
            t_stats = ti
        t_qev = None
        if not is_comp:
            t_qev = proj_finish(q_groups, qh, ql, csr_q, True, t_nm, t_stats, b0=0)
            t_k0ev = proj_finish(k0_groups, kh, kl, csr_k[:, 0, :], True, t_nm, t_stats, b0=4)
        else:
            t_k0ev = proj_finish(k0_groups, kh, kl, csr_k[:, 0, :], True, t_nm, t_stats, b0=4)
        t_gate = None
        if not is_comp:
            for s in range(NS):
                def gmm(e, s=s):
                    o = PS[:, 7, 0:8]
                    for d in range(ND):
                        e.matmul(o, xth[:, d, s * 128:(s + 1) * 128], wgh[:, d, :],
                                 start=(d == 0), stop=False)
                        e.matmul(o, xtl[:, d, s * 128:(s + 1) * 128], wgh[:, d, :],
                                 start=False, stop=False)
                        e.matmul(o, xth[:, d, s * 128:(s + 1) * 128], wgl[:, d, :],
                                 start=False, stop=False)
                    return e.matmul(o, nmrow[:, s * 128:(s + 1) * 128], csr_g[:],
                                    start=False, stop=True)
                tg1 = p.step("tensor", gmm, waits=[t_xt, t_nm, bfree[7]], serial=False)
                tg2 = p.step("vector", lambda e, s=s: e.tensor_scalar_mul(
                    gl3[:, s, 0:3], PS[:, 7, 0:3], stf[:, s, 4:5]), waits=[tg1, t_stats])
                bfree[7] = tg2
                tg3 = p.step("vector", lambda e, s=s: e.reduce_max(
                    gl3[:, s, 4:5], gl3[:, s, 0:3], AX.X, negate=True), waits=[tg2])
                tg4 = p.step("scalar", lambda e, s=s: e.activation(
                    gl3[:, s, 0:3], gl3[:, s, 0:3], AF.Exp, bias=gl3[:, s, 4:5]), waits=[tg3])
                tg5 = p.step("vector", lambda e, s=s: e.reduce_sum(
                    gl3[:, s, 4:5], gl3[:, s, 0:3], AX.X), waits=[tg4])
                tg6 = p.step("vector", lambda e, s=s: e.reciprocal(
                    gl3[:, s, 5:6], gl3[:, s, 4:5]), waits=[tg5])
                tg7 = p.step("vector", lambda e, s=s: e.tensor_scalar_mul(
                    gl3[:, s, 0:3], gl3[:, s, 0:3], gl3[:, s, 5:6]), waits=[tg6])
                t_gate = p.step("vector", lambda e, s=s: e.tensor_scalar_mul(
                    giv[:, s, 0:3], gl3[:, s, 0:3], stf[:, s, 4:5]), waits=[tg7])
        for i in range(3):
            if i == 0:
                t_kev = t_k0ev
            else:
                t_kev = proj_pass(kh, kl, s_w["kh"][i], s_w["kl"][i], csr_k[:, i, :], True,
                                  t_nm, t_xt, t_stats, wcnt)
            t_ktev = [None] * NS
            tb4 = [bfree[4], bfree[5], bfree[6], bfree[7]]
            for hl, (src, dst) in enumerate(((kh, kth), (kl, ktl))):
                for ec in range(NE):
                    tts = []
                    for ss in range(4):
                        bank = 4 + ss
                        tts.append(p.step("tensor", lambda e, ec=ec, ss=ss, bank=bank, src=src:
                                          e.matmul(PS[:, bank, 0:128],
                                                   src[:, ec, ss * 128:(ss + 1) * 128], identb[:],
                                                   start=True, stop=True),
                                          waits=[t_kev[ec], tb4[ss]], serial=False))
                    tev = p.step("scalar", lambda e, ec=ec, dst=dst: e.activation(
                        dst[:, 0:4, ec * 128:(ec + 1) * 128].rearrange("p s j -> p (s j)")
                        if False else dst[:, 0:4, ec * 128:(ec + 1) * 128],
                        PS[:, 4:8, 0:128], AF.Copy), waits=tts)
                    for ss in range(4):
                        tb4[ss] = tev
                        t_ktev[ss] = tev
            bfree[4], bfree[5], bfree[6], bfree[7] = tb4
            t_scev = [None] * NS
            if not is_comp:
                for ss in range(NS):
                    bank = 6 + (ss % 2)
                    def smm(e, ss=ss, bank=bank):
                        o = PS[:, bank, :]
                        last = None
                        for ec in range(NE):
                            last = e.matmul(o, kh[:, ec, ss * 128:(ss + 1) * 128], qh[:, ec, :],
                                            start=(ec == 0), stop=False)
                            last = e.matmul(o, kl[:, ec, ss * 128:(ss + 1) * 128], qh[:, ec, :],
                                            start=False, stop=False)
                            last = e.matmul(o, kh[:, ec, ss * 128:(ss + 1) * 128], ql[:, ec, :],
                                            start=False, stop=(ec == NE - 1))
                        return last
                    tsc = p.step("tensor", smm, waits=t_kev + t_qev + [bfree[bank]], serial=False)
                    e1 = p.step("vector", lambda e, ss=ss, bank=bank, i=i: e.scalar_tensor_tensor(
                        scm32[:], PS[:, bank, :], stf[:, ss, 4:5], msk[:, i, ss, :],
                        ALU.mult, ALU.mult), waits=[tsc, t_stats])
                    bfree[bank] = e1
                    e2 = p.step("scalar", lambda e, ss=ss: e.activation(sch[:, ss, :], scm32[:],
                                                                        AF.Copy), waits=[e1])
                    t_scev[ss] = p.step("vector", lambda e, ss=ss: e.tensor_tensor(
                        scl_[:, ss, :], scm32[:], sch[:, ss, :], ALU.subtract), waits=[e2])
            if is_comp and i == 0:
                t_vev = proj_finish(v0_groups, vth, vtl, csr_v[:, 0, :], False, t_nm, t_stats,
                                    extra=(lambda g: stf[:, g, 4:5], (vsh, vsl),
                                           lambda g, i=i: einv2[:, g, i:i + 1]), b0=0)
            else:
                t_vev = proj_pass(vth, vtl, s_w["vh"][i], s_w["vl"][i], csr_v[:, i, :], False,
                                  t_nm, t_xt, t_stats, wcnt,
                                  extra=(lambda g: stf[:, g, 4:5], (vsh, vsl),
                                         lambda g, i=i: einv2[:, g, i:i + 1]))
            t_wc = []
            if not is_comp:
                for ts in range(NS):
                    bw = 6 if ts % 2 == 0 else 7
                    bc = 4 if ts % 2 == 0 else 5
                    def wmm(e, ts=ts, bw=bw):
                        o = PS[:, bw, :]
                        last = None
                        for ss in range(NS):
                            last = e.matmul(o, sch[:, ss, ts * 128:(ts + 1) * 128], vth[:, ss, :],
                                            start=(ss == 0), stop=False)
                            last = e.matmul(o, scl_[:, ss, ts * 128:(ts + 1) * 128], vth[:, ss, :],
                                            start=False, stop=False)
                            last = e.matmul(o, sch[:, ss, ts * 128:(ts + 1) * 128], vtl[:, ss, :],
                                            start=False, stop=(ss == NS - 1))
                        return last
                    tw_ = p.step("tensor", wmm, waits=t_scev + t_vev + [bfree[bw]], serial=False)
                    def cmm(e, ts=ts, bc=bc, i=i):
                        o = PS[:, bc, :]
                        last = None
                        for ec in range(NE):
                            last = e.matmul(o, qh[:, ec, ts * 128:(ts + 1) * 128], Sh[:, i, ec, :],
                                            start=(ec == 0), stop=False)
                            last = e.matmul(o, ql[:, ec, ts * 128:(ts + 1) * 128], Sh[:, i, ec, :],
                                            start=False, stop=False)
                            last = e.matmul(o, qh[:, ec, ts * 128:(ts + 1) * 128], Sl[:, i, ec, :],
                                            start=False, stop=(ec == NE - 1))
                        return last
                    tc_ = p.step("tensor", cmm,
                                 waits=t_qev + [tok["S_ready"][i], bfree[bc]], serial=False)
                    e1 = p.step("vector", lambda e, ts=ts, bc=bc, i=i: e.tensor_scalar_mul(
                        rc32[:], PS[:, bc, :], cvec[:, ts, i:i + 1]), waits=[tc_])
                    bfree[bc] = e1
                    e2 = p.step("vector", lambda e, bw=bw: e.tensor_tensor(
                        ra32[:], PS[:, bw, :], rc32[:], ALU.add), waits=[e1, tw_])
                    bfree[bw] = e2
                    if i == 0:
                        e3 = p.step("vector", lambda e, ts=ts, i=i: e.tensor_scalar_mul(
                            blsb[:, ts, :], ra32[:], giv[:, ts, i:i + 1]), waits=[e2, t_gate])
                    else:
                        e2b = p.step("vector", lambda e, ts=ts, i=i: e.tensor_scalar_mul(
                            ra32[:], ra32[:], giv[:, ts, i:i + 1]), waits=[e2, t_gate])
                        e3 = p.step("vector", lambda e, ts=ts: e.tensor_tensor(
                            blsb[:, ts, :], blsb[:, ts, :], ra32[:], ALU.add), waits=[e2b])
                    t_wc.append(e3)
                    tok["S_read"][i] = tc_
            t_sup = []
            for ek in range(NE):
                def supmm(e, ek=ek, i=i, cx=cx):
                    o = PS[:, 4 + ek, :]
                    last = None
                    fs = True
                    if cx != 0:
                        last = e.matmul(o, dgh[:, i, :], Sh[:, i, ek, :], start=True, stop=False)
                        last = e.matmul(o, dgh[:, i, :], Sl[:, i, ek, :], start=False, stop=False)
                        last = e.matmul(o, dgl[:, i, :], Sh[:, i, ek, :], start=False, stop=False)
                        fs = False
                    for ss in range(NS):
                        last = e.matmul(o, kth[:, ss, ek * 128:(ek + 1) * 128], vsh[:, ss, :],
                                        start=fs, stop=False)
                        fs = False
                        last = e.matmul(o, ktl[:, ss, ek * 128:(ek + 1) * 128], vsh[:, ss, :],
                                        start=False, stop=False)
                        last = e.matmul(o, kth[:, ss, ek * 128:(ek + 1) * 128], vsl[:, ss, :],
                                        start=False, stop=(ss == NS - 1))
                    return last
                t_sup.append(p.step("tensor", supmm,
                                    waits=t_ktev + t_vev + [tok["S_ready"][i], bfree[4 + ek]],
                                    serial=False))
            t_sev = None
            for ek in range(NE):
                e1 = p.step("scalar", lambda e, ek=ek, i=i: e.activation(
                    Sh[:, i, ek, :], PS[:, 4 + ek, :], AF.Copy), waits=[t_sup[ek], tok["S_read"][i]])
                t_sev = p.step("vector", lambda e, ek=ek, i=i: e.tensor_tensor(
                    Sl[:, i, ek, :], PS[:, 4 + ek, :], Sh[:, i, ek, :], ALU.subtract), waits=[e1])
                bfree[4 + ek] = t_sev
            tok["S_ready"][i] = t_sev
        if not is_comp:
            t_blev = [None] * NE
            tpar = [bfree[6], bfree[7]]
            for idx in range(16):
                ts, ec = idx // 4, idx % 4
                bank = 6 + (idx % 2)
                tt = p.step("tensor", lambda e, ts=ts, ec=ec, bank=bank: e.transpose(
                    PS[:, bank, 0:128], blsb[:, ts, ec * 128:(ec + 1) * 128], ident[:]),
                    waits=[t_wc[ts], tpar[idx % 2]], serial=False)
                th_ = p.step("vector", lambda e, ts=ts, ec=ec, bank=bank: e.tensor_copy(
                    blth[:, ec, ts * 128:(ts + 1) * 128], PS[:, bank, 0:128]), waits=[tt])
                tv = p.step("vector", lambda e, ts=ts, ec=ec, bank=bank: e.tensor_tensor(
                    bltl[:, ec, ts * 128:(ts + 1) * 128], PS[:, bank, 0:128],
                    blth[:, ec, ts * 128:(ts + 1) * 128], ALU.subtract), waits=[th_])
                tpar[idx % 2] = tv
                t_blev[ec] = tv
            bfree[6], bfree[7] = tpar
            t_r1 = None
            for dq in range(4):
                gmm = [None] * NS
                for ec in range(NE):
                    tw, par = load_wslab(s_w["oh"], s_w["ol"], ec * 128, dq * 512, (dq + 1) * 512,
                                         wcnt[0])
                    wcnt[0] += 1
                    for ts in range(NS):
                        def omm(e, ts=ts, ec=ec, par=par):
                            o = PS[:, ts, :]
                            e.matmul(o, blth[:, ec, ts * 128:(ts + 1) * 128], wsl_h[:, par, :],
                                     start=(ec == 0), stop=False)
                            e.matmul(o, bltl[:, ec, ts * 128:(ts + 1) * 128], wsl_h[:, par, :],
                                     start=False, stop=False)
                            return e.matmul(o, blth[:, ec, ts * 128:(ts + 1) * 128],
                                            wsl_l[:, par, :], start=False, stop=(ec == NE - 1))
                        gmm[ts] = p.step("tensor", omm,
                                         waits=[tw, t_blev[ec]] +
                                               ([bfree[ts]] if ec == 0 else []), serial=False)
                    tok["wslab"][par] = gmm[NS - 1]
                for ts in range(NS):
                    e2 = p.step("vector", lambda e, ts=ts, dq=dq: e.scalar_tensor_tensor(
                        xsb[:, ts, dq * 512:(dq + 1) * 512], PS[:, ts, :], sbc[:, 12:13],
                        xsb[:, ts, dq * 512:(dq + 1) * 512], ALU.mult, ALU.add),
                        waits=[gmm[ts], t_xs[ts]])
                    bfree[ts] = e2
                    if dq == 3:
                        tok["xsb_free"][ts] = p.dma(
                            "gpsimd", "st_r1", r1d[base + ts * 128:base + (ts + 1) * 128, :],
                            xsb[:, ts, :], waits=[e2])
        else:
            for ts in range(NS):
                tok["xsb_free"][ts] = t_stats
        tok["xcast"] = [None] * ND
        tok["xth_cons"] = ("p_tensor", p.ctr["p_tensor"])

    cx = 0
    for j in range(noch - 1, -1, -1):
        do_chunk(xo, xot, j * C, True, cx)
        cx += 1
    for j in range(nch - 1, -1, -1):
        do_chunk(x, xt, j * C, False, cx)
        cx += 1
    n_r1_stores = p.ctr["st_r1"]
    st1.close()
    ap_st.close()

    p.barrier()

    ffn_st = ExitStack()
    sbf = lambda nm, sh, dt: ffn_st.enter_context(nc.sbuf_tensor(nm, sh, dt))
    bth = sbf("bth", [128, ND, n_own], BF16)
    btl = sbf("btl", [128, ND, n_own], BF16)
    ssqP = sbf("ssqP", [128, n_own], F32)
    t_ssq0 = p.step("vector", lambda e: e.memset(ssqP[:], 0.0))

    st2 = ExitStack()
    sb2 = lambda nm, sh, dt: st2.enter_context(nc.sbuf_tensor(nm, sh, dt))
    r1sb = sb2("r1sb", [128, 2, DM], F32)
    sq2 = sb2("sq2", [128, DM], F32)
    b32 = sq2
    st2f = sb2("st2f", [128, NS, 8], F32)
    t_r1free = [None, None]
    t_blast = None
    r1tok = ("st_r1", n_r1_stores)
    for gsub in range(NTQ * NS):
        tc2, s = gsub // NS, gsub % NS
        pr2 = gsub % 2
        t_x = p.dma("sync", "d_r1%d" % pr2, r1sb[:, pr2, :],
                    r1d[gsub * 128:(gsub + 1) * 128, :],
                    waits=[t_r1free[pr2], r1tok])
        if True:
            tq_ = p.step("scalar", lambda e, pr2=pr2: e.activation(sq2[:], r1sb[:, pr2, :], AF.Square),
                         waits=[t_x, t_blast])
            ta = p.step("vector", lambda e, s=s, pr2=pr2: e.reduce_sum(st2f[:, s, 0:1], r1sb[:, pr2, :], AX.X),
                        waits=[t_x])
            tb = p.step("vector", lambda e, s=s: e.reduce_sum(st2f[:, s, 6:7], sq2[:], AX.X),
                        waits=[tq_])
            tc1 = p.step("vector", lambda e, s=s: e.tensor_scalar_mul(
                st2f[:, s, 0:1], st2f[:, s, 0:1], 1.0 / DM), waits=[ta])
            td = p.step("vector", lambda e, s=s: e.tensor_scalar(
                st2f[:, s, 1:2], st2f[:, s, 6:7], 1.0 / DM, EPS_LN, ALU.mult, ALU.add), waits=[tb])
            te = p.step("vector", lambda e, s=s: e.tensor_scalar_mul(
                st2f[:, s, 2:3], st2f[:, s, 0:1], st2f[:, s, 0:1]), waits=[tc1])
            tf = p.step("vector", lambda e, s=s: e.scalar_tensor_tensor(
                st2f[:, s, 3:4], st2f[:, s, 2:3], -1.0, st2f[:, s, 1:2], ALU.mult, ALU.add),
                waits=[td, te])
            tg = p.step("vector", lambda e, s=s: e.reciprocal(st2f[:, s, 4:5], st2f[:, s, 3:4]),
                        waits=[tf])
            tn = p.step("scalar", lambda e, s=s: e.activation(
                st2f[:, s, 4:5], st2f[:, s, 4:5], AF.Sqrt), waits=[tg])
            for _ in range(2):
                n1 = p.step("scalar", lambda e, s=s: e.activation(
                    st2f[:, s, 6:7], st2f[:, s, 4:5], AF.Square), waits=[tn])
                n2 = p.step("vector", lambda e, s=s: e.tensor_tensor(
                    st2f[:, s, 6:7], st2f[:, s, 6:7], st2f[:, s, 3:4], ALU.mult), waits=[n1])
                n3 = p.step("vector", lambda e, s=s: e.tensor_scalar(
                    st2f[:, s, 6:7], st2f[:, s, 6:7], -0.5, 1.5, ALU.mult, ALU.add), waits=[n2])
                tn = p.step("vector", lambda e, s=s: e.tensor_tensor(
                    st2f[:, s, 4:5], st2f[:, s, 4:5], st2f[:, s, 6:7], ALU.mult), waits=[n3])
            tbb = p.step("vector", lambda e, s=s, pr2=pr2: e.tensor_scalar(
                b32[:], r1sb[:, pr2, :], st2f[:, s, 0:1], st2f[:, s, 4:5],
                ALU.subtract, ALU.mult), waits=[tn, t_x, tb])
            tlast = None
            col = gsub * 128
            for d0 in range(0, ND, 4):
                tts = []
                for si in range(4):
                    dch = d0 + si
                    bank = 4 + si
                    tts.append(p.step("tensor", lambda e, dch=dch, bank=bank: e.transpose(
                        PS[:, bank, 0:128], b32[:, dch * 128:(dch + 1) * 128], ident[:]),
                        waits=[tbb, bfree[bank]], serial=False))
                th_ = p.step("scalar", lambda e, d0=d0, col=col: e.activation(
                    bth[:, d0:d0 + 4, col:col + 128], PS[:, 4:8, 0:128], AF.Copy),
                    waits=tts)
                tv = p.step("vector", lambda e, d0=d0, col=col: e.tensor_tensor(
                    btl[:, d0:d0 + 4, col:col + 128], PS[:, 4:8, 0:128],
                    bth[:, d0:d0 + 4, col:col + 128], ALU.subtract), waits=[th_])
                for si in range(4):
                    bfree[4 + si] = tv
                tlast = tv
            t_r1free[pr2] = tlast
            t_blast = tlast
    st2.close()

    p.barrier()

    st3 = ExitStack()
    sb3 = lambda nm, sh, dt: st3.enter_context(nc.sbuf_tensor(nm, sh, dt))
    QF = 1024
    NG = QF // 64
    wstq = sb3("wstq", [128, 2, QF], F32)
    wbb = sb3("wbb", [128, 2, QF], BF16)
    redq = sb3("redq", [128, NG], F32)
    sbq = sb3("sbq", [128, NG], BF16)
    lnq = sb3("lnq", [128, NG], F32)
    rcq = sb3("rcq", [128, 2, NG], F32)
    rtq = sb3("rtq", [128, QF], BF16)
    g1q = sb3("g1q", [128, QF], BF16)
    nqq = sb3("nqq", [128, QF], BF16)
    whq = rtq
    d1q = g1q
    wqv = sb3("wqv", [128, QF], BF16)
    ustg = sb3("ustg", [128, 2, 8, 128], U16)
    wsg = sb3("wsg", [128, 2, ND, 128], BF16)
    wsu = sb3("wsu", [128, 2, ND, 128], BF16)
    uslab = sb3("uslab", [128, 2, 2, ND, 128], U16)
    silu_s = sb3("silu_s", [128, 1024], F32)
    h32 = sb3("h32", [128, 1024], F32)
    hsq = silu_s
    hhs = sb3("hhs", [128, 1024], BF16)
    hls = sb3("hls", [128, 1024], BF16)
    qs = {"par": 0, "wst_free": [[], []], "wbb_free": [None, None],
          "rcq_free": [None, None], "ustg_free": [None, None],
          "tbank": [None, None, None, None], "last_store": None}

    def quant_chain(load_src, dst_hbm_ap, transpose):
        par = qs["par"]
        qs["par"] ^= 1
        tl = p.dma("sync", "d_wq%d" % par, wstq[:, par, :], load_src,
                   waits=list(qs["wst_free"][par]))
        tcp = p.step("scalar", lambda e, par=par: e.activation(wbb[:, par, :], wstq[:, par, :],
                                                               AF.Copy),
                     waits=[tl, qs["wbb_free"][par]])
        qs["wst_free"][par] = [tcp]
        trd = p.step("vector", lambda e, par=par: e.reduce_sum(
            redq[:], wbb[:, par, :].rearrange("p (g k) -> p g k", k=64), AX.X,
            apply_absolute_value=True), waits=[tcp])
        tcl = p.step("vector", lambda e: e.tensor_scalar(
            sbq[:], redq[:], 1.0 / 64, 1e-8, ALU.mult, ALU.max), waits=[trd])
        tln = p.step("scalar", lambda e: e.activation(lnq[:], sbq[:], AF.Ln), waits=[tcl])
        tex = p.step("scalar", lambda e, par=par: e.activation(rcq[:, par, :], lnq[:], AF.Exp,
                                                               scale=-1.0),
                     waits=[tln, qs["rcq_free"][par]])
        rc_rep = bass.AP(rcq.ap().tensor, par * NG, [[2 * NG, 128], [1, NG], [0, 64]])
        sb_rep = bass.AP(sbq.ap().tensor, 0, [[NG, 128], [1, NG], [0, 64]])
        g3 = lambda t: t.rearrange("p (g k) -> p g k", k=64)
        trt = p.step("vector", lambda e, par=par: e.tensor_tensor(
            g3(rtq[:]), g3(wbb[:, par, :]), rc_rep, ALU.mult), waits=[tex])
        qs["rcq_free"][par] = trt
        tgt = p.step("vector", lambda e: e.tensor_scalar(g1q[:], rtq[:], 0.5, None, ALU.is_gt),
                     waits=[trt])
        tnq = p.step("vector", lambda e: e.scalar_tensor_tensor(
            nqq[:], rtq[:], -0.5, g1q[:], ALU.is_lt, ALU.subtract), waits=[tgt])
        twh = p.step("vector", lambda e: e.scalar_tensor_tensor(
            g3(whq[:]), g3(nqq[:]), -1.0, sb_rep, ALU.mult, ALU.mult), waits=[tnq])
        td1 = p.step("vector", lambda e, par=par: e.tensor_tensor(
            d1q[:], whq[:], wbb[:, par, :], ALU.subtract), waits=[twh])
        tqv = p.step("vector", lambda e, par=par: e.tensor_tensor(
            wqv[:], wbb[:, par, :], d1q[:], ALU.add), waits=[td1, qs["last_store"]])
        qs["wbb_free"][par] = tqv
        if not transpose:
            tst = p.dma("gpsimd", "st_wq", dst_hbm_ap, wqv[:].bitcast(U16), waits=[tqv])
        else:
            ev = []
            for half in range(2):
                tts = []
                for si in range(4):
                    sub = half * 4 + si
                    bank = 4 + si
                    tts.append(p.step("tensor", lambda e, sub=sub, bank=bank: e.matmul(
                        PS[:, bank, 0:128], wqv[:, sub * 128:(sub + 1) * 128], identb[:],
                        start=True, stop=True), waits=[tqv, qs["tbank"][si]], serial=False))
                tev = p.step("scalar", lambda e, half=half, par=par: e.activation(
                    ustg[:, par, half * 4:half * 4 + 4, :].bitcast(BF16), PS[:, 4:8, 0:128],
                    AF.Copy), waits=tts + [qs["ustg_free"][par]])
                for si in range(4):
                    qs["tbank"][si] = tev
                ev.append(tev)
            tst = p.dma("gpsimd", "st_wq", dst_hbm_ap, ustg[:, par, :, :], waits=ev)
            qs["ustg_free"][par] = tst
        qs["last_store"] = tst
        return tst

    t_hh_free = [None]
    t_ssq = [t_ssq0]
    t_uslab_free = [None, None]
    t_wsgu_free = [None, None]

    def emit_mm_item(jj, th2, wq_tok, tctx=[None]):
        par = jj % 2
        tcol = th2 * 1024
        if th2 == 0:
            l1 = p.dma("sync", "d_sl%d" % par, uslab[:, par, 0, :, :],
                       wgh_n[jj * 128:(jj + 1) * 128, :].rearrange("p (c n) -> p c n", n=128),
                       waits=[t_uslab_free[par], wq_tok])
            l2 = p.dma("sync", "d_sl%d" % par, uslab[:, par, 1, :, :],
                       wgh_n[Hs + jj * 128:Hs + (jj + 1) * 128, :]
                       .rearrange("p (c n) -> p c n", n=128))
            evs = []
            for gu_i in range(2):
                dst = wsg if gu_i == 0 else wsu
                for d0 in range(0, ND, 4):
                    tts = []
                    for si in range(4):
                        d = d0 + si
                        bank = 4 + si
                        tts.append(p.step("tensor", lambda e, gu_i=gu_i, d=d, bank=bank, par=par:
                                          e.matmul(PS[:, bank, 0:128],
                                                   uslab[:, par, gu_i, d, :].bitcast(BF16),
                                                   identb[:], start=True, stop=True),
                                          waits=[l2, qs["tbank"][si]], serial=False))
                    tev = p.step("scalar", lambda e, dst=dst, d0=d0, par=par: e.activation(
                        dst[:, par, d0:d0 + 4, :], PS[:, 4:8, 0:128], AF.Copy),
                        waits=tts + [t_wsgu_free[par]])
                    for si in range(4):
                        qs["tbank"][si] = tev
                    evs.append(tev)
            tctx[0] = evs
        evs = tctx[0]
        def gatemm(e, par=par, tcol=tcol):
            last = None
            for d in range(ND):
                lh = wsg[:, par, d, :]
                for half in range(2):
                    cs = tcol + half * 512
                    last = e.matmul(PS[:, half, :], lh, bth[:, d, cs:cs + 512],
                                    start=(d == 0), stop=False)
                    last = e.matmul(PS[:, half, :], lh, btl[:, d, cs:cs + 512],
                                    start=False, stop=(d == ND - 1))
            return last
        def upmm(e, par=par, tcol=tcol):
            last = None
            for d in range(ND):
                lh = wsu[:, par, d, :]
                for half in range(2):
                    cs = tcol + half * 512
                    last = e.matmul(PS[:, 2 + half, :], lh, bth[:, d, cs:cs + 512],
                                    start=(d == 0), stop=False)
                    last = e.matmul(PS[:, 2 + half, :], lh, btl[:, d, cs:cs + 512],
                                    start=False, stop=(d == ND - 1))
            return last
        tg_ = p.step("tensor", gatemm, waits=evs + [bfree[0], bfree[1]], serial=False)
        tu_ = p.step("tensor", upmm, waits=evs + [bfree[2], bfree[3]], serial=False)
        t_uslab_free[par] = tu_
        t_wsgu_free[par] = tu_
        return (jj, tcol, tg_, tu_)

    def emit_evac(ctx):
        jj, tcol, tg_, tu_ = ctx
        hrow = jj * 128
        e1 = p.step("scalar", lambda e: e.activation(silu_s[:], PS[:, 0:2, :], AF.Silu),
                    waits=[tg_, t_ssq[0]])
        e2 = p.step("vector", lambda e: e.tensor_tensor(h32[:], silu_s[:], PS[:, 2:4, :],
                                                        ALU.mult), waits=[e1, tu_])
        bfree[0] = e2; bfree[1] = e2; bfree[2] = e2; bfree[3] = e2
        e3 = p.step("scalar", lambda e: e.activation(hsq[:], h32[:], AF.Square), waits=[e2])
        t_ssq[0] = p.step("vector", lambda e, tcol=tcol: e.tensor_tensor(
            ssqP[:, tcol:tcol + 1024], ssqP[:, tcol:tcol + 1024], hsq[:], ALU.add),
            waits=[e3, t_ssq[0]])
        e5 = p.step("vector", lambda e: e.tensor_copy(hhs[:], h32[:]), waits=[t_hh_free[0]])
        e6 = p.step("vector", lambda e: e.tensor_tensor(hls[:], h32[:], hhs[:], ALU.subtract),
                    waits=[e5])
        p.dma("gpsimd", "st_h", hhi[hrow:hrow + 128, tcol:tcol + 1024], hhs[:], waits=[e6])
        t_hh_free[0] = p.dma("gpsimd", "st_h", hlo[hrow:hrow + 128, tcol:tcol + 1024],
                             hls[:], waits=[e6])
        return tu_

    NBP = Hs // 1024
    NJT = 1024 // 128
    NTH = max(1, n_own // 1024)
    NWPC = (DM // 128) * (Hs // 1024)
    wp_i = [0]
    wq_tok_of = {}
    pend = []
    for bp in range(NBP + 1):
        if bp < NBP:
            qthunks = []
            for jr in range(8 * bp, 8 * (bp + 1)):
                for dh in range(2):
                    qthunks.append((wgun[jr * 128:(jr + 1) * 128, dh * 1024:(dh + 1) * 1024],
                                    wgh_n[jr * 128:(jr + 1) * 128, dh * 1024:(dh + 1) * 1024],
                                    False))
                    qthunks.append((wgun[Hs + jr * 128:Hs + (jr + 1) * 128, dh * 1024:(dh + 1) * 1024],
                                    wgh_n[Hs + jr * 128:Hs + (jr + 1) * 128, dh * 1024:(dh + 1) * 1024],
                                    False))
            w0 = (NWPC * bp) // NBP
            w1 = (NWPC * (bp + 1)) // NBP
            for wi in range(w0, w1):
                er, h8 = wi // (Hs // 1024), wi % (Hs // 1024)
                qthunks.append((wpn[er * 128:(er + 1) * 128, h8 * 1024:(h8 + 1) * 1024],
                                wph[h8 * 1024:(h8 + 1) * 1024, er * 128:(er + 1) * 128]
                                .rearrange("(s p) j -> p s j", p=128),
                                True))
        else:
            qthunks = []
        items = [(jj, th2) for jj in range((bp - 1) * NJT, bp * NJT) for th2 in range(NTH)]             if bp > 0 else []
        nq, ni = len(qthunks), len(items)
        qi = ii = 0
        last_st = None
        prev_item = None
        while qi < nq or ii < ni:
            if qi < nq and (ii >= ni or qi * max(ni, 1) <= ii * nq):
                last_st = quant_chain(*qthunks[qi]); qi += 1
            else:
                jj, th2 = items[ii]
                if th2 == 0:
                    cur_tctx = [None]
                if prev_item is not None:
                    emit_evac(prev_item)
                prev_item = emit_mm_item(jj, th2, wq_tok_of[bp - 1], tctx=cur_tctx)
                ii += 1
        if prev_item is not None:
            emit_evac(prev_item)
        if bp < NBP:
            wq_tok_of[bp] = ("st_wq", p.ctr["st_wq"])
    n_h_stores = p.ctr["st_h"]
    n_wp_stores = p.ctr["st_wq"]
    st3.close()

    p.barrier()

    st3b = ExitStack()
    rowt = st3b.enter_context(nc.sbuf_tensor("rowt", [1, 4, 512], F32))
    for tq in range(NTQ):
        tm = p.step("tensor", lambda e, tq=tq: e.matmul(
            PS[0:1, 7, :], ones1fs[:], ssqP[:, tq * 512:(tq + 1) * 512],
            start=True, stop=True), waits=[bfree[7]], serial=False)
        t1 = p.step("vector", lambda e: e.tensor_scalar(
            rowt[:, 0, :], PS[0:1, 7, :], 1.0 / Hs, EPS_RMS, ALU.mult, ALU.add), waits=[tm])
        bfree[7] = t1
        t2 = p.step("vector", lambda e: e.reciprocal(rowt[:, 1, :], rowt[:, 0, :]), waits=[t1])
        t3 = p.step("scalar", lambda e: e.activation(rowt[:, 1, :], rowt[:, 1, :], AF.Sqrt),
                    waits=[t2])
        tn = t3
        for _ in range(2):
            n1 = p.step("scalar", lambda e: e.activation(rowt[:, 2, :], rowt[:, 1, :], AF.Square),
                        waits=[tn])
            n2 = p.step("vector", lambda e: e.tensor_tensor(rowt[:, 2, :], rowt[:, 2, :],
                                                            rowt[:, 0, :], ALU.mult), waits=[n1])
            n3 = p.step("vector", lambda e: e.tensor_scalar(rowt[:, 2, :], rowt[:, 2, :],
                                                            -0.5, 1.5, ALU.mult, ALU.add),
                        waits=[n2])
            tn = p.step("vector", lambda e: e.tensor_tensor(rowt[:, 1, :], rowt[:, 1, :],
                                                            rowt[:, 2, :], ALU.mult), waits=[n3])
        for s in range(NS):
            tt = p.step("tensor", lambda e, s=s: e.transpose(
                PS[:, 6, 0:1], rowt[0:1, 1, s * 128:(s + 1) * 128], onesrs[0:1, 0:1]),
                waits=[tn, bfree[6]], serial=False)
            tv = p.step("vector", lambda e, tq=tq, s=s: e.tensor_copy(
                invr[:, tq, s:s + 1], PS[:, 6, 0:1]), waits=[tt])
            bfree[6] = tv
    st3b.close()

    p.barrier()

    st4 = ExitStack()
    sb4 = lambda nm, sh, dt: st4.enter_context(nc.sbuf_tensor(nm, sh, dt))
    hts_h = sb4("hts_h", [128, 4, 512], BF16)
    hts_l = sb4("hts_l", [128, 4, 512], BF16)
    wps = sb4("wps", [128, 4, 1024], U16)
    o32s = sb4("o32s", [128, 1024], F32)
    r1s = sb4("r1s", [128, 1024], F32)
    ots = sb4("ots", [128, 1024], F32)
    htok = ("st_h", n_h_stores)
    wptok = ("st_wq", n_wp_stores)
    r1tok = ("st_r1", n_r1_stores)
    t_out_dma = None
    t_hfree = [None, None, None, None]
    t_r1s_free = None
    for tp in range(NTQ):
        for eh in range(2):
            gtoks = [None] * NS
            for hc in range(NHC):
                par = hc % 4
                ta_ = p.dma("sync", "d_hh%d" % par, hts_h[:, par, :],
                      hhi[hc * 128:(hc + 1) * 128, tp * 512:(tp + 1) * 512],
                      waits=[t_hfree[par], htok])
                tb_ = p.dma("gpsimd", "d_hl%d" % par, hts_l[:, par, :],
                      hlo[hc * 128:(hc + 1) * 128, tp * 512:(tp + 1) * 512],
                      waits=[t_hfree[par], htok])
                tl3 = p.dma("sync", "d_hw%d" % par, wps[:, par, :],
                            wph[hc * 128:(hc + 1) * 128, eh * 1024:(eh + 1) * 1024],
                            waits=[t_hfree[par], wptok])
                for ts in range(NS):
                    def rmm(e, ts=ts, par=par, hc=hc):
                        last = None
                        for half in range(2):
                            o = PS[:, ts * 2 + half, :]
                            wslab = wps[:, par, half * 512:(half + 1) * 512].bitcast(BF16)
                            last = e.matmul(o, hts_h[:, par, ts * 128:(ts + 1) * 128], wslab,
                                            start=(hc == 0), stop=False)
                            last = e.matmul(o, hts_l[:, par, ts * 128:(ts + 1) * 128], wslab,
                                            start=False, stop=(hc == NHC - 1))
                        return last
                    gtoks[ts] = p.step("tensor", rmm,
                                       waits=[ta_, tb_, tl3] + ([bfree[ts * 2], bfree[ts * 2 + 1]]
                                                                if hc == 0 else []), serial=False)
                t_hfree[par] = gtoks[NS - 1]
            for ts in range(NS):
                rows = tp * 512 + ts * 128
                e1 = p.step("vector", lambda e, ts=ts, tp=tp: e.tensor_scalar_mul(
                    o32s[:], PS[:, ts * 2:ts * 2 + 2, :], invr[:, tp, ts:ts + 1]),
                    waits=[gtoks[ts]])
                bfree[ts * 2] = e1
                bfree[ts * 2 + 1] = e1
                tlr = p.dma("sync", "d_r1s", r1s[:], r1d[rows:rows + 128, eh * 1024:(eh + 1) * 1024],
                            waits=[t_r1s_free, r1tok])
                e2 = p.step("vector", lambda e: e.tensor_tensor(ots[:], o32s[:], r1s[:], ALU.add),
                            waits=[e1, tlr, t_out_dma])
                t_r1s_free = e2
                t_out_dma = p.dma("gpsimd", "st_out", out[rows:rows + 128, eh * 1024:(eh + 1) * 1024],
                                  ots[:], waits=[e2])
    p.step("gpsimd", lambda e: None, waits=[t_out_dma], serial=False)
    st4.close()
    ffn_st.close()

    p.emit_all()
    st.close()
    return nc


def _host_consts():
    import ml_dtypes
    BF = ml_dtypes.bfloat16
    sig = np.arange(C)[:, None]
    ta = np.arange(C)[None, :]
    expnt = np.where(sig > ta, (sig - ta - 1).astype(np.float32),
                     np.float32(1e30)).astype(np.float32)
    expn_in = np.ascontiguousarray(expnt.reshape(4, 128, C))
    tauv_in = np.ascontiguousarray((C - 1 - np.arange(C, dtype=np.float32)).reshape(4, 128, 1))
    sigv_in = np.ascontiguousarray(np.arange(C, dtype=np.float32).reshape(4, 128, 1))
    id32_in = np.eye(128, dtype=np.float32)
    seg_in = np.zeros((128, 2), dtype=np.float32)
    seg_in[0:64, 0] = 1.0 / 64
    seg_in[64:128, 1] = 1.0 / 64
    segt_in = np.zeros((2, 128), dtype=np.float32)
    segt_in[0, 0:64] = 1.0
    segt_in[1, 64:128] = 1.0
    return dict(expn=expn_in, tauv=tauv_in, sigv=sigv_in, id32=id32_in,
                idbf=id32_in.astype(BF), seg=seg_in.astype(BF),
                segtb=segt_in.astype(BF), segtf=segt_in,
                onesr=np.ones((1, 128), dtype=np.float32),
                ones1f=np.ones((128, 1), dtype=np.float32),
                ones1b=np.ones((128, 1), dtype=np.float32).astype(BF))


def kernel(L0_attn_x, L0_attn_Wq, L0_attn_Wk, L0_attn_Wv, L0_attn_Wo, L0_attn_Wg,
           L0_attn_decay_logits, L0_attn_out_scale, L0_ffn_w_gate_up, L0_ffn_w_proj):
    xfull = np.ascontiguousarray(np.asarray(L0_attn_x, dtype=np.float32))
    B, T, _ = xfull.shape
    half = T // 2
    wq_in = np.ascontiguousarray(np.asarray(L0_attn_Wq, np.float32))
    wk_in = np.ascontiguousarray(np.asarray(L0_attn_Wk, np.float32))
    wv_in = np.ascontiguousarray(np.asarray(L0_attn_Wv, np.float32))
    wo_in = np.ascontiguousarray(np.asarray(L0_attn_Wo, np.float32))
    wg_in = np.zeros((DM, 8), dtype=np.float32)
    wg_in[:, 0:3] = np.asarray(L0_attn_Wg, np.float32)
    wgun_in = np.ascontiguousarray(np.asarray(L0_ffn_w_gate_up, np.float32))
    wpn_in = np.ascontiguousarray(np.asarray(L0_ffn_w_proj, np.float32))
    dl = np.asarray(L0_attn_decay_logits, np.float32)
    osc = np.float32(L0_attn_out_scale)
    consts = _host_consts()

    nc = build_nc(half, half, wpn_in.shape[1])
    in_maps = []
    for c in range(8):
        b, hf = c // 2, c % 2
        t0 = hf * half
        xc = np.ascontiguousarray(xfull[b, t0:t0 + half])
        xtc = np.ascontiguousarray(xfull[b, t0:t0 + half].T)
        if hf == 0:
            xoc = np.ascontiguousarray(xfull[b, half:])
            xotc = np.ascontiguousarray(xfull[b, half:].T)
            smask = 1.0
        else:
            xoc, xotc, smask = xc, xtc, 0.0
        cstv = np.zeros((1, 16), dtype=np.float32)
        cstv[0, 0:3] = dl
        cstv[0, 3] = osc
        cstv[0, 4] = smask
        cstv[0, 5] = float(C)
        m = dict(x=xc, xt=xtc, xo=xoc, xot=xotc, wq=wq_in, wk=wk_in, wv=wv_in,
                 wo=wo_in, wg=wg_in, cst=cstv, wgun=wgun_in, wpn=wpn_in, **consts)
        in_maps.append(m)
    res = run_bass_kernel_spmd(nc, in_maps, core_ids=list(range(8)))
    outv = np.empty_like(xfull)
    for c in range(8):
        b, hf = c // 2, c % 2
        outv[b, hf * half:(hf + 1) * half] = res.results[c]["out"]
    return outv
